# version 02
# baseline (speedup 1.0000x reference)
import numpy as np
import ml_dtypes
import concourse.bass as bass
import concourse.mybir as mybir
from concourse.bass_utils import run_bass_kernel_spmd

AF = mybir.ActivationFunctionType
ALU = mybir.AluOpType
AX = mybir.AxisListType
F32 = mybir.dt.float32
BF16 = mybir.dt.bfloat16
U32 = mybir.dt.uint32
I32 = mybir.dt.int32

B, S, D, HID, E = 4, 4096, 1024, 4096, 8
NTOK = B * S
NCORES = 8
TPC = NTOK // NCORES
NT = TPC // 128
KT = D // 128
HT = HID // 128
CAP = 4301
NHEAD, DK = 16, 64
EPS = 1e-5
GC0 = 0.7978845608028654
GC1 = 0.044715


def bf16np(x):
    return np.ascontiguousarray(x).astype(ml_dtypes.bfloat16)


_ffn_bufs = {}


ENGINES = ("tensor", "vector", "scalar", "gpsimd", "sync")


class Ref:
    __slots__ = ("sem", "cnt", "engine")

    def __init__(self, sem, cnt, engine):
        self.sem, self.cnt, self.engine = sem, cnt, engine


class Prog:
    def __init__(self, nc):
        self.nc = nc
        self.items = {e: [] for e in ENGINES}
        self.esem = {e: nc.alloc_semaphore(f"E_{e}") for e in ENGINES}
        self.ecnt = {e: 0 for e in ENGINES}
        self.elast = {}
        self.dsem = {}
        self.dcnt = {}

    def _dma_sem(self, stream):
        if stream not in self.dsem:
            self.dsem[stream] = self.nc.alloc_semaphore(f"D_{stream}")
            self.dcnt[stream] = 0
        return self.dsem[stream]

    def op(self, engine, fn, deps=(), inc=True):
        sem = self.esem[engine]
        deps = tuple(deps)
        if engine in ("vector", "scalar", "gpsimd"):
            prev = self.elast.get(engine)
            if prev is not None:
                deps = deps + (prev,)
        if inc:
            self.ecnt[engine] += 1
        cnt = self.ecnt[engine]
        self.items[engine].append((fn, deps, sem if inc else None, 1))
        ref = Ref(sem, cnt, engine) if inc else None
        if inc:
            self.elast[engine] = ref
        return ref

    def dma(self, engine, out, in_, stream, deps=(), **kw):
        sem = self._dma_sem(stream)
        self.dcnt[stream] += 16
        cnt = self.dcnt[stream]

        def fn(eng, out=out, in_=in_, kw=kw):
            return eng.dma_start(out=out, in_=in_, **kw)
        self.items[engine].append((fn, tuple(deps), sem, 16))
        return Ref(sem, cnt, engine)

    def idma(self, engine, fn, stream, deps=()):
        sem = self._dma_sem(stream)
        self.dcnt[stream] += 16
        cnt = self.dcnt[stream]
        self.items[engine].append((fn, tuple(deps), sem, 16))
        return Ref(sem, cnt, engine)

    def raw(self, engine, fn):
        self.items[engine].append((fn, (), None, 0))
        return None

    def barrier_all(self):
        snap = [Ref(self.esem[e], self.ecnt[e], e) for e in ENGINES]
        snapd = [Ref(self.dsem[s], self.dcnt[s], None) for s in self.dsem]
        for e in ENGINES:
            self.items[e].append((None, tuple(snap + snapd), None, 0))

    def emit(self):
        nc = self.nc
        with nc.Block() as block:
            for engine in ENGINES:
                items = self.items[engine]

                def make(engine=engine, items=items):
                    def body(eng):
                        waited = {}
                        for fn, deps, sem, inc in items:
                            want = {}
                            for d in deps:
                                if d is None:
                                    continue
                                if want.get(d.sem, -1) < d.cnt:
                                    want[d.sem] = d.cnt
                            for wsem, wcnt in want.items():
                                if waited.get(wsem, -1) >= wcnt:
                                    continue
                                eng.wait_ge(wsem, wcnt)
                                waited[wsem] = wcnt
                            if fn is None:
                                continue
                            inst = fn(eng)
                            if sem is not None:
                                assert inst is not None, "op fn returned None"
                                inst.then_inc(sem, inc)
                    return body
                getattr(block, engine)(make())


def alloc_consts(nc, p):
    c = {}
    tri = nc.inline_tensor(np.triu(np.ones((128, 128), np.float32), 1), "c_tri").ap()
    onescol = nc.inline_tensor(np.ones((128, 1), np.float32), "c_onescol").ap()
    onesrow = nc.inline_tensor(np.ones((1, 128), np.float32), "c_onesrow").ap()
    ident = nc.inline_tensor(np.eye(128, dtype=ml_dtypes.bfloat16), "c_ident").ap()
    iota8 = nc.inline_tensor(
        np.tile(np.arange(8, dtype=np.float32), (128, 1)), "c_iota8").ap()
    ones11 = nc.inline_tensor(np.ones((1, 1), np.float32), "c_ones11").ap()
    identf = nc.inline_tensor(np.eye(128, dtype=np.float32), "c_identf").ap()
    tril = nc.inline_tensor(
        np.tril(np.ones((128, 128), np.float32)).astype(ml_dtypes.bfloat16),
        "c_tril").ap()
    ones64 = nc.inline_tensor(np.ones((64, 1), np.float32), "c_ones64").ap()
    c["tri"] = nc.alloc_sbuf_tensor("c_tri_s", [128, 128], F32)
    c["onescol"] = nc.alloc_sbuf_tensor("c_onescol_s", [128, 1], F32)
    c["onesrow"] = nc.alloc_sbuf_tensor("c_onesrow_s", [1, 128], F32)
    c["ident"] = nc.alloc_sbuf_tensor("c_ident_s", [128, 128], BF16)
    c["iota8"] = nc.alloc_sbuf_tensor("c_iota8_s", [128, 8], F32)
    c["ones11"] = nc.alloc_sbuf_tensor("c_ones11_s", [1, 1], F32)
    c["identf"] = nc.alloc_sbuf_tensor("c_identf_s", [128, 128], F32)
    c["tril"] = nc.alloc_sbuf_tensor("c_tril_s", [128, 128], BF16)
    c["ones64"] = nc.alloc_sbuf_tensor("c_ones64_s", [64, 1], F32)
    c["onesrowf"] = c["onesrow"]
    refs = []
    for name, src in [("tri", tri), ("onescol", onescol), ("onesrow", onesrow),
                      ("ident", ident), ("iota8", iota8), ("ones11", ones11),
                      ("identf", identf), ("tril", tril), ("ones64", ones64)]:
        refs.append(p.dma("sync", c[name][:], src, stream="consts"))
    return c, refs


def emit_ln_tile(p, xs, xln_out, sq, stcols, deps=()):
    r_sum = p.op("vector", lambda v: v.reduce_sum(stcols[:, 0:1], xs, axis=AX.X),
                 deps=deps)
    r_sq = p.op("scalar", lambda s: s.activation(sq[:], xs, AF.Square,
                                                accum_out=stcols[:, 1:2]),
                deps=deps)
    r_m = p.op("vector", lambda v: v.tensor_scalar(
        stcols[:, 2:3], stcols[:, 0:1], 1.0 / D, None, ALU.mult), deps=[r_sum])
    r_v1 = p.op("vector", lambda v: v.scalar_tensor_tensor(
        stcols[:, 3:4], stcols[:, 2:3], -1.0, stcols[:, 2:3], ALU.mult,
        ALU.mult), deps=[r_m])
    r_v2 = p.op("vector", lambda v: v.scalar_tensor_tensor(
        stcols[:, 3:4], stcols[:, 1:2], 1.0 / D, stcols[:, 3:4], ALU.mult,
        ALU.add), deps=[r_sq, r_v1])
    r_v3 = p.op("vector", lambda v: v.tensor_scalar(
        stcols[:, 3:4], stcols[:, 3:4], EPS, None, ALU.add), deps=[r_v2])
    r_sq2 = p.op("scalar", lambda s: s.activation(
        stcols[:, 4:5], stcols[:, 3:4], AF.Sqrt), deps=[r_v3])
    r_rc = p.op("vector", lambda v: v.reciprocal(
        stcols[:, 5:6], stcols[:, 4:5]), deps=[r_sq2])
    r_ap = p.op("vector", lambda v: v.tensor_scalar(
        xln_out, xs, stcols[:, 2:3], stcols[:, 5:6],
        ALU.subtract, ALU.mult), deps=[r_rc])
    return r_ap


def emit_transpose_1024(p, src_bf16, dst_bf16, ps_t, ident, deps):
    for k in range(KT):
        last = (k == KT - 1)
        p.op("tensor", lambda t, k=k: t.transpose(
            ps_t[:, k * 128:(k + 1) * 128],
            src_bf16[:, k * 128:(k + 1) * 128], ident[:]),
            deps=(deps if k == 0 else ()), inc=last)
    r_t = Ref(p.esem["tensor"], p.ecnt["tensor"], "tensor")
    r_c = p.op("vector", lambda v: v.tensor_copy(dst_bf16, ps_t[:]),
               deps=[r_t])
    return r_c


def emit_ln_routing(nc, p, cst, pref, xt, gate_deps):
    sq = nc.alloc_sbuf_tensor(f"{pref}_sq", [128, D], F32)
    stats = nc.alloc_sbuf_tensor(f"{pref}_stats", [128, 8], F32)
    xln = nc.alloc_sbuf_tensor(f"{pref}_xln", [128, NT * D], BF16)
    xlnT = nc.alloc_sbuf_tensor(f"{pref}_xlnT", [128, D], BF16)
    logits = nc.alloc_sbuf_tensor(f"{pref}_logits", [128, E], F32)
    top8v = nc.alloc_sbuf_tensor(f"{pref}_top8v", [128, 8], F32)
    top8i = nc.alloc_sbuf_tensor(f"{pref}_top8i", [128, 8], U32)
    ohbuf = nc.alloc_sbuf_tensor(f"{pref}_ohbuf", [128, 3 * E], F32)
    carry = nc.alloc_sbuf_tensor(f"{pref}_carry", [1, E], F32)
    rt = nc.alloc_sbuf_tensor(f"{pref}_rt", [128, NT * 8], F32)
    wgt = nc.alloc_sbuf_tensor(f"{pref}_wgt", [128, KT * E], BF16)

    ps_t = nc.alloc_psum_tensor(f"{pref}_ps_t", [128, D], BF16)
    ps_l = nc.alloc_psum_tensor(f"{pref}_ps_l", [128, E], F32)
    ps_p = nc.alloc_psum_tensor(f"{pref}_ps_p", [128, E], F32)
    ps_c = nc.alloc_psum_tensor(f"{pref}_ps_c", [1, E], F32)

    r_wg = p.dma("gpsimd", wgt.ap().rearrange("p (k e) -> p k e", k=KT),
                 _WG_IN[0].rearrange("(k p) e -> p k e", p=128),
                 stream=f"{pref}_wg")
    r_cz = p.op("vector", lambda v: v.memset(carry[:], 0.0))
    r_rtz = p.op("vector", lambda v: v.memset(rt[:], 0.0))

    gate = list(gate_deps) + [r_wg, r_rtz]
    rt_done = []
    prev_lgc = None
    for t in range(NT):
        xs = xt[:, t * D:(t + 1) * D]
        r_ln = emit_ln_tile(p, xs, xln[:, t * D:(t + 1) * D], sq, stats,
                            deps=(gate if t == 0 else ()))
        r_tr = emit_transpose_1024(p, xln[:, t * D:(t + 1) * D], xlnT[:],
                                   ps_t, cst["ident"],
                                   deps=[r_ln] + (gate if t == 0 else []))
        for k in range(KT):
            kd = []
            if k == 0:
                kd = [r_tr, r_wg]
                if prev_lgc is not None:
                    kd.append(prev_lgc)
            p.op("tensor", lambda te, k=k: te.matmul(
                ps_l[:], xlnT[:, k * 128:(k + 1) * 128],
                wgt[:, k * E:(k + 1) * E],
                start=(k == 0), stop=(k == KT - 1)),
                deps=kd, inc=(k == KT - 1))
        r_lg = Ref(p.esem["tensor"], p.ecnt["tensor"], "tensor")
        r_lgc = p.op("vector", lambda v: v.tensor_copy(logits[:], ps_l[:]),
                     deps=[r_lg])
        prev_lgc = r_lgc
        r_t8 = p.op("vector", lambda v: v.max(top8v[:], logits[:]),
                    deps=[r_lgc])
        r_i8 = p.op("vector", lambda v: v.max_index(top8i[:], top8v[:],
                                                    logits[:]), deps=[r_t8])
        ro = t * 8
        r_d12 = p.op("vector", lambda v: v.tensor_sub(
            stats[:, 6:7], top8v[:, 0:1], top8v[:, 1:2]), deps=[r_t8])
        r_sg = p.op("scalar", lambda s, ro=ro: s.activation(
            rt[:, ro + 2:ro + 3], stats[:, 6:7], AF.Sigmoid), deps=[r_d12])
        r_g1 = r_sg
        r_g2 = p.op("vector", lambda v, ro=ro: v.tensor_scalar(
            rt[:, ro + 3:ro + 4], rt[:, ro + 2:ro + 3], -1.0, 1.0,
            ALU.mult, ALU.add), deps=[r_g1])
        r_i1 = p.op("vector", lambda v, ro=ro: v.tensor_copy(
            rt[:, ro + 0:ro + 1], top8i[:, 0:1]), deps=[r_i8])
        r_i2 = p.op("vector", lambda v, ro=ro: v.tensor_copy(
            rt[:, ro + 1:ro + 2], top8i[:, 1:2]), deps=[r_i8])
        r_oh1 = p.op("vector", lambda v, ro=ro: v.tensor_scalar(
            ohbuf[:, 0:E], cst["iota8"][:], rt[:, ro + 0:ro + 1], None,
            ALU.is_equal), deps=[r_i1])
        r_oh2 = p.op("vector", lambda v, ro=ro: v.tensor_scalar(
            ohbuf[:, E:2 * E], cst["iota8"][:], rt[:, ro + 1:ro + 2], None,
            ALU.is_equal), deps=[r_i2])
        r_ohs = p.op("vector", lambda v: v.tensor_add(
            ohbuf[:, 2 * E:3 * E], ohbuf[:, 0:E], ohbuf[:, E:2 * E]),
            deps=[r_oh1, r_oh2])
        p.op("tensor", lambda te: te.matmul(
            ps_p[:], cst["tri"][:], ohbuf[:, 2 * E:3 * E],
            start=True, stop=False), deps=[r_ohs], inc=False)
        p.op("tensor", lambda te: te.matmul(
            ps_p[:], cst["onesrow"][:], carry[:],
            start=False, stop=True), deps=[r_cz], inc=True)
        r_pos = Ref(p.esem["tensor"], p.ecnt["tensor"], "tensor")
        r_tc = p.op("tensor", lambda te: te.matmul(
            ps_c[:], cst["onescol"][:], ohbuf[:, 2 * E:3 * E],
            start=True, stop=True), deps=[r_ohs])
        r_p1m = p.op("vector", lambda v: v.tensor_mul(
            ohbuf[:, 0:E], ohbuf[:, 0:E], ps_p[:]), deps=[r_pos])
        r_p1 = p.op("vector", lambda v, ro=ro: v.reduce_sum(
            rt[:, ro + 4:ro + 5], ohbuf[:, 0:E], axis=AX.X), deps=[r_p1m])
        r_p2m = p.op("vector", lambda v: v.tensor_mul(
            ohbuf[:, E:2 * E], ohbuf[:, E:2 * E], ps_p[:]), deps=[r_pos])
        r_p2 = p.op("vector", lambda v, ro=ro: v.reduce_sum(
            rt[:, ro + 5:ro + 6], ohbuf[:, E:2 * E], axis=AX.X), deps=[r_p2m])
        r_cu = p.op("vector", lambda v: v.tensor_add(
            carry[:], carry[:], ps_c[:]), deps=[r_tc, r_p1m, r_p2m])
        r_cz = r_cu
        rt_done += [r_p1, r_p2, r_g2, r_i1, r_i2, r_sg]
    return xln, rt, carry, rt_done, r_cz


_WG_IN = [None]


WSLICE = E * D * HID // NCORES


def build_L1():
    nc = bass.Bass("TRN2", target_bir_lowering=False, debug=False,
                   detect_race_conditions=False)
    p = Prog(nc)
    x_in = nc.dram_tensor("x", [TPC, D], F32, kind="ExternalInput").ap()
    wg_in = nc.dram_tensor("wg", [D, E], F32, kind="ExternalInput").ap()
    w1s_in = nc.dram_tensor("w1s", [WSLICE // 8192, 8192], F32,
                            kind="ExternalInput").ap()
    w2s_in = nc.dram_tensor("w2s", [WSLICE // 8192, 8192], F32,
                            kind="ExternalInput").ap()
    _WG_IN[0] = wg_in
    xln_out = nc.dram_tensor("xln", [TPC, D], BF16, kind="ExternalOutput").ap()
    cnt_out = nc.dram_tensor("cnt", [1, E], F32, kind="ExternalOutput").ap()
    rt_out = nc.dram_tensor("rt", [TPC, 8], F32, kind="ExternalOutput").ap()
    w1b_out = nc.dram_tensor("w1b", [WSLICE // 8192, 8192], BF16,
                             kind="ExternalOutput").ap()
    w2b_out = nc.dram_tensor("w2b", [WSLICE // 8192, 8192], BF16,
                             kind="ExternalOutput").ap()

    cst, cst_refs = alloc_consts(nc, p)
    xt = nc.alloc_sbuf_tensor("xt", [128, NT * D], F32)
    r_x = p.dma("sync", xt.ap().rearrange("p (t d) -> p t d", t=NT),
                x_in.rearrange("(t p) d -> p t d", p=128), stream="xload")

    xln, rt, carry, rt_done, r_cz = emit_ln_routing(
        nc, p, cst, "r", xt, [r_x] + cst_refs)

    wcv = [nc.alloc_sbuf_tensor(f"wcv{i}", [128, 8192], BF16) for i in range(2)]
    NROW = WSLICE // 8192
    RCH = NROW // 128
    conv_done = []
    prev_st = {}
    for which, win, wout in (("w1", w1s_in, w1b_out), ("w2", w2s_in, w2b_out)):
        for ch in range(RCH):
            par = ch % 2
            ldeps = [prev_st[par]] if par in prev_st else []
            r_l = p.dma("gpsimd", wcv[par][:],
                        win[ch * 128:(ch + 1) * 128, :],
                        stream=f"cv{which}{par}", deps=ldeps)
            r_s = p.dma("sync", wout[ch * 128:(ch + 1) * 128, :],
                        wcv[par][:], stream=f"cs{which}{par}", deps=[r_l])
            prev_st[par] = r_s
            conv_done.append(r_s)

    s1 = p.dma("sync", xln_out.rearrange("(t p) d -> p t d", p=128),
               xln.ap().rearrange("p (t d) -> p t d", t=NT), stream="st",
               deps=rt_done)
    s2 = p.dma("sync", rt_out.rearrange("(t p) e -> p t e", p=128),
               rt.ap().rearrange("p (t e) -> p t e", t=NT), stream="st",
               deps=rt_done)
    s3 = p.dma("sync", cnt_out, carry[:], stream="st", deps=[r_cz])
    p.op("sync", lambda sy: sy.nop(), deps=[s1, s2, s3] + conv_done)
    p.emit()
    return nc


def emit_dense_ffn(nc, p, cst, name, xres, xlnT, rtbuf, offs_bcast, scr,
                   w1_in, b1_in, w2_in, b2_in, gate_deps, r_offs, psum):
    HALF = TPC // 4
    NCH = HALF // 512

    coef = nc.alloc_sbuf_tensor(f"{name}_coef", [128, NT * E], F32)
    coefT = nc.alloc_sbuf_tensor(f"{name}_coefT", [1, TPC], F32)
    hbuf = nc.alloc_sbuf_tensor(f"{name}_hbuf", [128, HT * HALF], BF16)
    accT = nc.alloc_sbuf_tensor(f"{name}_accT", [128, KT * TPC], BF16)
    _ffn_bufs["hbuf"] = hbuf
    _ffn_bufs["accT"] = accT
    w1t = [nc.alloc_sbuf_tensor(f"{name}_w1t{i}", [128, KT * 128], BF16)
           for i in range(2)]
    w2t = [nc.alloc_sbuf_tensor(f"{name}_w2t{i}", [128, HT * 128], BF16)
           for i in range(2)]
    b1q = nc.alloc_sbuf_tensor(f"{name}_b1q", [128, HT], F32)
    b2h = nc.alloc_sbuf_tensor(f"{name}_b2h", [128, KT], F32)
    ytmp = nc.alloc_sbuf_tensor(f"{name}_ytmp", [128, 512], F32)
    ycs = nc.alloc_sbuf_tensor(f"{name}_ycs", [128, 512], BF16)

    ps_a = psum["a"]
    ps_b = psum["b"]
    ps_y = psum["y"]
    ps_y2 = psum["y2"]
    ps_tr = psum["tr"]
    ps_cb = psum["cb"]
    ps_sm = psum["sm"]

    coef_refs = []
    for t in range(NT):
        ro = t * 8
        co = t * E
        r_o1 = p.op("vector", lambda v, ro=ro, co=co: v.tensor_scalar(
            coef[:, co:co + E], cst["iota8"][:], rtbuf[:, ro:ro + 1], None,
            ALU.is_equal), deps=list(gate_deps) + [r_offs])
        r_s1 = p.op("vector", lambda v, co=co: v.tensor_mul(
            scr[:, 0:E], coef[:, co:co + E], offs_bcast[:]), deps=[r_o1])
        r_g1 = p.op("vector", lambda v: v.reduce_sum(
            scr[:, 8:9], scr[:, 0:E], axis=AX.X), deps=[r_s1])
        r_k1 = p.op("vector", lambda v, ro=ro: v.tensor_add(
            scr[:, 9:10], scr[:, 8:9], rtbuf[:, ro + 4:ro + 5]), deps=[r_g1])
        r_m1 = p.op("vector", lambda v: v.tensor_scalar(
            scr[:, 10:11], scr[:, 9:10], float(CAP), None, ALU.is_lt),
            deps=[r_k1])
        r_w1 = p.op("vector", lambda v, ro=ro: v.tensor_mul(
            scr[:, 11:12], scr[:, 10:11], rtbuf[:, ro + 2:ro + 3]),
            deps=[r_m1])
        r_o2 = p.op("vector", lambda v, ro=ro: v.tensor_scalar(
            scr[:, 16:16 + E], cst["iota8"][:], rtbuf[:, ro + 1:ro + 2], None,
            ALU.is_equal), deps=[r_o1])
        r_s2 = p.op("vector", lambda v: v.tensor_mul(
            scr[:, 24:24 + E], scr[:, 16:16 + E], offs_bcast[:]), deps=[r_o2])
        r_g2 = p.op("vector", lambda v: v.reduce_sum(
            scr[:, 12:13], scr[:, 24:24 + E], axis=AX.X), deps=[r_s2])
        r_k2 = p.op("vector", lambda v, ro=ro: v.tensor_add(
            scr[:, 13:14], scr[:, 12:13], rtbuf[:, ro + 5:ro + 6]), deps=[r_g2])
        r_m2 = p.op("vector", lambda v: v.tensor_scalar(
            scr[:, 14:15], scr[:, 13:14], float(CAP), None, ALU.is_lt),
            deps=[r_k2])
        r_w2 = p.op("vector", lambda v, ro=ro: v.tensor_mul(
            scr[:, 15:16], scr[:, 14:15], rtbuf[:, ro + 3:ro + 4]),
            deps=[r_m2])
        r_c1 = p.op("vector", lambda v, co=co: v.tensor_scalar(
            coef[:, co:co + E], coef[:, co:co + E], scr[:, 11:12], None,
            ALU.mult), deps=[r_w1])
        r_c2 = p.op("vector", lambda v, co=co: v.tensor_scalar(
            scr[:, 16:16 + E], scr[:, 16:16 + E], scr[:, 15:16], None,
            ALU.mult), deps=[r_w2, r_c1])
        r_cf = p.op("vector", lambda v, co=co: v.tensor_add(
            coef[:, co:co + E], coef[:, co:co + E], scr[:, 16:16 + E]),
            deps=[r_c2])
        coef_refs.append(r_cf)

    r_az = p.op("vector", lambda v: v.memset(accT[:], 0.0))

    acc_last = [r_az]
    mm1_group = {}
    mm2_group = {}
    ps_reader = {}
    prev_hb = None
    for e in range(E):
        brow = scr[0:1, 2048:2560]
        r_prev_chunk = acc_last[-1]
        for cc in range(HT // 4):
            bdeps = [r_prev_chunk]
            if cc == 0 and "sm" in ps_reader:
                bdeps.append(ps_reader["sm"])
            r_br = p.dma("sync", brow,
                         b1_in[e:e + 1, cc * 512:(cc + 1) * 512],
                         stream=f"{name}_br", deps=bdeps)
            for j in range(4):
                p.op("tensor", lambda te, cc=cc, j=j: te.matmul(
                    ps_sm[:, cc * 4 + j:cc * 4 + j + 1],
                    brow[0:1, j * 128:(j + 1) * 128], cst["ones11"][:],
                    start=True, stop=True), deps=[r_br], inc=(j == 3))
            r_prev_chunk = Ref(p.esem["tensor"], p.ecnt["tensor"], "tensor")
        r_b1q = p.op("vector", lambda v: v.tensor_scalar(
            b1q[:], ps_sm[:, 0:HT], 0.25, None, ALU.mult),
            deps=[r_prev_chunk])
        for cc in range(KT // 4):
            r_br = p.dma("sync", brow,
                         b2_in[e:e + 1, cc * 512:(cc + 1) * 512],
                         stream=f"{name}_br", deps=[r_prev_chunk, r_b1q])
            for j in range(4):
                p.op("tensor", lambda te, cc=cc, j=j: te.matmul(
                    ps_sm[:, HT + cc * 4 + j:HT + cc * 4 + j + 1],
                    brow[0:1, j * 128:(j + 1) * 128], cst["ones11"][:],
                    start=True, stop=True), deps=[r_br], inc=(j == 3))
            r_prev_chunk = Ref(p.esem["tensor"], p.ecnt["tensor"], "tensor")
        r_b2h = p.op("vector", lambda v: v.tensor_scalar(
            b2h[:], ps_sm[:, HT:HT + KT], 0.5, None, ALU.mult),
            deps=[r_prev_chunk])
        ps_reader["sm"] = r_b2h

        ct_refs = []
        prev_ctc = None
        for half in range(4):
            for ch in range(NCH):
                base = half * HALF + ch * 512
                for j in range(4):
                    t = (base + j * 128) // 128
                    jdeps = [coef_refs[t]] + acc_last[-1:]
                    if j == 0 and prev_ctc is not None:
                        jdeps.append(prev_ctc)
                    if j == 0 and "cb" in ps_reader:
                        jdeps.append(ps_reader["cb"])
                    p.op("tensor", lambda te, t=t, j=j, e=e: te.matmul(
                        ps_cb[0:1, j * 128:(j + 1) * 128],
                        coef[:, t * E + e: t * E + e + 1],
                        cst["identf"][:],
                        start=True, stop=True),
                        deps=jdeps, inc=(j == 3))
                r_ctm = Ref(p.esem["tensor"], p.ecnt["tensor"], "tensor")
                r_ctc = p.op("vector", lambda v, base=base: v.tensor_copy(
                    coefT[0:1, base:base + 512], ps_cb[0:1, 0:512]),
                    deps=[r_ctm])
                prev_ctc = r_ctc
                ct_refs.append(r_ctc)

        for half in range(4):
            tok0 = half * HALF
            gelu_refs = []
            for ht in range(HT):
                par = ht % 2
                wb = w1t[par]
                wdeps = [mm1_group[par]] if par in mm1_group else []
                r_w = p.dma("gpsimd", wb[:], w1_in[e, ht],
                            stream=f"{name}_w1{par}", deps=wdeps)
                for ch in range(NCH):
                    ps = ps_a if ch % 2 == 0 else ps_b
                    psname = "a" if ch % 2 == 0 else "b"
                    t0 = (tok0 + ch * 512) // 128
                    for k in range(KT):
                        kdeps = []
                        if k == 0:
                            kdeps = [r_w]
                            if psname in ps_reader:
                                kdeps.append(ps_reader[psname])
                        p.op("tensor", lambda te, wb=wb, k=k, t0=t0, ps=ps:
                             te.matmul(
                                 ps[:],
                                 wb[:, k * 128:(k + 1) * 128],
                                 xlnT.ap().rearrange(
                                     "p (t d) -> p t d", t=NT)[
                                     :, t0:t0 + 4, k * 128:(k + 1) * 128],
                                 start=(k == 0), stop=(k == KT - 1)),
                             deps=kdeps if k == 0 else (),
                             inc=(k == KT - 1))
                    r_mm = Ref(p.esem["tensor"], p.ecnt["tensor"], "tensor")
                    if ch == NCH - 1:
                        mm1_group[par] = r_mm
                    hb = hbuf[:, ht * HALF + ch * 512:
                              ht * HALF + (ch + 1) * 512]
                    s0 = scr[:, 0:512]
                    s1 = scr[:, 512:1024]
                    s2 = scr[:, 1024:1536]
                    hq_deps = [r_mm, r_b1q]
                    if prev_hb is not None:
                        hq_deps.append(prev_hb)
                    r_hq = p.op("scalar", lambda s, ps=ps, s0=s0, ht=ht:
                                s.activation(s0, ps[:], AF.Identity,
                                             bias=b1q[:, ht:ht + 1],
                                             scale=0.25),
                                deps=hq_deps)
                    ps_reader[psname] = r_hq
                    r_sq2 = p.op("scalar", lambda s, s0=s0, s1=s1:
                                 s.activation(s1, s0, AF.Square, scale=4.0),
                                 deps=[r_hq])
                    r_v = p.op("vector", lambda v, s1=s1: v.tensor_scalar(
                        s1, s1, 4.0 * GC0 * GC1, 4.0 * GC0,
                        ALU.mult, ALU.add), deps=[r_sq2])
                    r_u = p.op("vector", lambda v, s0=s0, s1=s1:
                               v.tensor_mul(s1, s1, s0), deps=[r_v])
                    r_th = p.op("scalar", lambda s, s1=s1, s2=s2:
                                s.activation(s2, s1, AF.Tanh), deps=[r_u])
                    r_hb = p.op("vector", lambda v, s0=s0, s2=s2, hb=hb:
                                v.scalar_tensor_tensor(hb, s2, 1.0, s0,
                                                       ALU.add, ALU.mult),
                                deps=[r_th])
                    prev_hb = r_hb
                    gelu_refs.append(r_hb)
            for ch in range(NCH):
                base = tok0 + ch * 512
                cbdeps = ct_refs + acc_last[-1:]
                if "cb" in ps_reader:
                    cbdeps.append(ps_reader["cb"])
                r_cb = p.op("tensor", lambda te, base=base: te.matmul(
                    ps_cb[:], cst["onesrowf"][:],
                    coefT[0:1, base:base + 512],
                    start=True, stop=True), deps=cbdeps)
                for dt in range(KT):
                    par = dt % 2
                    wb2 = w2t[par]
                    if ch == 0:
                        wdeps2 = [mm2_group[par]] if par in mm2_group else []
                        r_w2 = p.dma("scalar", wb2[:], w2_in[e, dt],
                                      stream=f"{name}_w2{par}", deps=wdeps2)
                        mm2_wref = r_w2
                    else:
                        mm2_wref = None
                    psy = ps_y if dt % 2 == 0 else ps_y2
                    psyname = "y" if dt % 2 == 0 else "y2"
                    for hk in range(HT):
                        dd = []
                        if hk == 0:
                            dd = gelu_refs[:]
                            if mm2_wref is not None:
                                dd.append(mm2_wref)
                            if psyname in ps_reader:
                                dd.append(ps_reader[psyname])
                        p.op("tensor", lambda te, wb2=wb2, hk=hk, ch=ch,
                             psy=psy, base=base, tok0=tok0: te.matmul(
                                 psy[:],
                                 wb2[:, hk * 128:(hk + 1) * 128],
                                 hbuf[:, hk * HALF + (base - tok0):
                                      hk * HALF + (base - tok0) + 512],
                                 start=(hk == 0), stop=(hk == HT - 1)),
                             deps=dd, inc=(hk == HT - 1))
                    r_mm2 = Ref(p.esem["tensor"], p.ecnt["tensor"], "tensor")
                    if ch == NCH - 1:
                        mm2_group[par] = r_mm2
                    ybdeps = [r_mm2, r_b2h]
                    if "yc" in ps_reader:
                        ybdeps.append(ps_reader["yc"])
                    r_yb = p.op("scalar", lambda s, psy=psy, dt=dt:
                                s.activation(ytmp[:], psy[:], AF.Identity,
                                             bias=b2h[:, dt:dt + 1]),
                                deps=ybdeps)
                    ps_reader[psyname] = r_yb
                    r_yc = p.op("vector", lambda v: v.tensor_mul(
                        ycs[:], ytmp[:], ps_cb[:]), deps=[r_yb, r_cb])
                    ps_reader["cb"] = r_yc
                    ps_reader["yc"] = r_yc
                    acc = accT[:, dt * TPC + base: dt * TPC + base + 512]
                    r_up = p.op("vector", lambda v, acc=acc: v.tensor_add(
                        acc, acc, ycs[:]), deps=[r_yc, r_az])
                    acc_last.append(r_up)

    done = []
    r_cp = None
    for tt in range(NT):
        for dt in range(KT):
            tdeps = []
            if dt == 0:
                tdeps = acc_last[-1:]
                if r_cp is not None:
                    tdeps.append(r_cp)
            p.op("tensor", lambda te, tt=tt, dt=dt: te.transpose(
                ps_tr[:, dt * 128:(dt + 1) * 128],
                accT[:, dt * TPC + tt * 128: dt * TPC + (tt + 1) * 128],
                cst["ident"][:]), deps=tdeps,
                inc=(dt == KT - 1))
        r_trg = Ref(p.esem["tensor"], p.ecnt["tensor"], "tensor")
        r_cp = p.op("vector", lambda v: v.tensor_copy(
            scr[:, 0:1024], ps_tr[:]), deps=[r_trg])
        r_ad = p.op("vector", lambda v, tt=tt: v.tensor_add(
            xres[:, tt * D:(tt + 1) * D], xres[:, tt * D:(tt + 1) * D],
            scr[:, 0:1024]), deps=[r_cp])
        done.append(r_ad)
    return done


def emit_ln_transpose_all(nc, p, cst, name, xres, xlnT, scr, deps, ps_t,
                          stage):
    last = None
    outs = []
    for t in range(NT):
        dd = list(deps) + ([last] if last is not None else [])
        r_ln = emit_ln_tile(p, xres[:, t * D:(t + 1) * D], stage[:],
                            scr[:, 0:D], scr[:, D:D + 8], deps=dd)
        for k in range(KT):
            p.op("tensor", lambda te, k=k: te.transpose(
                ps_t[:, k * 128:(k + 1) * 128],
                stage[:, k * 128:(k + 1) * 128], cst["ident"][:]),
                deps=([r_ln] + ([last] if last is not None else [])
                      if k == 0 else ()),
                inc=(k == KT - 1))
        r_tg = Ref(p.esem["tensor"], p.ecnt["tensor"], "tensor")
        r_c = p.op("vector", lambda v, t=t: v.tensor_copy(
            xlnT[:, t * D:(t + 1) * D], ps_t[:]), deps=[r_tg])
        last = r_c
        outs.append(r_c)
    return outs


def build_L2(with_qkv=True, name="L2"):
    nc = bass.Bass("TRN2", target_bir_lowering=False, debug=False,
                   detect_race_conditions=False)
    p = Prog(nc)
    x_in = nc.dram_tensor("x", [TPC, D], F32, kind="ExternalInput").ap()
    xln_in = nc.dram_tensor("xln", [TPC, D], BF16, kind="ExternalInput").ap()
    rt_in = nc.dram_tensor("rt", [TPC, 8], F32, kind="ExternalInput").ap()
    cnts_in = nc.dram_tensor("cnts", [8, E], F32, kind="ExternalInput").ap()
    cmask_in = nc.dram_tensor("cmask", [8, 1], F32, kind="ExternalInput").ap()
    w1_in = nc.dram_tensor("w1", [E, HT, 128, KT * 128], BF16,
                           kind="ExternalInput").ap()
    b1_in = nc.dram_tensor("b1", [E, HID], F32, kind="ExternalInput").ap()
    w2_in = nc.dram_tensor("w2", [E, KT, 128, HT * 128], BF16,
                           kind="ExternalInput").ap()
    b2_in = nc.dram_tensor("b2", [E, D], F32, kind="ExternalInput").ap()
    x1_out = nc.dram_tensor("x1", [TPC, D], F32, kind="ExternalOutput").ap()
    if with_qkv:
        wq_in = nc.dram_tensor("wq", [D, D], F32, kind="ExternalInput").ap()
        wk_in = nc.dram_tensor("wk", [D, D], F32, kind="ExternalInput").ap()
        wv_in = nc.dram_tensor("wv", [D, D], F32, kind="ExternalInput").ap()
        ropec_in = nc.dram_tensor("ropec", [128, TPC], F32,
                                  kind="ExternalInput").ap()
        ropes_in = nc.dram_tensor("ropes", [128, TPC], F32,
                                  kind="ExternalInput").ap()
        qT_out = nc.dram_tensor("qT", [D, TPC], BF16, kind="ExternalOutput").ap()
        kT_out = nc.dram_tensor("kT", [D, TPC], BF16, kind="ExternalOutput").ap()
        v_out = nc.dram_tensor("v", [TPC, D], BF16, kind="ExternalOutput").ap()

    cst, cst_refs = alloc_consts(nc, p)
    xt = nc.alloc_sbuf_tensor("xt", [128, NT * D], F32)
    xlnT = nc.alloc_sbuf_tensor("xlnT", [128, NT * D], BF16)
    rtb = nc.alloc_sbuf_tensor("rtb", [128, NT * 8], F32)
    scr = nc.alloc_sbuf_tensor("scr", [128, 2560], F32)
    cnts = nc.alloc_sbuf_tensor("s_cnts", [8, E], F32)
    cmask = nc.alloc_sbuf_tensor("s_cmask", [8, 1], F32)
    offs = nc.alloc_sbuf_tensor("offs", [128, E], F32)
    stage2 = nc.alloc_sbuf_tensor("stage2", [128, D], BF16)

    psum = {
        "a": nc.alloc_psum_tensor("ps_a", [128, 512], F32),
        "b": nc.alloc_psum_tensor("ps_b", [128, 512], F32),
        "y": nc.alloc_psum_tensor("ps_y", [128, 512], F32),
        "y2": nc.alloc_psum_tensor("ps_y2", [128, 512], F32),
        "tr": nc.alloc_psum_tensor("ps_tr", [128, 1024], BF16),
        "cb": nc.alloc_psum_tensor("ps_cb", [128, 512], F32),
        "sm": nc.alloc_psum_tensor("ps_sm", [128, 64], F32),
    }
    ps_o = psum["sm"]

    r_x = p.dma("sync", xt.ap().rearrange("p (t d) -> p t d", t=NT),
                x_in.rearrange("(t p) d -> p t d", p=128), stream="ld_x")
    r_rt = p.dma("sync", rtb.ap().rearrange("p (t e) -> p t e", t=NT),
                 rt_in.rearrange("(t p) e -> p t e", p=128), stream="ld_rt")
    r_cn = p.dma("sync", cnts[:], cnts_in, stream="ld_cn")
    r_cm = p.dma("sync", cmask[:], cmask_in, stream="ld_cm")

    r_of1 = p.op("tensor", lambda te: te.matmul(
        ps_o[0:1, 0:E], cmask[:], cnts[:], start=True, stop=True),
        deps=[r_cn, r_cm] + cst_refs)
    r_of2 = p.op("vector", lambda v: v.tensor_copy(offs[0:1, :], ps_o[0:1, 0:E]),
                 deps=[r_of1])
    r_of3 = p.op("tensor", lambda te: te.matmul(
        ps_o[:, 0:E], cst["onesrow"][:], offs[0:1, :], start=True, stop=True),
        deps=[r_of2])
    r_offs = p.op("vector", lambda v: v.tensor_copy(offs[:], ps_o[:, 0:E]),
                  deps=[r_of3])

    ps_ti = psum["tr"]
    last_tr = None
    tr_refs = []
    for t in range(NT):
        r_l = p.dma("sync", stage2[:],
                    xln_in[t * 128:(t + 1) * 128, :], stream=f"xlnld{t % 2}",
                    deps=[last_tr] if last_tr is not None else cst_refs)
        for k in range(KT):
            p.op("tensor", lambda te, k=k: te.transpose(
                ps_ti[:, k * 128:(k + 1) * 128],
                stage2[:, k * 128:(k + 1) * 128], cst["ident"][:]),
                deps=([r_l] + ([last_tr] if last_tr is not None else [])
                      if k == 0 else ()),
                inc=(k == KT - 1))
        r_tg = Ref(p.esem["tensor"], p.ecnt["tensor"], "tensor")
        r_c = p.op("vector", lambda v, t=t: v.tensor_copy(
            xlnT[:, t * D:(t + 1) * D], ps_ti[:]), deps=[r_tg])
        last_tr = r_c
        tr_refs.append(r_c)

    ffn_done = emit_dense_ffn(nc, p, cst, "f", xt, xlnT, rtb, offs, scr,
                              w1_in, b1_in, w2_in, b2_in,
                              [r_x, r_rt] + tr_refs, r_offs, psum)

    r_sx = p.dma("sync", x1_out.rearrange("(t p) d -> p t d", p=128),
                 xt.ap().rearrange("p (t d) -> p t d", t=NT), stream="stx",
                 deps=ffn_done)
    finals = [r_sx]

    if with_qkv:
        ln_refs = emit_ln_transpose_all(nc, p, cst, "q", xt, xlnT, scr,
                                        deps=ffn_done, ps_t=psum["tr"],
                                        stage=stage2)
        finals += emit_qkv(nc, p, cst, xt, xlnT, scr, ln_refs,
                           wq_in, wk_in, wv_in, ropec_in, ropes_in,
                           qT_out, kT_out, v_out,
                           hbuf=_ffn_bufs["hbuf"], accT=_ffn_bufs["accT"],
                           psum=psum)

    p.op("sync", lambda sy: sy.nop(), deps=finals)
    p.emit()
    return nc


def emit_qkv(nc, p, cst, xt, xlnT, scr, ln_refs,
             wq_in, wk_in, wv_in, ropec_in, ropes_in, qT_out, kT_out, v_out,
             hbuf=None, accT=None, psum=None):
    hbF = hbuf.ap().bitcast(F32)
    ropec = hbF[:, 0:TPC]
    ropes = hbF[:, TPC:2 * TPC]
    qs = hbF[:, 2 * TPC:2 * TPC + 512]
    qsh = hbF[:, 2 * TPC + 512:2 * TPC + 1024]
    wqt = [hbuf.ap()[:, 2 * 2 * TPC + 2048 + i * KT * 128:
                     2 * 2 * TPC + 2048 + (i + 1) * KT * 128]
           for i in range(2)]
    acB = accT.ap()
    wvt = acB[:, 0:KT * 1024]
    qcb = [acB[:, KT * 1024 + i * 512:KT * 1024 + (i + 1) * 512]
           for i in range(2)]
    ps_q = psum["a"]
    ps_q2 = psum["b"]

    r_rc = p.dma("gpsimd", ropec, ropec_in, stream="ropec", deps=ln_refs[:1])
    r_rs = p.dma("gpsimd", ropes, ropes_in, stream="ropes", deps=ln_refs[:1])
    finals = []
    NCH4 = TPC // 512
    wstream = {}
    ps_rd = {}
    qc_user = {}
    nchunk = 0
    for which, w_in, outT in (("q", wq_in, qT_out), ("k", wk_in, kT_out)):
        for ot in range(KT):
            par = ot % 2
            wb = wqt[par]
            wdeps = ([wstream[par]] if par in wstream else []) + ln_refs[:1]
            r_w = p.dma("gpsimd", wb.rearrange("p (k c) -> p k c", k=KT),
                        w_in[:, ot * 128:(ot + 1) * 128].rearrange(
                            "(k p) c -> p k c", p=128),
                        stream=f"w{which}{par}", deps=wdeps)
            for ch in range(NCH4):
                ps = ps_q if ch % 2 == 0 else ps_q2
                pname = ch % 2
                for k in range(KT):
                    dd = []
                    if k == 0:
                        dd = [r_w] + ln_refs
                        if pname in ps_rd:
                            dd.append(ps_rd[pname])
                    p.op("tensor", lambda te, wb=wb, k=k, ch=ch, ps=ps:
                         te.matmul(
                             ps[:], wb[:, k * 128:(k + 1) * 128],
                             xlnT.ap().rearrange("p (t d) -> p t d", t=NT)[
                                 :, ch * 4:ch * 4 + 4, k * 128:(k + 1) * 128],
                             start=(k == 0), stop=(k == KT - 1)),
                         deps=dd, inc=(k == KT - 1))
                r_mm = Ref(p.esem["tensor"], p.ecnt["tensor"], "tensor")
                if ch == NCH4 - 1:
                    wstream[par] = r_mm
                r_ev = p.op("vector", lambda v, ps=ps: v.tensor_copy(
                    qs[:], ps[:]), deps=[r_mm])
                ps_rd[pname] = r_ev
                r_sh = p.op("vector", lambda v: v.stream_shuffle(
                    qsh[:], qs[:], [i ^ 1 for i in range(32)]), deps=[r_ev])
                r_m1 = p.op("vector", lambda v, ch=ch: v.tensor_mul(
                    qs[:], qs[:], ropec[:, ch * 512:(ch + 1) * 512]),
                    deps=[r_sh, r_rc])
                r_m2 = p.op("vector", lambda v, ch=ch: v.tensor_mul(
                    qsh[:], qsh[:], ropes[:, ch * 512:(ch + 1) * 512]),
                    deps=[r_m1, r_rs])
                qb = nchunk % 2
                addeps = [r_m2]
                if qb in qc_user:
                    addeps.append(qc_user[qb])
                r_ad = p.op("vector", lambda v, qb=qb: v.tensor_add(
                    qcb[qb], qs[:], qsh[:]), deps=addeps)
                r_st = p.dma("sync", outT[ot * 128:(ot + 1) * 128,
                                          ch * 512:(ch + 1) * 512],
                             qcb[qb], stream=f"st{which}{qb}", deps=[r_ad])
                qc_user[qb] = r_st
                nchunk += 1
                finals.append(r_st)
    r_wv = p.dma("gpsimd", wvt.rearrange("p (k c) -> p k c", k=KT),
                 wv_in.rearrange("(k p) c -> p k c", p=128), stream="wv",
                 deps=ln_refs[:1])
    for tt in range(NT):
        for dc in range(2):
            ps = ps_q if dc == 0 else ps_q2
            pname = dc
            for k in range(KT):
                dd = []
                if k == 0:
                    dd = [r_wv] + ln_refs
                    if pname in ps_rd:
                        dd.append(ps_rd[pname])
                p.op("tensor", lambda te, tt=tt, dc=dc, k=k, ps=ps:
                     te.matmul(
                         ps[:],
                         xlnT[:, tt * D + k * 128: tt * D + (k + 1) * 128],
                         wvt[:, k * 1024 + dc * 512: k * 1024 + (dc + 1) * 512],
                         start=(k == 0), stop=(k == KT - 1)),
                     deps=dd, inc=(k == KT - 1))
            r_mm = Ref(p.esem["tensor"], p.ecnt["tensor"], "tensor")
            qb = nchunk % 2
            evdeps = [r_mm]
            if qb in qc_user:
                evdeps.append(qc_user[qb])
            r_ev = p.op("vector", lambda v, ps=ps, qb=qb: v.tensor_copy(
                qcb[qb], ps[:]), deps=evdeps)
            ps_rd[pname] = r_ev
            r_st = p.dma("sync", v_out[tt * 128:(tt + 1) * 128,
                                       dc * 512:(dc + 1) * 512],
                         qcb[qb], stream=f"stv{qb}", deps=[r_ev])
            qc_user[qb] = r_st
            nchunk += 1
            finals.append(r_st)
    return finals


NHPC = 8


def build_L3():
    nc = bass.Bass("TRN2", target_bir_lowering=False, debug=False,
                   detect_race_conditions=False)
    p = Prog(nc)
    qh_in = nc.dram_tensor("qh", [NHPC, DK, S], BF16, kind="ExternalInput").ap()
    kh_in = nc.dram_tensor("kh", [NHPC, DK, S], BF16, kind="ExternalInput").ap()
    vh_in = nc.dram_tensor("vh", [NHPC, S, DK], BF16, kind="ExternalInput").ap()
    oh_out = nc.dram_tensor("oh", [NHPC, DK, S], BF16, kind="ExternalOutput").ap()

    cst, cst_refs = alloc_consts(nc, p)
    NQT = S // 128

    qT = [nc.alloc_sbuf_tensor(f"qT{i}", [64, S], BF16) for i in range(2)]
    kT = [nc.alloc_sbuf_tensor(f"kT{i}", [64, S], BF16) for i in range(2)]
    vt = [nc.alloc_sbuf_tensor(f"vt{i}", [128, NQT * DK], BF16)
          for i in range(2)]
    sqf = nc.alloc_sbuf_tensor("sqf", [64, S], F32)
    nrow = nc.alloc_sbuf_tensor("nrow", [1, 2 * S + 8], F32)
    mrow = nc.alloc_sbuf_tensor("mrow", [1, S], F32)
    mcol = nc.alloc_sbuf_tensor("mcol", [128, NQT], F32)
    P = nc.alloc_sbuf_tensor("P", [128, S], BF16)
    PT = nc.alloc_sbuf_tensor("PT", [128, S], BF16)
    rs = nc.alloc_sbuf_tensor("rs", [128, 16], F32)
    dnm = nc.alloc_sbuf_tensor("dnm", [128, 2], F32)
    obuf = nc.alloc_sbuf_tensor("obuf", [64, S], BF16)

    ps_s = [nc.alloc_psum_tensor(f"ps_s{i}", [128, 512], F32) for i in range(2)]
    ps_tr = nc.alloc_psum_tensor("ps_tr", [128, 1024], BF16)
    ps_o = nc.alloc_psum_tensor("ps_o", [64, 128], F32)
    ps_m = nc.alloc_psum_tensor("ps_m", [128, NQT], F32)
    ps_n = nc.alloc_psum_tensor("ps_n", [1, 512], F32)

    ld_refs = {}
    rd_last = {}
    ps_rd = {}
    r_store = None

    def load_head(j):
        par = j % 2
        deps = []
        for key in (f"q{par}", f"k{par}", f"v{par}"):
            if key in rd_last:
                deps.append(rd_last[key])
        r1 = p.dma("sync", qT[par][:, :], qh_in[j], stream=f"ldq{par}",
                   deps=deps)
        r2 = p.dma("sync", kT[par][:, :], kh_in[j], stream=f"ldk{par}",
                   deps=deps)
        r3 = p.dma("sync", vt[par].ap().rearrange("p (t d) -> p t d", t=NQT),
                   vh_in[j].rearrange("(t p) d -> p t d", p=128),
                   stream=f"ldv{par}", deps=deps)
        return [r1, r2, r3]

    ld_refs[0] = load_head(0)
    for j in range(NHPC):
        par = j % 2
        r_q, r_k, r_v = ld_refs[j]
        if j + 1 < NHPC:
            ld_refs[j + 1] = load_head(j + 1)
        r_sq = p.op("scalar", lambda s, par=par: s.activation(
            sqf[:], qT[par][:, :], AF.Square), deps=[r_q] + cst_refs)
        nq_parts = []
        for c in range(8):
            r_nm = p.op("tensor", lambda te, c=c, par=par: te.matmul(
                ps_n[:], cst["ones64"][:], sqf[:, c * 512:(c + 1) * 512],
                start=True, stop=True),
                deps=[r_sq] + ([ps_rd["n"]] if "n" in ps_rd else []))
            r_nc = p.op("vector", lambda v, c=c: v.tensor_copy(
                nrow[0:1, c * 512:(c + 1) * 512], ps_n[:]), deps=[r_nm])
            ps_rd["n"] = r_nc
            nq_parts.append(r_nc)
        r_sqk = p.op("scalar", lambda s, par=par: s.activation(
            sqf[:], kT[par][:, :], AF.Square), deps=[r_k] + nq_parts)
        nk_parts = []
        for c in range(8):
            r_nm = p.op("tensor", lambda te, c=c: te.matmul(
                ps_n[:], cst["ones64"][:], sqf[:, c * 512:(c + 1) * 512],
                start=True, stop=True), deps=[r_sqk, ps_rd["n"]])
            r_nc = p.op("vector", lambda v, c=c: v.tensor_copy(
                nrow[0:1, S + c * 512:S + (c + 1) * 512], ps_n[:]),
                deps=[r_nm])
            ps_rd["n"] = r_nc
            nk_parts.append(r_nc)
        r_mx = p.op("vector", lambda v: v.reduce_max(
            nrow[0:1, 2 * S:2 * S + 1], nrow[0:1, S:2 * S], axis=AX.X),
            deps=nk_parts)
        r_sq1 = p.op("scalar", lambda s: s.activation(
            mrow[0:1, :], nrow[0:1, 0:S], AF.Sqrt), deps=nq_parts + [r_mx])
        r_sq2 = p.op("scalar", lambda s: s.activation(
            nrow[0:1, 2 * S + 1:2 * S + 2], nrow[0:1, 2 * S:2 * S + 1],
            AF.Sqrt), deps=[r_mx])
        r_mr = p.op("vector", lambda v: v.tensor_scalar(
            mrow[0:1, :], mrow[0:1, :], nrow[0:1, 2 * S + 1:2 * S + 2],
            -0.125, ALU.mult, ALU.mult), deps=[r_sq1, r_sq2])
        for qt in range(NQT):
            p.op("tensor", lambda te, qt=qt: te.matmul(
                ps_m[:, qt:qt + 1], mrow[0:1, qt * 128:(qt + 1) * 128],
                cst["ones11"][:], start=True, stop=True),
                deps=([r_mr] + ([ps_rd["m"]] if "m" in ps_rd else [])
                      if qt == 0 else ()),
                inc=(qt == NQT - 1))
        r_mt = Ref(p.esem["tensor"], p.ecnt["tensor"], "tensor")
        r_mc = p.op("vector", lambda v: v.tensor_copy(mcol[:], ps_m[:]),
                    deps=[r_mt])
        ps_rd["m"] = r_mc

        prev_ptr = None
        prev_exp = None
        sc_group = None
        av_group = None
        for qt in range(NQT):
            L = 128 * (qt + 1)
            nfull = L // 512
            tail = L - nfull * 512
            if tail == 0:
                nfull -= 1
                tail = 512
            exp_refs = []
            for c in range(nfull + 1):
                w = 512 if c < nfull else tail
                pp = c % 2
                sdeps = []
                if f"s{pp}" in ps_rd:
                    sdeps.append(ps_rd[f"s{pp}"])
                if c == 0:
                    sdeps += [r_k, r_mc]
                p.op("tensor", lambda te, c=c, w=w, qt=qt, par=par, pp=pp:
                     te.matmul(ps_s[pp][:, 0:w],
                               qT[par][:, qt * 128:(qt + 1) * 128],
                               kT[par][:, c * 512:c * 512 + w],
                               start=True, stop=True),
                     deps=sdeps)
                r_sc = Ref(p.esem["tensor"], p.ecnt["tensor"], "tensor")
                sc_group = r_sc
                edeps = [r_sc]
                if prev_exp is not None:
                    edeps.append(prev_exp)
                if c == 0 and prev_ptr is not None:
                    edeps.append(prev_ptr)
                if c < nfull:
                    r_e = p.op("scalar", lambda s, c=c, w=w, qt=qt, pp=pp:
                               s.activation(
                                   P[:, c * 512:c * 512 + w], ps_s[pp][:, 0:w],
                                   AF.Exp, bias=mcol[:, qt:qt + 1],
                                   scale=0.125, accum_out=rs[:, c:c + 1]),
                               deps=edeps)
                else:
                    r_e = p.op("scalar", lambda s, c=c, w=w, qt=qt, pp=pp:
                               s.activation(
                                   P[:, c * 512:c * 512 + w], ps_s[pp][:, 0:w],
                                   AF.Exp, bias=mcol[:, qt:qt + 1],
                                   scale=0.125),
                               deps=edeps)
                ps_rd[f"s{pp}"] = r_e
                prev_exp = r_e
                exp_refs.append(r_e)
            r_mk = p.op("vector", lambda v, L=L: v.tensor_mul(
                P[:, L - 128:L], P[:, L - 128:L], cst["tril"][:]),
                deps=[exp_refs[-1]])
            r_ts = p.op("vector", lambda v, qt=qt, nfull=nfull, tail=tail:
                        v.reduce_sum(rs[:, nfull:nfull + 1],
                                     P[:, nfull * 512:nfull * 512 + tail],
                                     axis=AX.X), deps=[r_mk])
            r_dn = p.op("vector", lambda v, nfull=nfull: v.reduce_sum(
                dnm[:, 0:1], rs[:, 0:nfull + 1], axis=AX.X),
                deps=[r_ts] + exp_refs)
            r_rc = p.op("vector", lambda v: v.reciprocal(
                dnm[:, 1:2], dnm[:, 0:1]), deps=[r_dn])
            r_nm = p.op("vector", lambda v, L=L: v.tensor_scalar(
                P[:, 0:L], P[:, 0:L], dnm[:, 1:2], None, ALU.mult),
                deps=[r_rc])
            pt_evacs = []
            nkt = qt + 1
            for blk in range((nkt + 7) // 8):
                k0 = blk * 8
                k1 = min(k0 + 8, nkt)
                for ki in range(k0, k1):
                    p.op("tensor", lambda te, ki=ki, k0=k0: te.transpose(
                        ps_tr[:, (ki - k0) * 128:(ki - k0 + 1) * 128],
                        P[:, ki * 128:(ki + 1) * 128], cst["ident"][:]),
                        deps=([r_nm] + ([ps_rd["tr"]] if "tr" in ps_rd else [])
                              if ki == k0 else ()),
                        inc=(ki == k1 - 1))
                r_tg = Ref(p.esem["tensor"], p.ecnt["tensor"], "tensor")
                r_pc = p.op("vector", lambda v, k0=k0, k1=k1: v.tensor_copy(
                    PT[:, k0 * 128:k1 * 128],
                    ps_tr[:, 0:(k1 - k0) * 128]), deps=[r_tg])
                ps_rd["tr"] = r_pc
                pt_evacs.append(r_pc)
            prev_ptr = pt_evacs[-1]
            for ki in range(nkt):
                avdeps = []
                if ki == 0:
                    avdeps = pt_evacs + [r_v]
                    if "o" in ps_rd:
                        avdeps.append(ps_rd["o"])
                p.op("tensor", lambda te, ki=ki, par=par, nkt=nkt: te.matmul(
                    ps_o[:], vt[par][:, ki * DK:(ki + 1) * DK],
                    PT[:, ki * 128:(ki + 1) * 128],
                    start=(ki == 0), stop=(ki == nkt - 1)),
                    deps=avdeps, inc=(ki == nkt - 1))
            r_av = Ref(p.esem["tensor"], p.ecnt["tensor"], "tensor")
            av_group = r_av
            r_oe = p.op("vector", lambda v, qt=qt: v.tensor_copy(
                obuf[:, qt * 128:(qt + 1) * 128], ps_o[:]),
                deps=[r_av] + ([r_store] if r_store is not None and qt == 0
                               else []))
            ps_rd["o"] = r_oe
        rd_last[f"q{par}"] = sc_group
        rd_last[f"k{par}"] = sc_group
        rd_last[f"v{par}"] = av_group
        r_store = p.dma("sync", oh_out[j], obuf[:, :], stream=f"sto{j % 2}",
                        deps=[ps_rd["o"]])
    p.op("sync", lambda sy: sy.nop(), deps=[r_store])
    p.emit()
    return nc


def build_L4():
    nc = bass.Bass("TRN2", target_bir_lowering=False, debug=False,
                   detect_race_conditions=False)
    p = Prog(nc)
    x1_in = nc.dram_tensor("x1", [TPC, D], F32, kind="ExternalInput").ap()
    aoT_in = nc.dram_tensor("aoT", [D, TPC], BF16, kind="ExternalInput").ap()
    wo_in = nc.dram_tensor("wo", [D, D], F32, kind="ExternalInput").ap()
    wg_in = nc.dram_tensor("wg", [D, E], F32, kind="ExternalInput").ap()
    _WG_IN[0] = wg_in
    x2_out = nc.dram_tensor("x2", [TPC, D], F32, kind="ExternalOutput").ap()
    xln_out = nc.dram_tensor("xln", [TPC, D], BF16, kind="ExternalOutput").ap()
    cnt_out = nc.dram_tensor("cnt", [1, E], F32, kind="ExternalOutput").ap()
    rt_out = nc.dram_tensor("rt", [TPC, 8], F32, kind="ExternalOutput").ap()

    cst, cst_refs = alloc_consts(nc, p)
    xt = nc.alloc_sbuf_tensor("xt", [128, NT * D], F32)
    aoT = nc.alloc_sbuf_tensor("s_aoT", [128, KT * TPC], BF16)
    wot = nc.alloc_sbuf_tensor("wot", [128, KT * D], BF16)
    xsc = nc.alloc_sbuf_tensor("xsc", [128, 512], F32)
    ps_pr = [nc.alloc_psum_tensor(f"ps_pr{i}", [128, 512], F32)
             for i in range(2)]

    r_x = p.dma("sync", xt.ap().rearrange("p (t d) -> p t d", t=NT),
                x1_in.rearrange("(t p) d -> p t d", p=128), stream="xload")
    r_ao = p.dma("sync", aoT.ap().rearrange("p (k t) -> p k t", k=KT),
                 aoT_in.rearrange("(k p) t -> p k t", p=128), stream="aoload")
    r_wo = p.dma("gpsimd", wot.ap().rearrange("p (k c) -> p k c", k=KT),
                 wo_in.rearrange("(k p) c -> p k c", p=128), stream="woload")

    ps_rd = {}
    proj_done = []
    for tt in range(NT):
        for dc in range(2):
            pp = dc
            for k in range(KT):
                dd = []
                if k == 0:
                    dd = [r_ao, r_wo] + cst_refs
                    if f"pr{pp}" in ps_rd:
                        dd.append(ps_rd[f"pr{pp}"])
                p.op("tensor", lambda te, tt=tt, dc=dc, k=k, pp=pp: te.matmul(
                    ps_pr[pp][:],
                    aoT[:, k * TPC + tt * 128: k * TPC + (tt + 1) * 128],
                    wot[:, k * D + dc * 512: k * D + (dc + 1) * 512],
                    start=(k == 0), stop=(k == KT - 1)),
                    deps=dd, inc=(k == KT - 1))
            r_mm = Ref(p.esem["tensor"], p.ecnt["tensor"], "tensor")
            r_ad = p.op("vector", lambda v, tt=tt, dc=dc, pp=pp: v.tensor_add(
                xt[:, tt * D + dc * 512: tt * D + (dc + 1) * 512],
                xt[:, tt * D + dc * 512: tt * D + (dc + 1) * 512],
                ps_pr[pp][:]), deps=[r_mm, r_x])
            ps_rd[f"pr{pp}"] = r_ad
            proj_done.append(r_ad)

    xln, rt, carry, rt_done, r_cz = emit_ln_routing(
        nc, p, cst, "r", xt, proj_done + cst_refs)

    s0 = p.dma("sync", x2_out.rearrange("(t p) d -> p t d", p=128),
               xt.ap().rearrange("p (t d) -> p t d", t=NT), stream="st",
               deps=proj_done + rt_done)
    s1 = p.dma("sync", xln_out.rearrange("(t p) d -> p t d", p=128),
               xln.ap().rearrange("p (t d) -> p t d", t=NT), stream="st",
               deps=rt_done)
    s2 = p.dma("sync", rt_out.rearrange("(t p) e -> p t e", p=128),
               rt.ap().rearrange("p (t e) -> p t e", t=NT), stream="st",
               deps=rt_done)
    s3 = p.dma("sync", cnt_out, carry[:], stream="st", deps=[r_cz])
    p.op("sync", lambda sy: sy.nop(), deps=[s0, s1, s2, s3])
    p.emit()
    return nc


def build_L5():
    return build_L2(with_qkv=False, name="L5")


def _rope_tables(core):
    pos = (core % 2) * TPC + np.arange(TPC)
    inv = (1.0 / 10000.0 ** (np.arange(0, DK, 2, dtype=np.float32) / DK)
           ).astype(np.float32)
    ang = np.outer(pos.astype(np.float32), inv).astype(np.float32)
    didx = np.arange(128) % 64
    ct = np.cos(ang)[:, didx // 2].T.astype(np.float32)
    st = np.sin(ang)[:, didx // 2].T.astype(np.float32)
    st = st * np.where(didx % 2 == 1, 1.0, -1.0)[:, None].astype(np.float32)
    return np.ascontiguousarray(ct), np.ascontiguousarray(st)


def kernel(L0_attn_x, L0_attn_Wq, L0_attn_Wk, L0_attn_Wv, L0_attn_Wo,
           L0_ffn_Wg, L0_ffn_W1, L0_ffn_b1, L0_ffn_W2, L0_ffn_b2):
    xf = np.ascontiguousarray(np.asarray(L0_attn_x, np.float32)
                              ).reshape(NTOK, D)
    shards = [np.ascontiguousarray(xf[c * TPC:(c + 1) * TPC])
              for c in range(NCORES)]
    wg = np.ascontiguousarray(np.asarray(L0_ffn_Wg, np.float32))
    w1 = np.ascontiguousarray(np.asarray(L0_ffn_W1, np.float32))
    b1 = np.ascontiguousarray(np.asarray(L0_ffn_b1, np.float32))
    w2 = np.ascontiguousarray(np.asarray(L0_ffn_W2, np.float32))
    b2 = np.ascontiguousarray(np.asarray(L0_ffn_b2, np.float32))
    wq = np.ascontiguousarray(np.asarray(L0_attn_Wq, np.float32))
    wk = np.ascontiguousarray(np.asarray(L0_attn_Wk, np.float32))
    wv = np.ascontiguousarray(np.asarray(L0_attn_Wv, np.float32))
    wo = np.ascontiguousarray(np.asarray(L0_attn_Wo, np.float32))
    cores = list(range(NCORES))

    w1f = w1.reshape(-1, 8192)
    w2f = w2.reshape(-1, 8192)
    nsl = w1f.shape[0] // NCORES
    nc1 = build_L1()
    r1 = run_bass_kernel_spmd(
        nc1, [{"x": shards[c], "wg": wg,
               "w1s": w1f[c * nsl:(c + 1) * nsl],
               "w2s": w2f[c * nsl:(c + 1) * nsl]} for c in cores],
        core_ids=cores).results
    cnts = np.ascontiguousarray(
        np.concatenate([r1[c]["cnt"] for c in cores], axis=0))
    w1b = np.concatenate([r1[c]["w1b"] for c in cores], axis=0)
    w2b = np.concatenate([r1[c]["w2b"] for c in cores], axis=0)
    w1t = np.ascontiguousarray(
        w1b.reshape(E, KT, 128, HT, 128).transpose(0, 3, 2, 1, 4)
        .reshape(E, HT, 128, KT * 128))
    w2t = np.ascontiguousarray(
        w2b.reshape(E, HT, 128, KT, 128).transpose(0, 3, 2, 1, 4)
        .reshape(E, KT, 128, HT * 128))

    nc2 = build_L2()
    ims = []
    for c in cores:
        ct, st = _rope_tables(c)
        cm = np.zeros((8, 1), np.float32)
        cm[:c] = 1.0
        ims.append({"x": shards[c], "xln": r1[c]["xln"], "rt": r1[c]["rt"],
                    "cnts": cnts, "cmask": cm, "w1": w1t, "b1": b1, "w2": w2t,
                    "b2": b2, "wq": wq, "wk": wk, "wv": wv,
                    "ropec": ct, "ropes": st})
    r2 = run_bass_kernel_spmd(nc2, ims, core_ids=cores).results

    nc3 = build_L3()
    ims3 = []
    for c in cores:
        b = c // 2
        qTb = np.concatenate([r2[2 * b]["qT"], r2[2 * b + 1]["qT"]], axis=1)
        kTb = np.concatenate([r2[2 * b]["kT"], r2[2 * b + 1]["kT"]], axis=1)
        vb = np.concatenate([r2[2 * b]["v"], r2[2 * b + 1]["v"]], axis=0)
        hs = [(c % 2) * NHPC + i for i in range(NHPC)]
        qh = np.ascontiguousarray(
            np.stack([qTb[h * DK:(h + 1) * DK] for h in hs]))
        kh = np.ascontiguousarray(
            np.stack([kTb[h * DK:(h + 1) * DK] for h in hs]))
        vh = np.ascontiguousarray(
            np.stack([vb[:, h * DK:(h + 1) * DK] for h in hs]))
        ims3.append({"qh": qh, "kh": kh, "vh": vh})
    r3 = run_bass_kernel_spmd(nc3, ims3, core_ids=cores).results

    nc4 = build_L4()
    ims4 = []
    for c in cores:
        b = c // 2
        aoTb = np.concatenate([r3[2 * b]["oh"].reshape(NHEAD // 2 * DK, S),
                               r3[2 * b + 1]["oh"].reshape(NHEAD // 2 * DK, S)],
                              axis=0)
        ao = np.ascontiguousarray(aoTb[:, (c % 2) * TPC:(c % 2 + 1) * TPC])
        ims4.append({"x1": r2[c]["x1"], "aoT": ao, "wo": wo, "wg": wg})
    r4 = run_bass_kernel_spmd(nc4, ims4, core_ids=cores).results
    cnts2 = np.ascontiguousarray(
        np.concatenate([r4[c]["cnt"] for c in cores], axis=0))

    nc5 = build_L5()
    ims5 = []
    for c in cores:
        cm = np.zeros((8, 1), np.float32)
        cm[:c] = 1.0
        ims5.append({"x": r4[c]["x2"], "xln": r4[c]["xln"], "rt": r4[c]["rt"],
                     "cnts": cnts2, "cmask": cm, "w1": w1t, "b1": b1,
                     "w2": w2t, "b2": b2})
    r5 = run_bass_kernel_spmd(nc5, ims5, core_ids=cores).results

    out = np.concatenate([r5[c]["x1"] for c in cores], axis=0)
    return np.ascontiguousarray(out.reshape(B, S, D).astype(np.float32))


if __name__ == "__main__":
    for f in (build_L1, build_L2, build_L3, build_L4, build_L5):
        nc = f()
        print(f.__name__, nc.n_instructions())


# version 03
# speedup vs baseline: 1.0831x; 1.0831x over previous
import numpy as np
import ml_dtypes
import concourse.bass as bass
import concourse.mybir as mybir
from concourse.bass_utils import run_bass_kernel_spmd

AF = mybir.ActivationFunctionType
ALU = mybir.AluOpType
AX = mybir.AxisListType
F32 = mybir.dt.float32
BF16 = mybir.dt.bfloat16
U32 = mybir.dt.uint32
I32 = mybir.dt.int32

B, S, D, HID, E = 4, 4096, 1024, 4096, 8
NTOK = B * S
NCORES = 8
TPC = NTOK // NCORES
NT = TPC // 128
KT = D // 128
HT = HID // 128
CAP = 4301
NHEAD, DK = 16, 64
EPS = 1e-5
GC0 = 0.7978845608028654
GC1 = 0.044715


def bf16np(x):
    return np.ascontiguousarray(x).astype(ml_dtypes.bfloat16)


_ffn_bufs = {}


ENGINES = ("tensor", "vector", "scalar", "gpsimd", "sync")


class Ref:
    __slots__ = ("sem", "cnt", "engine")

    def __init__(self, sem, cnt, engine):
        self.sem, self.cnt, self.engine = sem, cnt, engine


class Prog:
    def __init__(self, nc):
        self.nc = nc
        self.items = {e: [] for e in ENGINES}
        self.esem = {e: nc.alloc_semaphore(f"E_{e}") for e in ENGINES}
        self.ecnt = {e: 0 for e in ENGINES}
        self.elast = {}
        self.dsem = {}
        self.dcnt = {}

    def _dma_sem(self, stream):
        if stream not in self.dsem:
            self.dsem[stream] = self.nc.alloc_semaphore(f"D_{stream}")
            self.dcnt[stream] = 0
        return self.dsem[stream]

    def op(self, engine, fn, deps=(), inc=True):
        sem = self.esem[engine]
        deps = tuple(deps)
        if engine in ("vector", "scalar", "gpsimd"):
            prev = self.elast.get(engine)
            if prev is not None:
                deps = deps + (prev,)
        if inc:
            self.ecnt[engine] += 1
        cnt = self.ecnt[engine]
        self.items[engine].append((fn, deps, sem if inc else None, 1))
        ref = Ref(sem, cnt, engine) if inc else None
        if inc:
            self.elast[engine] = ref
        return ref

    def dma(self, engine, out, in_, stream, deps=(), **kw):
        sem = self._dma_sem(stream)
        self.dcnt[stream] += 16
        cnt = self.dcnt[stream]

        def fn(eng, out=out, in_=in_, kw=kw):
            return eng.dma_start(out=out, in_=in_, **kw)
        self.items[engine].append((fn, tuple(deps), sem, 16))
        return Ref(sem, cnt, engine)

    def idma(self, engine, fn, stream, deps=()):
        sem = self._dma_sem(stream)
        self.dcnt[stream] += 16
        cnt = self.dcnt[stream]
        self.items[engine].append((fn, tuple(deps), sem, 16))
        return Ref(sem, cnt, engine)

    def raw(self, engine, fn):
        self.items[engine].append((fn, (), None, 0))
        return None

    def barrier_all(self):
        snap = [Ref(self.esem[e], self.ecnt[e], e) for e in ENGINES]
        snapd = [Ref(self.dsem[s], self.dcnt[s], None) for s in self.dsem]
        for e in ENGINES:
            self.items[e].append((None, tuple(snap + snapd), None, 0))

    def emit(self):
        nc = self.nc
        with nc.Block() as block:
            for engine in ENGINES:
                items = self.items[engine]

                def make(engine=engine, items=items):
                    def body(eng):
                        waited = {}
                        for fn, deps, sem, inc in items:
                            want = {}
                            for d in deps:
                                if d is None:
                                    continue
                                if want.get(d.sem, -1) < d.cnt:
                                    want[d.sem] = d.cnt
                            for wsem, wcnt in want.items():
                                if waited.get(wsem, -1) >= wcnt:
                                    continue
                                eng.wait_ge(wsem, wcnt)
                                waited[wsem] = wcnt
                            if fn is None:
                                continue
                            inst = fn(eng)
                            if sem is not None:
                                assert inst is not None, "op fn returned None"
                                inst.then_inc(sem, inc)
                    return body
                getattr(block, engine)(make())


def alloc_consts(nc, p):
    c = {}
    tri = nc.inline_tensor(np.triu(np.ones((128, 128), np.float32), 1), "c_tri").ap()
    onescol = nc.inline_tensor(np.ones((128, 1), np.float32), "c_onescol").ap()
    onesrow = nc.inline_tensor(np.ones((1, 128), np.float32), "c_onesrow").ap()
    ident = nc.inline_tensor(np.eye(128, dtype=ml_dtypes.bfloat16), "c_ident").ap()
    iota8 = nc.inline_tensor(
        np.tile(np.arange(8, dtype=np.float32), (128, 1)), "c_iota8").ap()
    ones11 = nc.inline_tensor(np.ones((1, 1), np.float32), "c_ones11").ap()
    identf = nc.inline_tensor(np.eye(128, dtype=np.float32), "c_identf").ap()
    tril = nc.inline_tensor(
        np.tril(np.ones((128, 128), np.float32)).astype(ml_dtypes.bfloat16),
        "c_tril").ap()
    ones64 = nc.inline_tensor(np.ones((64, 1), np.float32), "c_ones64").ap()
    c["tri"] = nc.alloc_sbuf_tensor("c_tri_s", [128, 128], F32)
    c["onescol"] = nc.alloc_sbuf_tensor("c_onescol_s", [128, 1], F32)
    c["onesrow"] = nc.alloc_sbuf_tensor("c_onesrow_s", [1, 128], F32)
    c["ident"] = nc.alloc_sbuf_tensor("c_ident_s", [128, 128], BF16)
    c["iota8"] = nc.alloc_sbuf_tensor("c_iota8_s", [128, 8], F32)
    c["ones11"] = nc.alloc_sbuf_tensor("c_ones11_s", [1, 1], F32)
    c["identf"] = nc.alloc_sbuf_tensor("c_identf_s", [128, 128], F32)
    c["tril"] = nc.alloc_sbuf_tensor("c_tril_s", [128, 128], BF16)
    c["ones64"] = nc.alloc_sbuf_tensor("c_ones64_s", [64, 1], F32)
    c["onesrowf"] = c["onesrow"]
    refs = []
    for name, src in [("tri", tri), ("onescol", onescol), ("onesrow", onesrow),
                      ("ident", ident), ("iota8", iota8), ("ones11", ones11),
                      ("identf", identf), ("tril", tril), ("ones64", ones64)]:
        refs.append(p.dma("sync", c[name][:], src, stream="consts"))
    return c, refs


def emit_ln_tile(p, xs, xln_out, sq, stcols, deps=()):
    r_sum = p.op("vector", lambda v: v.reduce_sum(stcols[:, 0:1], xs, axis=AX.X),
                 deps=deps)
    r_sq = p.op("scalar", lambda s: s.activation(sq[:], xs, AF.Square,
                                                accum_out=stcols[:, 1:2]),
                deps=deps)
    r_m = p.op("vector", lambda v: v.tensor_scalar(
        stcols[:, 2:3], stcols[:, 0:1], 1.0 / D, None, ALU.mult), deps=[r_sum])
    r_v1 = p.op("vector", lambda v: v.scalar_tensor_tensor(
        stcols[:, 3:4], stcols[:, 2:3], -1.0, stcols[:, 2:3], ALU.mult,
        ALU.mult), deps=[r_m])
    r_v2 = p.op("vector", lambda v: v.scalar_tensor_tensor(
        stcols[:, 3:4], stcols[:, 1:2], 1.0 / D, stcols[:, 3:4], ALU.mult,
        ALU.add), deps=[r_sq, r_v1])
    r_v3 = p.op("vector", lambda v: v.tensor_scalar(
        stcols[:, 3:4], stcols[:, 3:4], EPS, None, ALU.add), deps=[r_v2])
    r_sq2 = p.op("scalar", lambda s: s.activation(
        stcols[:, 4:5], stcols[:, 3:4], AF.Sqrt), deps=[r_v3])
    r_rc = p.op("vector", lambda v: v.reciprocal(
        stcols[:, 5:6], stcols[:, 4:5]), deps=[r_sq2])
    r_ap = p.op("vector", lambda v: v.tensor_scalar(
        xln_out, xs, stcols[:, 2:3], stcols[:, 5:6],
        ALU.subtract, ALU.mult), deps=[r_rc])
    return r_ap


def emit_transpose_1024(p, src_bf16, dst_bf16, ps_t, ident, deps):
    for k in range(KT):
        last = (k == KT - 1)
        p.op("tensor", lambda t, k=k: t.transpose(
            ps_t[:, k * 128:(k + 1) * 128],
            src_bf16[:, k * 128:(k + 1) * 128], ident[:]),
            deps=(deps if k == 0 else ()), inc=last)
    r_t = Ref(p.esem["tensor"], p.ecnt["tensor"], "tensor")
    r_c = p.op("vector", lambda v: v.tensor_copy(dst_bf16, ps_t[:]),
               deps=[r_t])
    return r_c


def emit_ln_routing(nc, p, cst, pref, xt, gate_deps):
    sq = nc.alloc_sbuf_tensor(f"{pref}_sq", [128, D], F32)
    stats = nc.alloc_sbuf_tensor(f"{pref}_stats", [128, 8], F32)
    xln = nc.alloc_sbuf_tensor(f"{pref}_xln", [128, NT * D], BF16)
    xlnT = nc.alloc_sbuf_tensor(f"{pref}_xlnT", [128, D], BF16)
    logits = nc.alloc_sbuf_tensor(f"{pref}_logits", [128, E], F32)
    top8v = nc.alloc_sbuf_tensor(f"{pref}_top8v", [128, 8], F32)
    top8i = nc.alloc_sbuf_tensor(f"{pref}_top8i", [128, 8], U32)
    ohbuf = nc.alloc_sbuf_tensor(f"{pref}_ohbuf", [128, 3 * E], F32)
    carry = nc.alloc_sbuf_tensor(f"{pref}_carry", [1, E], F32)
    rt = nc.alloc_sbuf_tensor(f"{pref}_rt", [128, NT * 8], F32)
    wgt = nc.alloc_sbuf_tensor(f"{pref}_wgt", [128, KT * E], BF16)

    ps_t = nc.alloc_psum_tensor(f"{pref}_ps_t", [128, D], BF16)
    ps_l = nc.alloc_psum_tensor(f"{pref}_ps_l", [128, E], F32)
    ps_p = nc.alloc_psum_tensor(f"{pref}_ps_p", [128, E], F32)
    ps_c = nc.alloc_psum_tensor(f"{pref}_ps_c", [1, E], F32)

    r_wg = p.dma("gpsimd", wgt.ap().rearrange("p (k e) -> p k e", k=KT),
                 _WG_IN[0].rearrange("(k p) e -> p k e", p=128),
                 stream=f"{pref}_wg")
    r_cz = p.op("vector", lambda v: v.memset(carry[:], 0.0))
    r_rtz = p.op("vector", lambda v: v.memset(rt[:], 0.0))

    gate = list(gate_deps) + [r_wg, r_rtz]
    rt_done = []
    prev_lgc = None
    for t in range(NT):
        xs = xt[:, t * D:(t + 1) * D]
        r_ln = emit_ln_tile(p, xs, xln[:, t * D:(t + 1) * D], sq, stats,
                            deps=(gate if t == 0 else ()))
        r_tr = emit_transpose_1024(p, xln[:, t * D:(t + 1) * D], xlnT[:],
                                   ps_t, cst["ident"],
                                   deps=[r_ln] + (gate if t == 0 else []))
        for k in range(KT):
            kd = []
            if k == 0:
                kd = [r_tr, r_wg]
                if prev_lgc is not None:
                    kd.append(prev_lgc)
            p.op("tensor", lambda te, k=k: te.matmul(
                ps_l[:], xlnT[:, k * 128:(k + 1) * 128],
                wgt[:, k * E:(k + 1) * E],
                start=(k == 0), stop=(k == KT - 1)),
                deps=kd, inc=(k == KT - 1))
        r_lg = Ref(p.esem["tensor"], p.ecnt["tensor"], "tensor")
        r_lgc = p.op("vector", lambda v: v.tensor_copy(logits[:], ps_l[:]),
                     deps=[r_lg])
        prev_lgc = r_lgc
        r_t8 = p.op("vector", lambda v: v.max(top8v[:], logits[:]),
                    deps=[r_lgc])
        r_i8 = p.op("vector", lambda v: v.max_index(top8i[:], top8v[:],
                                                    logits[:]), deps=[r_t8])
        ro = t * 8
        r_d12 = p.op("vector", lambda v: v.tensor_sub(
            stats[:, 6:7], top8v[:, 0:1], top8v[:, 1:2]), deps=[r_t8])
        r_sg = p.op("scalar", lambda s, ro=ro: s.activation(
            rt[:, ro + 2:ro + 3], stats[:, 6:7], AF.Sigmoid), deps=[r_d12])
        r_g1 = r_sg
        r_g2 = p.op("vector", lambda v, ro=ro: v.tensor_scalar(
            rt[:, ro + 3:ro + 4], rt[:, ro + 2:ro + 3], -1.0, 1.0,
            ALU.mult, ALU.add), deps=[r_g1])
        r_i1 = p.op("vector", lambda v, ro=ro: v.tensor_copy(
            rt[:, ro + 0:ro + 1], top8i[:, 0:1]), deps=[r_i8])
        r_i2 = p.op("vector", lambda v, ro=ro: v.tensor_copy(
            rt[:, ro + 1:ro + 2], top8i[:, 1:2]), deps=[r_i8])
        r_oh1 = p.op("vector", lambda v, ro=ro: v.tensor_scalar(
            ohbuf[:, 0:E], cst["iota8"][:], rt[:, ro + 0:ro + 1], None,
            ALU.is_equal), deps=[r_i1])
        r_oh2 = p.op("vector", lambda v, ro=ro: v.tensor_scalar(
            ohbuf[:, E:2 * E], cst["iota8"][:], rt[:, ro + 1:ro + 2], None,
            ALU.is_equal), deps=[r_i2])
        r_ohs = p.op("vector", lambda v: v.tensor_add(
            ohbuf[:, 2 * E:3 * E], ohbuf[:, 0:E], ohbuf[:, E:2 * E]),
            deps=[r_oh1, r_oh2])
        p.op("tensor", lambda te: te.matmul(
            ps_p[:], cst["tri"][:], ohbuf[:, 2 * E:3 * E],
            start=True, stop=False), deps=[r_ohs], inc=False)
        p.op("tensor", lambda te: te.matmul(
            ps_p[:], cst["onesrow"][:], carry[:],
            start=False, stop=True), deps=[r_cz], inc=True)
        r_pos = Ref(p.esem["tensor"], p.ecnt["tensor"], "tensor")
        r_tc = p.op("tensor", lambda te: te.matmul(
            ps_c[:], cst["onescol"][:], ohbuf[:, 2 * E:3 * E],
            start=True, stop=True), deps=[r_ohs])
        r_p1m = p.op("vector", lambda v: v.tensor_mul(
            ohbuf[:, 0:E], ohbuf[:, 0:E], ps_p[:]), deps=[r_pos])
        r_p1 = p.op("vector", lambda v, ro=ro: v.reduce_sum(
            rt[:, ro + 4:ro + 5], ohbuf[:, 0:E], axis=AX.X), deps=[r_p1m])
        r_p2m = p.op("vector", lambda v: v.tensor_mul(
            ohbuf[:, E:2 * E], ohbuf[:, E:2 * E], ps_p[:]), deps=[r_pos])
        r_p2 = p.op("vector", lambda v, ro=ro: v.reduce_sum(
            rt[:, ro + 5:ro + 6], ohbuf[:, E:2 * E], axis=AX.X), deps=[r_p2m])
        r_cu = p.op("vector", lambda v: v.tensor_add(
            carry[:], carry[:], ps_c[:]), deps=[r_tc, r_p1m, r_p2m])
        r_cz = r_cu
        rt_done += [r_p1, r_p2, r_g2, r_i1, r_i2, r_sg]
    return xln, rt, carry, rt_done, r_cz


_WG_IN = [None]


WSLICE = E * D * HID // NCORES


def build_L1():
    nc = bass.Bass("TRN2", target_bir_lowering=False, debug=False,
                   detect_race_conditions=False)
    p = Prog(nc)
    x_in = nc.dram_tensor("x", [TPC, D], F32, kind="ExternalInput").ap()
    wg_in = nc.dram_tensor("wg", [D, E], F32, kind="ExternalInput").ap()
    w1s_in = nc.dram_tensor("w1s", [WSLICE // 8192, 8192], F32,
                            kind="ExternalInput").ap()
    w2s_in = nc.dram_tensor("w2s", [WSLICE // 8192, 8192], F32,
                            kind="ExternalInput").ap()
    _WG_IN[0] = wg_in
    xln_out = nc.dram_tensor("xln", [TPC, D], BF16, kind="ExternalOutput").ap()
    cnt_out = nc.dram_tensor("cnt", [1, E], F32, kind="ExternalOutput").ap()
    rt_out = nc.dram_tensor("rt", [TPC, 8], F32, kind="ExternalOutput").ap()
    w1b_out = nc.dram_tensor("w1b", [WSLICE // 8192, 8192], BF16,
                             kind="ExternalOutput").ap()
    w2b_out = nc.dram_tensor("w2b", [WSLICE // 8192, 8192], BF16,
                             kind="ExternalOutput").ap()

    cst, cst_refs = alloc_consts(nc, p)
    xt = nc.alloc_sbuf_tensor("xt", [128, NT * D], F32)
    r_x = p.dma("sync", xt.ap().rearrange("p (t d) -> p t d", t=NT),
                x_in.rearrange("(t p) d -> p t d", p=128), stream="xload")

    xln, rt, carry, rt_done, r_cz = emit_ln_routing(
        nc, p, cst, "r", xt, [r_x] + cst_refs)

    wcv = [nc.alloc_sbuf_tensor(f"wcv{i}", [128, 8192], BF16) for i in range(2)]
    NROW = WSLICE // 8192
    RCH = NROW // 128
    conv_done = []
    prev_st = {}
    for which, win, wout in (("w1", w1s_in, w1b_out), ("w2", w2s_in, w2b_out)):
        for ch in range(RCH):
            par = ch % 2
            ldeps = [prev_st[par]] if par in prev_st else []
            r_l = p.dma("gpsimd", wcv[par][:],
                        win[ch * 128:(ch + 1) * 128, :],
                        stream=f"cv{which}{par}", deps=ldeps)
            r_s = p.dma("sync", wout[ch * 128:(ch + 1) * 128, :],
                        wcv[par][:], stream=f"cs{which}{par}", deps=[r_l])
            prev_st[par] = r_s
            conv_done.append(r_s)

    s1 = p.dma("sync", xln_out.rearrange("(t p) d -> p t d", p=128),
               xln.ap().rearrange("p (t d) -> p t d", t=NT), stream="st",
               deps=rt_done)
    s2 = p.dma("sync", rt_out.rearrange("(t p) e -> p t e", p=128),
               rt.ap().rearrange("p (t e) -> p t e", t=NT), stream="st",
               deps=rt_done)
    s3 = p.dma("sync", cnt_out, carry[:], stream="st", deps=[r_cz])
    p.op("sync", lambda sy: sy.nop(), deps=[s1, s2, s3] + conv_done)
    p.emit()
    return nc


def emit_dense_ffn(nc, p, cst, name, xres, xlnT, rtbuf, offs_bcast, scr,
                   w1_in, b1_in, w2_in, b2_in, gate_deps, r_offs, psum):
    HALF = TPC // 4
    NCH = HALF // 512

    coef = nc.alloc_sbuf_tensor(f"{name}_coef", [128, NT * E], F32)
    coefT = nc.alloc_sbuf_tensor(f"{name}_coefT", [1, TPC], F32)
    hbuf = nc.alloc_sbuf_tensor(f"{name}_hbuf", [128, HT * HALF], BF16)
    accT = nc.alloc_sbuf_tensor(f"{name}_accT", [128, KT * TPC], BF16)
    _ffn_bufs["hbuf"] = hbuf
    _ffn_bufs["accT"] = accT
    w1t = [nc.alloc_sbuf_tensor(f"{name}_w1t{i}", [128, KT * 128], BF16)
           for i in range(2)]
    w2t = [nc.alloc_sbuf_tensor(f"{name}_w2t{i}", [128, HT * 128], BF16)
           for i in range(2)]
    b1q = nc.alloc_sbuf_tensor(f"{name}_b1q", [128, HT], F32)
    b2h = nc.alloc_sbuf_tensor(f"{name}_b2h", [128, KT], F32)
    ytmp = nc.alloc_sbuf_tensor(f"{name}_ytmp", [128, 512], F32)
    ycs = nc.alloc_sbuf_tensor(f"{name}_ycs", [128, 512], BF16)

    ps_a = psum["a"]
    ps_b = psum["b"]
    ps_y = psum["y"]
    ps_y2 = psum["y2"]
    ps_tr = psum["tr"]
    ps_cb = psum["cb"]
    ps_sm = psum["sm"]

    coef_refs = []
    for t in range(NT):
        ro = t * 8
        co = t * E
        r_o1 = p.op("vector", lambda v, ro=ro, co=co: v.tensor_scalar(
            coef[:, co:co + E], cst["iota8"][:], rtbuf[:, ro:ro + 1], None,
            ALU.is_equal), deps=list(gate_deps) + [r_offs])
        r_s1 = p.op("vector", lambda v, co=co: v.tensor_mul(
            scr[:, 0:E], coef[:, co:co + E], offs_bcast[:]), deps=[r_o1])
        r_g1 = p.op("vector", lambda v: v.reduce_sum(
            scr[:, 8:9], scr[:, 0:E], axis=AX.X), deps=[r_s1])
        r_k1 = p.op("vector", lambda v, ro=ro: v.tensor_add(
            scr[:, 9:10], scr[:, 8:9], rtbuf[:, ro + 4:ro + 5]), deps=[r_g1])
        r_m1 = p.op("vector", lambda v: v.tensor_scalar(
            scr[:, 10:11], scr[:, 9:10], float(CAP), None, ALU.is_lt),
            deps=[r_k1])
        r_w1 = p.op("vector", lambda v, ro=ro: v.tensor_mul(
            scr[:, 11:12], scr[:, 10:11], rtbuf[:, ro + 2:ro + 3]),
            deps=[r_m1])
        r_o2 = p.op("vector", lambda v, ro=ro: v.tensor_scalar(
            scr[:, 16:16 + E], cst["iota8"][:], rtbuf[:, ro + 1:ro + 2], None,
            ALU.is_equal), deps=[r_o1])
        r_s2 = p.op("vector", lambda v: v.tensor_mul(
            scr[:, 24:24 + E], scr[:, 16:16 + E], offs_bcast[:]), deps=[r_o2])
        r_g2 = p.op("vector", lambda v: v.reduce_sum(
            scr[:, 12:13], scr[:, 24:24 + E], axis=AX.X), deps=[r_s2])
        r_k2 = p.op("vector", lambda v, ro=ro: v.tensor_add(
            scr[:, 13:14], scr[:, 12:13], rtbuf[:, ro + 5:ro + 6]), deps=[r_g2])
        r_m2 = p.op("vector", lambda v: v.tensor_scalar(
            scr[:, 14:15], scr[:, 13:14], float(CAP), None, ALU.is_lt),
            deps=[r_k2])
        r_w2 = p.op("vector", lambda v, ro=ro: v.tensor_mul(
            scr[:, 15:16], scr[:, 14:15], rtbuf[:, ro + 3:ro + 4]),
            deps=[r_m2])
        r_c1 = p.op("vector", lambda v, co=co: v.tensor_scalar(
            coef[:, co:co + E], coef[:, co:co + E], scr[:, 11:12], None,
            ALU.mult), deps=[r_w1])
        r_c2 = p.op("vector", lambda v, co=co: v.tensor_scalar(
            scr[:, 16:16 + E], scr[:, 16:16 + E], scr[:, 15:16], None,
            ALU.mult), deps=[r_w2, r_c1])
        r_cf = p.op("vector", lambda v, co=co: v.tensor_add(
            coef[:, co:co + E], coef[:, co:co + E], scr[:, 16:16 + E]),
            deps=[r_c2])
        coef_refs.append(r_cf)

    r_az = p.op("vector", lambda v: v.memset(accT[:], 0.0))

    acc_last = [r_az]
    mm1_group = {}
    mm2_group = {}
    ps_reader = {}
    prev_hb = {}
    for e in range(E):
        brow = scr[0:1, 2048:2560]
        r_prev_chunk = acc_last[-1]
        for cc in range(HT // 4):
            bdeps = [r_prev_chunk]
            if cc == 0 and "sm" in ps_reader:
                bdeps.append(ps_reader["sm"])
            r_br = p.dma("sync", brow,
                         b1_in[e:e + 1, cc * 512:(cc + 1) * 512],
                         stream=f"{name}_br", deps=bdeps)
            for j in range(4):
                p.op("tensor", lambda te, cc=cc, j=j: te.matmul(
                    ps_sm[:, cc * 4 + j:cc * 4 + j + 1],
                    brow[0:1, j * 128:(j + 1) * 128], cst["ones11"][:],
                    start=True, stop=True), deps=[r_br], inc=(j == 3))
            r_prev_chunk = Ref(p.esem["tensor"], p.ecnt["tensor"], "tensor")
        r_b1q = p.op("vector", lambda v: v.tensor_scalar(
            b1q[:], ps_sm[:, 0:HT], 0.25, None, ALU.mult),
            deps=[r_prev_chunk])
        for cc in range(KT // 4):
            r_br = p.dma("sync", brow,
                         b2_in[e:e + 1, cc * 512:(cc + 1) * 512],
                         stream=f"{name}_br", deps=[r_prev_chunk, r_b1q])
            for j in range(4):
                p.op("tensor", lambda te, cc=cc, j=j: te.matmul(
                    ps_sm[:, HT + cc * 4 + j:HT + cc * 4 + j + 1],
                    brow[0:1, j * 128:(j + 1) * 128], cst["ones11"][:],
                    start=True, stop=True), deps=[r_br], inc=(j == 3))
            r_prev_chunk = Ref(p.esem["tensor"], p.ecnt["tensor"], "tensor")
        r_b2h = p.op("vector", lambda v: v.tensor_scalar(
            b2h[:], ps_sm[:, HT:HT + KT], 0.5, None, ALU.mult),
            deps=[r_prev_chunk])
        ps_reader["sm"] = r_b2h

        ct_refs = []
        prev_ctc = None
        for half in range(4):
            for ch in range(NCH):
                base = half * HALF + ch * 512
                for j in range(4):
                    t = (base + j * 128) // 128
                    jdeps = [coef_refs[t]] + acc_last[-1:]
                    if j == 0 and prev_ctc is not None:
                        jdeps.append(prev_ctc)
                    if j == 0 and "cb" in ps_reader:
                        jdeps.append(ps_reader["cb"])
                    p.op("tensor", lambda te, t=t, j=j, e=e: te.matmul(
                        ps_cb[0:1, j * 128:(j + 1) * 128],
                        coef[:, t * E + e: t * E + e + 1],
                        cst["identf"][:],
                        start=True, stop=True),
                        deps=jdeps, inc=(j == 3))
                r_ctm = Ref(p.esem["tensor"], p.ecnt["tensor"], "tensor")
                r_ctc = p.op("vector", lambda v, base=base: v.tensor_copy(
                    coefT[0:1, base:base + 512], ps_cb[0:1, 0:512]),
                    deps=[r_ctm])
                prev_ctc = r_ctc
                ct_refs.append(r_ctc)

        for half in range(4):
            tok0 = half * HALF
            gelu_refs = []
            for ht in range(HT):
                par = ht % 2
                wb = w1t[par]
                wdeps = [mm1_group[par]] if par in mm1_group else []
                r_w = p.dma("gpsimd", wb[:], w1_in[e, ht],
                            stream=f"{name}_w1{par}", deps=wdeps)
                for ch in range(NCH):
                    ps = ps_a if ht % 2 == 0 else ps_b
                    psname = "a" if ht % 2 == 0 else "b"
                    t0 = (tok0 + ch * 512) // 128
                    for k in range(KT):
                        kdeps = []
                        if k == 0:
                            kdeps = [r_w]
                            if psname in ps_reader:
                                kdeps.append(ps_reader[psname])
                        p.op("tensor", lambda te, wb=wb, k=k, t0=t0, ps=ps:
                             te.matmul(
                                 ps[:],
                                 wb[:, k * 128:(k + 1) * 128],
                                 xlnT.ap().rearrange(
                                     "p (t d) -> p t d", t=NT)[
                                     :, t0:t0 + 4, k * 128:(k + 1) * 128],
                                 start=(k == 0), stop=(k == KT - 1)),
                             deps=kdeps if k == 0 else (),
                             inc=(k == KT - 1))
                    r_mm = Ref(p.esem["tensor"], p.ecnt["tensor"], "tensor")
                    if ch == NCH - 1:
                        mm1_group[par] = r_mm
                    hb = hbuf[:, ht * HALF + ch * 512:
                              ht * HALF + (ch + 1) * 512]
                    sb = 1536 * (ht % 2)
                    s0 = scr[:, sb:sb + 512]
                    s1 = scr[:, sb + 512:sb + 1024]
                    s2 = scr[:, sb + 1024:sb + 1536]
                    hq_deps = [r_mm, r_b1q]
                    if (ht % 2) in prev_hb:
                        hq_deps.append(prev_hb[ht % 2])
                    r_hq = p.op("scalar", lambda s, ps=ps, s0=s0, ht=ht:
                                s.activation(s0, ps[:], AF.Identity,
                                             bias=b1q[:, ht:ht + 1],
                                             scale=0.25),
                                deps=hq_deps)
                    ps_reader[psname] = r_hq
                    r_sq2 = p.op("scalar", lambda s, s0=s0, s1=s1:
                                 s.activation(s1, s0, AF.Square, scale=4.0),
                                 deps=[r_hq])
                    r_v = p.op("vector", lambda v, s1=s1: v.tensor_scalar(
                        s1, s1, 4.0 * GC0 * GC1, 4.0 * GC0,
                        ALU.mult, ALU.add), deps=[r_sq2])
                    r_u = p.op("vector", lambda v, s0=s0, s1=s1:
                               v.tensor_mul(s1, s1, s0), deps=[r_v])
                    r_th = p.op("scalar", lambda s, s1=s1, s2=s2:
                                s.activation(s2, s1, AF.Tanh), deps=[r_u])
                    r_hb = p.op("vector", lambda v, s0=s0, s2=s2, hb=hb:
                                v.scalar_tensor_tensor(hb, s2, 1.0, s0,
                                                       ALU.add, ALU.mult),
                                deps=[r_th])
                    prev_hb[ht % 2] = r_hb
                    gelu_refs.append(r_hb)
            for ch in range(NCH):
                base = tok0 + ch * 512
                cbdeps = ct_refs + acc_last[-1:]
                if "cb" in ps_reader:
                    cbdeps.append(ps_reader["cb"])
                r_cb = p.op("tensor", lambda te, base=base: te.matmul(
                    ps_cb[:], cst["onesrowf"][:],
                    coefT[0:1, base:base + 512],
                    start=True, stop=True), deps=cbdeps)
                for dt in range(KT):
                    par = dt % 2
                    wb2 = w2t[par]
                    if ch == 0:
                        wdeps2 = [mm2_group[par]] if par in mm2_group else []
                        r_w2 = p.dma("scalar", wb2[:], w2_in[e, dt],
                                      stream=f"{name}_w2{par}", deps=wdeps2)
                        mm2_wref = r_w2
                    else:
                        mm2_wref = None
                    psy = ps_y if dt % 2 == 0 else ps_y2
                    psyname = "y" if dt % 2 == 0 else "y2"
                    for hk in range(HT):
                        dd = []
                        if hk == 0:
                            dd = gelu_refs[:]
                            if mm2_wref is not None:
                                dd.append(mm2_wref)
                            if psyname in ps_reader:
                                dd.append(ps_reader[psyname])
                        p.op("tensor", lambda te, wb2=wb2, hk=hk, ch=ch,
                             psy=psy, base=base, tok0=tok0: te.matmul(
                                 psy[:],
                                 wb2[:, hk * 128:(hk + 1) * 128],
                                 hbuf[:, hk * HALF + (base - tok0):
                                      hk * HALF + (base - tok0) + 512],
                                 start=(hk == 0), stop=(hk == HT - 1)),
                             deps=dd, inc=(hk == HT - 1))
                    r_mm2 = Ref(p.esem["tensor"], p.ecnt["tensor"], "tensor")
                    if ch == NCH - 1:
                        mm2_group[par] = r_mm2
                    ybdeps = [r_mm2, r_b2h]
                    if "yc" in ps_reader:
                        ybdeps.append(ps_reader["yc"])
                    r_yb = p.op("scalar", lambda s, psy=psy, dt=dt:
                                s.activation(ytmp[:], psy[:], AF.Identity,
                                             bias=b2h[:, dt:dt + 1]),
                                deps=ybdeps)
                    ps_reader[psyname] = r_yb
                    r_yc = p.op("vector", lambda v: v.tensor_mul(
                        ycs[:], ytmp[:], ps_cb[:]), deps=[r_yb, r_cb])
                    ps_reader["cb"] = r_yc
                    ps_reader["yc"] = r_yc
                    acc = accT[:, dt * TPC + base: dt * TPC + base + 512]
                    r_up = p.op("vector", lambda v, acc=acc: v.tensor_add(
                        acc, acc, ycs[:]), deps=[r_yc, r_az])
                    acc_last.append(r_up)

    done = []
    r_cp = None
    for tt in range(NT):
        for dt in range(KT):
            tdeps = []
            if dt == 0:
                tdeps = acc_last[-1:]
                if r_cp is not None:
                    tdeps.append(r_cp)
            p.op("tensor", lambda te, tt=tt, dt=dt: te.transpose(
                ps_tr[:, dt * 128:(dt + 1) * 128],
                accT[:, dt * TPC + tt * 128: dt * TPC + (tt + 1) * 128],
                cst["ident"][:]), deps=tdeps,
                inc=(dt == KT - 1))
        r_trg = Ref(p.esem["tensor"], p.ecnt["tensor"], "tensor")
        r_cp = p.op("vector", lambda v: v.tensor_copy(
            scr[:, 0:1024], ps_tr[:]), deps=[r_trg])
        r_ad = p.op("vector", lambda v, tt=tt: v.tensor_add(
            xres[:, tt * D:(tt + 1) * D], xres[:, tt * D:(tt + 1) * D],
            scr[:, 0:1024]), deps=[r_cp])
        done.append(r_ad)
    return done


def emit_ln_transpose_all(nc, p, cst, name, xres, xlnT, scr, deps, ps_t,
                          stage, stage_ap=None):
    if stage_ap is None:
        stage_ap = stage[:]
    last = None
    outs = []
    for t in range(NT):
        dd = list(deps) + ([last] if last is not None else [])
        r_ln = emit_ln_tile(p, xres[:, t * D:(t + 1) * D], stage_ap,
                            scr[:, 0:D], scr[:, D:D + 8], deps=dd)
        for k in range(KT):
            p.op("tensor", lambda te, k=k: te.transpose(
                ps_t[:, k * 128:(k + 1) * 128],
                stage_ap[:, k * 128:(k + 1) * 128], cst["ident"][:]),
                deps=([r_ln] + ([last] if last is not None else [])
                      if k == 0 else ()),
                inc=(k == KT - 1))
        r_tg = Ref(p.esem["tensor"], p.ecnt["tensor"], "tensor")
        r_c = p.op("vector", lambda v, t=t: v.tensor_copy(
            xlnT[:, t * D:(t + 1) * D], ps_t[:]), deps=[r_tg])
        last = r_c
        outs.append(r_c)
    return outs


def build_L2(with_qkv=True, name="L2"):
    nc = bass.Bass("TRN2", target_bir_lowering=False, debug=False,
                   detect_race_conditions=False)
    p = Prog(nc)
    x_in = nc.dram_tensor("x", [TPC, D], F32, kind="ExternalInput").ap()
    xln_in = nc.dram_tensor("xln", [TPC, D], BF16, kind="ExternalInput").ap()
    rt_in = nc.dram_tensor("rt", [TPC, 8], F32, kind="ExternalInput").ap()
    cnts_in = nc.dram_tensor("cnts", [8, E], F32, kind="ExternalInput").ap()
    cmask_in = nc.dram_tensor("cmask", [8, 1], F32, kind="ExternalInput").ap()
    w1_in = nc.dram_tensor("w1", [E, HT, 128, KT * 128], BF16,
                           kind="ExternalInput").ap()
    b1_in = nc.dram_tensor("b1", [E, HID], F32, kind="ExternalInput").ap()
    w2_in = nc.dram_tensor("w2", [E, KT, 128, HT * 128], BF16,
                           kind="ExternalInput").ap()
    b2_in = nc.dram_tensor("b2", [E, D], F32, kind="ExternalInput").ap()
    x1_out = nc.dram_tensor("x1", [TPC, D], F32, kind="ExternalOutput").ap()
    if with_qkv:
        wq_in = nc.dram_tensor("wq", [D, D], F32, kind="ExternalInput").ap()
        wk_in = nc.dram_tensor("wk", [D, D], F32, kind="ExternalInput").ap()
        wv_in = nc.dram_tensor("wv", [D, D], F32, kind="ExternalInput").ap()
        ropec_in = nc.dram_tensor("ropec", [128, TPC], F32,
                                  kind="ExternalInput").ap()
        ropes_in = nc.dram_tensor("ropes", [128, TPC], F32,
                                  kind="ExternalInput").ap()
        qT_out = nc.dram_tensor("qT", [D, TPC], BF16, kind="ExternalOutput").ap()
        kT_out = nc.dram_tensor("kT", [D, TPC], BF16, kind="ExternalOutput").ap()
        v_out = nc.dram_tensor("v", [TPC, D], BF16, kind="ExternalOutput").ap()

    cst, cst_refs = alloc_consts(nc, p)
    xt = nc.alloc_sbuf_tensor("xt", [128, NT * D], F32)
    xlnT = nc.alloc_sbuf_tensor("xlnT", [128, NT * D], BF16)
    rtb = nc.alloc_sbuf_tensor("rtb", [128, NT * 8], F32)
    scr = nc.alloc_sbuf_tensor("scr", [128, 3072], F32)
    cnts = nc.alloc_sbuf_tensor("s_cnts", [8, E], F32)
    cmask = nc.alloc_sbuf_tensor("s_cmask", [8, 1], F32)
    offs = nc.alloc_sbuf_tensor("offs", [128, E], F32)
    stage2 = None

    stage2_ap = scr.ap()[:, 2560:3072].bitcast(BF16)
    psum = {
        "a": nc.alloc_psum_tensor("ps_a", [128, 512], F32),
        "b": nc.alloc_psum_tensor("ps_b", [128, 512], F32),
        "y": nc.alloc_psum_tensor("ps_y", [128, 512], F32),
        "y2": nc.alloc_psum_tensor("ps_y2", [128, 512], F32),
        "tr": nc.alloc_psum_tensor("ps_tr", [128, 1024], BF16),
        "cb": nc.alloc_psum_tensor("ps_cb", [128, 512], F32),
        "sm": nc.alloc_psum_tensor("ps_sm", [128, 64], F32),
    }
    ps_o = psum["sm"]

    r_x = p.dma("sync", xt.ap().rearrange("p (t d) -> p t d", t=NT),
                x_in.rearrange("(t p) d -> p t d", p=128), stream="ld_x")
    r_rt = p.dma("sync", rtb.ap().rearrange("p (t e) -> p t e", t=NT),
                 rt_in.rearrange("(t p) e -> p t e", p=128), stream="ld_rt")
    r_cn = p.dma("sync", cnts[:], cnts_in, stream="ld_cn")
    r_cm = p.dma("sync", cmask[:], cmask_in, stream="ld_cm")

    r_of1 = p.op("tensor", lambda te: te.matmul(
        ps_o[0:1, 0:E], cmask[:], cnts[:], start=True, stop=True),
        deps=[r_cn, r_cm] + cst_refs)
    r_of2 = p.op("vector", lambda v: v.tensor_copy(offs[0:1, :], ps_o[0:1, 0:E]),
                 deps=[r_of1])
    r_of3 = p.op("tensor", lambda te: te.matmul(
        ps_o[:, 0:E], cst["onesrow"][:], offs[0:1, :], start=True, stop=True),
        deps=[r_of2])
    r_offs = p.op("vector", lambda v: v.tensor_copy(offs[:], ps_o[:, 0:E]),
                  deps=[r_of3])

    ps_ti = psum["tr"]
    last_tr = None
    tr_refs = []
    for t in range(NT):
        r_l = p.dma("sync", stage2_ap,
                    xln_in[t * 128:(t + 1) * 128, :], stream=f"xlnld{t % 2}",
                    deps=[last_tr] if last_tr is not None else cst_refs)
        for k in range(KT):
            p.op("tensor", lambda te, k=k: te.transpose(
                ps_ti[:, k * 128:(k + 1) * 128],
                stage2_ap[:, k * 128:(k + 1) * 128], cst["ident"][:]),
                deps=([r_l] + ([last_tr] if last_tr is not None else [])
                      if k == 0 else ()),
                inc=(k == KT - 1))
        r_tg = Ref(p.esem["tensor"], p.ecnt["tensor"], "tensor")
        r_c = p.op("vector", lambda v, t=t: v.tensor_copy(
            xlnT[:, t * D:(t + 1) * D], ps_ti[:]), deps=[r_tg])
        last_tr = r_c
        tr_refs.append(r_c)

    ffn_done = emit_dense_ffn(nc, p, cst, "f", xt, xlnT, rtb, offs, scr,
                              w1_in, b1_in, w2_in, b2_in,
                              [r_x, r_rt] + tr_refs, r_offs, psum)

    r_sx = p.dma("sync", x1_out.rearrange("(t p) d -> p t d", p=128),
                 xt.ap().rearrange("p (t d) -> p t d", t=NT), stream="stx",
                 deps=ffn_done)
    finals = [r_sx]

    if with_qkv:
        ln_refs = emit_ln_transpose_all(nc, p, cst, "q", xt, xlnT, scr,
                                        deps=ffn_done, ps_t=psum["tr"],
                                        stage=None, stage_ap=stage2_ap)
        finals += emit_qkv(nc, p, cst, xt, xlnT, scr, ln_refs,
                           wq_in, wk_in, wv_in, ropec_in, ropes_in,
                           qT_out, kT_out, v_out,
                           hbuf=_ffn_bufs["hbuf"], accT=_ffn_bufs["accT"],
                           psum=psum)

    p.op("sync", lambda sy: sy.nop(), deps=finals)
    p.emit()
    return nc


def emit_qkv(nc, p, cst, xt, xlnT, scr, ln_refs,
             wq_in, wk_in, wv_in, ropec_in, ropes_in, qT_out, kT_out, v_out,
             hbuf=None, accT=None, psum=None):
    hbF = hbuf.ap().bitcast(F32)
    ropec = hbF[:, 0:TPC]
    ropes = hbF[:, TPC:2 * TPC]
    qs = hbF[:, 2 * TPC:2 * TPC + 512]
    qsh = hbF[:, 2 * TPC + 512:2 * TPC + 1024]
    wqt = [hbuf.ap()[:, 2 * 2 * TPC + 2048 + i * KT * 128:
                     2 * 2 * TPC + 2048 + (i + 1) * KT * 128]
           for i in range(2)]
    acB = accT.ap()
    wvt = acB[:, 0:KT * 1024]
    qcb = [acB[:, KT * 1024 + i * 512:KT * 1024 + (i + 1) * 512]
           for i in range(2)]
    ps_q = psum["a"]
    ps_q2 = psum["b"]

    r_rc = p.dma("gpsimd", ropec, ropec_in, stream="ropec", deps=ln_refs[:1])
    r_rs = p.dma("gpsimd", ropes, ropes_in, stream="ropes", deps=ln_refs[:1])
    finals = []
    NCH4 = TPC // 512
    wstream = {}
    ps_rd = {}
    qc_user = {}
    nchunk = 0
    for which, w_in, outT in (("q", wq_in, qT_out), ("k", wk_in, kT_out)):
        for ot in range(KT):
            par = ot % 2
            wb = wqt[par]
            wdeps = ([wstream[par]] if par in wstream else []) + ln_refs[:1]
            r_w = p.dma("gpsimd", wb.rearrange("p (k c) -> p k c", k=KT),
                        w_in[:, ot * 128:(ot + 1) * 128].rearrange(
                            "(k p) c -> p k c", p=128),
                        stream=f"w{which}{par}", deps=wdeps)
            for ch in range(NCH4):
                ps = ps_q if ch % 2 == 0 else ps_q2
                pname = ch % 2
                for k in range(KT):
                    dd = []
                    if k == 0:
                        dd = [r_w] + ln_refs
                        if pname in ps_rd:
                            dd.append(ps_rd[pname])
                    p.op("tensor", lambda te, wb=wb, k=k, ch=ch, ps=ps:
                         te.matmul(
                             ps[:], wb[:, k * 128:(k + 1) * 128],
                             xlnT.ap().rearrange("p (t d) -> p t d", t=NT)[
                                 :, ch * 4:ch * 4 + 4, k * 128:(k + 1) * 128],
                             start=(k == 0), stop=(k == KT - 1)),
                         deps=dd, inc=(k == KT - 1))
                r_mm = Ref(p.esem["tensor"], p.ecnt["tensor"], "tensor")
                if ch == NCH4 - 1:
                    wstream[par] = r_mm
                r_ev = p.op("vector", lambda v, ps=ps: v.tensor_copy(
                    qs[:], ps[:]), deps=[r_mm])
                ps_rd[pname] = r_ev
                r_sh = p.op("vector", lambda v: v.stream_shuffle(
                    qsh[:], qs[:], [i ^ 1 for i in range(32)]), deps=[r_ev])
                r_m1 = p.op("vector", lambda v, ch=ch: v.tensor_mul(
                    qs[:], qs[:], ropec[:, ch * 512:(ch + 1) * 512]),
                    deps=[r_sh, r_rc])
                r_m2 = p.op("vector", lambda v, ch=ch: v.tensor_mul(
                    qsh[:], qsh[:], ropes[:, ch * 512:(ch + 1) * 512]),
                    deps=[r_m1, r_rs])
                qb = nchunk % 2
                addeps = [r_m2]
                if qb in qc_user:
                    addeps.append(qc_user[qb])
                r_ad = p.op("vector", lambda v, qb=qb: v.tensor_add(
                    qcb[qb], qs[:], qsh[:]), deps=addeps)
                r_st = p.dma("sync", outT[ot * 128:(ot + 1) * 128,
                                          ch * 512:(ch + 1) * 512],
                             qcb[qb], stream=f"st{which}{qb}", deps=[r_ad])
                qc_user[qb] = r_st
                nchunk += 1
                finals.append(r_st)
    r_wv = p.dma("gpsimd", wvt.rearrange("p (k c) -> p k c", k=KT),
                 wv_in.rearrange("(k p) c -> p k c", p=128), stream="wv",
                 deps=ln_refs[:1])
    for tt in range(NT):
        for dc in range(2):
            ps = ps_q if dc == 0 else ps_q2
            pname = dc
            for k in range(KT):
                dd = []
                if k == 0:
                    dd = [r_wv] + ln_refs
                    if pname in ps_rd:
                        dd.append(ps_rd[pname])
                p.op("tensor", lambda te, tt=tt, dc=dc, k=k, ps=ps:
                     te.matmul(
                         ps[:],
                         xlnT[:, tt * D + k * 128: tt * D + (k + 1) * 128],
                         wvt[:, k * 1024 + dc * 512: k * 1024 + (dc + 1) * 512],
                         start=(k == 0), stop=(k == KT - 1)),
                     deps=dd, inc=(k == KT - 1))
            r_mm = Ref(p.esem["tensor"], p.ecnt["tensor"], "tensor")
            qb = nchunk % 2
            evdeps = [r_mm]
            if qb in qc_user:
                evdeps.append(qc_user[qb])
            r_ev = p.op("vector", lambda v, ps=ps, qb=qb: v.tensor_copy(
                qcb[qb], ps[:]), deps=evdeps)
            ps_rd[pname] = r_ev
            r_st = p.dma("sync", v_out[tt * 128:(tt + 1) * 128,
                                       dc * 512:(dc + 1) * 512],
                         qcb[qb], stream=f"stv{qb}", deps=[r_ev])
            qc_user[qb] = r_st
            nchunk += 1
            finals.append(r_st)
    return finals


NHPC = 8


def build_L3():
    nc = bass.Bass("TRN2", target_bir_lowering=False, debug=False,
                   detect_race_conditions=False)
    p = Prog(nc)
    qh_in = nc.dram_tensor("qh", [NHPC, DK, S], BF16, kind="ExternalInput").ap()
    kh_in = nc.dram_tensor("kh", [NHPC, DK, S], BF16, kind="ExternalInput").ap()
    vh_in = nc.dram_tensor("vh", [NHPC, S, DK], BF16, kind="ExternalInput").ap()
    oh_out = nc.dram_tensor("oh", [NHPC, DK, S], BF16, kind="ExternalOutput").ap()

    cst, cst_refs = alloc_consts(nc, p)
    NQT = S // 128

    qT = [nc.alloc_sbuf_tensor(f"qT{i}", [64, S], BF16) for i in range(2)]
    kT = [nc.alloc_sbuf_tensor(f"kT{i}", [64, S], BF16) for i in range(2)]
    vt = [nc.alloc_sbuf_tensor(f"vt{i}", [128, NQT * DK], BF16)
          for i in range(2)]
    sqf = nc.alloc_sbuf_tensor("sqf", [64, S], F32)
    nrow = nc.alloc_sbuf_tensor("nrow", [1, 2 * S + 8], F32)
    mrow = nc.alloc_sbuf_tensor("mrow", [1, S], F32)
    mcol = nc.alloc_sbuf_tensor("mcol", [128, NQT], F32)
    P = nc.alloc_sbuf_tensor("P", [128, S], BF16)
    PT = nc.alloc_sbuf_tensor("PT", [128, S], BF16)
    rs = nc.alloc_sbuf_tensor("rs", [128, 16], F32)
    dnm = nc.alloc_sbuf_tensor("dnm", [128, 2], F32)
    obuf = nc.alloc_sbuf_tensor("obuf", [64, S], BF16)

    ps_s = [nc.alloc_psum_tensor(f"ps_s{i}", [128, 512], F32) for i in range(2)]
    ps_tr = nc.alloc_psum_tensor("ps_tr", [128, 1024], BF16)
    ps_o = nc.alloc_psum_tensor("ps_o", [64, 128], F32)
    ps_m = nc.alloc_psum_tensor("ps_m", [128, NQT], F32)
    ps_n = nc.alloc_psum_tensor("ps_n", [1, 512], F32)

    ld_refs = {}
    rd_last = {}
    ps_rd = {}
    r_store = None

    def load_head(j):
        par = j % 2
        deps = []
        for key in (f"q{par}", f"k{par}", f"v{par}"):
            if key in rd_last:
                deps.append(rd_last[key])
        r1 = p.dma("sync", qT[par][:, :], qh_in[j], stream=f"ldq{par}",
                   deps=deps)
        r2 = p.dma("sync", kT[par][:, :], kh_in[j], stream=f"ldk{par}",
                   deps=deps)
        r3 = p.dma("sync", vt[par].ap().rearrange("p (t d) -> p t d", t=NQT),
                   vh_in[j].rearrange("(t p) d -> p t d", p=128),
                   stream=f"ldv{par}", deps=deps)
        return [r1, r2, r3]

    ld_refs[0] = load_head(0)
    for j in range(NHPC):
        par = j % 2
        r_q, r_k, r_v = ld_refs[j]
        if j + 1 < NHPC:
            ld_refs[j + 1] = load_head(j + 1)
        r_sq = p.op("scalar", lambda s, par=par: s.activation(
            sqf[:], qT[par][:, :], AF.Square), deps=[r_q] + cst_refs)
        nq_parts = []
        for c in range(8):
            r_nm = p.op("tensor", lambda te, c=c, par=par: te.matmul(
                ps_n[:], cst["ones64"][:], sqf[:, c * 512:(c + 1) * 512],
                start=True, stop=True),
                deps=[r_sq] + ([ps_rd["n"]] if "n" in ps_rd else []))
            r_nc = p.op("vector", lambda v, c=c: v.tensor_copy(
                nrow[0:1, c * 512:(c + 1) * 512], ps_n[:]), deps=[r_nm])
            ps_rd["n"] = r_nc
            nq_parts.append(r_nc)
        r_sqk = p.op("scalar", lambda s, par=par: s.activation(
            sqf[:], kT[par][:, :], AF.Square), deps=[r_k] + nq_parts)
        nk_parts = []
        for c in range(8):
            r_nm = p.op("tensor", lambda te, c=c: te.matmul(
                ps_n[:], cst["ones64"][:], sqf[:, c * 512:(c + 1) * 512],
                start=True, stop=True), deps=[r_sqk, ps_rd["n"]])
            r_nc = p.op("vector", lambda v, c=c: v.tensor_copy(
                nrow[0:1, S + c * 512:S + (c + 1) * 512], ps_n[:]),
                deps=[r_nm])
            ps_rd["n"] = r_nc
            nk_parts.append(r_nc)
        r_mx = p.op("vector", lambda v: v.reduce_max(
            nrow[0:1, 2 * S:2 * S + 1], nrow[0:1, S:2 * S], axis=AX.X),
            deps=nk_parts)
        r_sq1 = p.op("scalar", lambda s: s.activation(
            mrow[0:1, :], nrow[0:1, 0:S], AF.Sqrt), deps=nq_parts + [r_mx])
        r_sq2 = p.op("scalar", lambda s: s.activation(
            nrow[0:1, 2 * S + 1:2 * S + 2], nrow[0:1, 2 * S:2 * S + 1],
            AF.Sqrt), deps=[r_mx])
        r_mr = p.op("vector", lambda v: v.tensor_scalar(
            mrow[0:1, :], mrow[0:1, :], nrow[0:1, 2 * S + 1:2 * S + 2],
            -0.125, ALU.mult, ALU.mult), deps=[r_sq1, r_sq2])
        for qt in range(NQT):
            p.op("tensor", lambda te, qt=qt: te.matmul(
                ps_m[:, qt:qt + 1], mrow[0:1, qt * 128:(qt + 1) * 128],
                cst["ones11"][:], start=True, stop=True),
                deps=([r_mr] + ([ps_rd["m"]] if "m" in ps_rd else [])
                      if qt == 0 else ()),
                inc=(qt == NQT - 1))
        r_mt = Ref(p.esem["tensor"], p.ecnt["tensor"], "tensor")
        r_mc = p.op("vector", lambda v: v.tensor_copy(mcol[:], ps_m[:]),
                    deps=[r_mt])
        ps_rd["m"] = r_mc

        prev_ptr = None
        prev_exp = None
        sc_group = None
        av_group = None
        for qt in range(NQT):
            L = 128 * (qt + 1)
            nfull = L // 512
            tail = L - nfull * 512
            if tail == 0:
                nfull -= 1
                tail = 512
            exp_refs = []
            for c in range(nfull + 1):
                w = 512 if c < nfull else tail
                pp = c % 2
                sdeps = []
                if f"s{pp}" in ps_rd:
                    sdeps.append(ps_rd[f"s{pp}"])
                if c == 0:
                    sdeps += [r_k, r_mc]
                p.op("tensor", lambda te, c=c, w=w, qt=qt, par=par, pp=pp:
                     te.matmul(ps_s[pp][:, 0:w],
                               qT[par][:, qt * 128:(qt + 1) * 128],
                               kT[par][:, c * 512:c * 512 + w],
                               start=True, stop=True),
                     deps=sdeps)
                r_sc = Ref(p.esem["tensor"], p.ecnt["tensor"], "tensor")
                sc_group = r_sc
                edeps = [r_sc]
                if prev_exp is not None:
                    edeps.append(prev_exp)
                if c == 0 and prev_ptr is not None:
                    edeps.append(prev_ptr)
                if c < nfull:
                    r_e = p.op("scalar", lambda s, c=c, w=w, qt=qt, pp=pp:
                               s.activation(
                                   P[:, c * 512:c * 512 + w], ps_s[pp][:, 0:w],
                                   AF.Exp, bias=mcol[:, qt:qt + 1],
                                   scale=0.125, accum_out=rs[:, c:c + 1]),
                               deps=edeps)
                else:
                    r_e = p.op("scalar", lambda s, c=c, w=w, qt=qt, pp=pp:
                               s.activation(
                                   P[:, c * 512:c * 512 + w], ps_s[pp][:, 0:w],
                                   AF.Exp, bias=mcol[:, qt:qt + 1],
                                   scale=0.125),
                               deps=edeps)
                ps_rd[f"s{pp}"] = r_e
                prev_exp = r_e
                exp_refs.append(r_e)
            r_mk = p.op("vector", lambda v, L=L: v.tensor_mul(
                P[:, L - 128:L], P[:, L - 128:L], cst["tril"][:]),
                deps=[exp_refs[-1]])
            r_ts = p.op("vector", lambda v, qt=qt, nfull=nfull, tail=tail:
                        v.reduce_sum(rs[:, nfull:nfull + 1],
                                     P[:, nfull * 512:nfull * 512 + tail],
                                     axis=AX.X), deps=[r_mk])
            r_dn = p.op("vector", lambda v, nfull=nfull: v.reduce_sum(
                dnm[:, 0:1], rs[:, 0:nfull + 1], axis=AX.X),
                deps=[r_ts] + exp_refs)
            r_rc = p.op("vector", lambda v: v.reciprocal(
                dnm[:, 1:2], dnm[:, 0:1]), deps=[r_dn])
            r_nm = p.op("vector", lambda v, L=L: v.tensor_scalar(
                P[:, 0:L], P[:, 0:L], dnm[:, 1:2], None, ALU.mult),
                deps=[r_rc])
            pt_evacs = []
            nkt = qt + 1
            for blk in range((nkt + 7) // 8):
                k0 = blk * 8
                k1 = min(k0 + 8, nkt)
                for ki in range(k0, k1):
                    p.op("tensor", lambda te, ki=ki, k0=k0: te.transpose(
                        ps_tr[:, (ki - k0) * 128:(ki - k0 + 1) * 128],
                        P[:, ki * 128:(ki + 1) * 128], cst["ident"][:]),
                        deps=([r_nm] + ([ps_rd["tr"]] if "tr" in ps_rd else [])
                              if ki == k0 else ()),
                        inc=(ki == k1 - 1))
                r_tg = Ref(p.esem["tensor"], p.ecnt["tensor"], "tensor")
                r_pc = p.op("vector", lambda v, k0=k0, k1=k1: v.tensor_copy(
                    PT[:, k0 * 128:k1 * 128],
                    ps_tr[:, 0:(k1 - k0) * 128]), deps=[r_tg])
                ps_rd["tr"] = r_pc
                pt_evacs.append(r_pc)
            prev_ptr = pt_evacs[-1]
            for ki in range(nkt):
                avdeps = []
                if ki == 0:
                    avdeps = pt_evacs + [r_v]
                    if "o" in ps_rd:
                        avdeps.append(ps_rd["o"])
                p.op("tensor", lambda te, ki=ki, par=par, nkt=nkt: te.matmul(
                    ps_o[:], vt[par][:, ki * DK:(ki + 1) * DK],
                    PT[:, ki * 128:(ki + 1) * 128],
                    start=(ki == 0), stop=(ki == nkt - 1)),
                    deps=avdeps, inc=(ki == nkt - 1))
            r_av = Ref(p.esem["tensor"], p.ecnt["tensor"], "tensor")
            av_group = r_av
            r_oe = p.op("vector", lambda v, qt=qt: v.tensor_copy(
                obuf[:, qt * 128:(qt + 1) * 128], ps_o[:]),
                deps=[r_av] + ([r_store] if r_store is not None and qt == 0
                               else []))
            ps_rd["o"] = r_oe
        rd_last[f"q{par}"] = sc_group
        rd_last[f"k{par}"] = sc_group
        rd_last[f"v{par}"] = av_group
        r_store = p.dma("sync", oh_out[j], obuf[:, :], stream=f"sto{j % 2}",
                        deps=[ps_rd["o"]])
    p.op("sync", lambda sy: sy.nop(), deps=[r_store])
    p.emit()
    return nc


def build_L4():
    nc = bass.Bass("TRN2", target_bir_lowering=False, debug=False,
                   detect_race_conditions=False)
    p = Prog(nc)
    x1_in = nc.dram_tensor("x1", [TPC, D], F32, kind="ExternalInput").ap()
    aoT_in = nc.dram_tensor("aoT", [D, TPC], BF16, kind="ExternalInput").ap()
    wo_in = nc.dram_tensor("wo", [D, D], F32, kind="ExternalInput").ap()
    wg_in = nc.dram_tensor("wg", [D, E], F32, kind="ExternalInput").ap()
    _WG_IN[0] = wg_in
    x2_out = nc.dram_tensor("x2", [TPC, D], F32, kind="ExternalOutput").ap()
    xln_out = nc.dram_tensor("xln", [TPC, D], BF16, kind="ExternalOutput").ap()
    cnt_out = nc.dram_tensor("cnt", [1, E], F32, kind="ExternalOutput").ap()
    rt_out = nc.dram_tensor("rt", [TPC, 8], F32, kind="ExternalOutput").ap()

    cst, cst_refs = alloc_consts(nc, p)
    xt = nc.alloc_sbuf_tensor("xt", [128, NT * D], F32)
    aoT = nc.alloc_sbuf_tensor("s_aoT", [128, KT * TPC], BF16)
    wot = nc.alloc_sbuf_tensor("wot", [128, KT * D], BF16)
    xsc = nc.alloc_sbuf_tensor("xsc", [128, 512], F32)
    ps_pr = [nc.alloc_psum_tensor(f"ps_pr{i}", [128, 512], F32)
             for i in range(2)]

    r_x = p.dma("sync", xt.ap().rearrange("p (t d) -> p t d", t=NT),
                x1_in.rearrange("(t p) d -> p t d", p=128), stream="xload")
    r_ao = p.dma("sync", aoT.ap().rearrange("p (k t) -> p k t", k=KT),
                 aoT_in.rearrange("(k p) t -> p k t", p=128), stream="aoload")
    r_wo = p.dma("gpsimd", wot.ap().rearrange("p (k c) -> p k c", k=KT),
                 wo_in.rearrange("(k p) c -> p k c", p=128), stream="woload")

    ps_rd = {}
    proj_done = []
    for tt in range(NT):
        for dc in range(2):
            pp = dc
            for k in range(KT):
                dd = []
                if k == 0:
                    dd = [r_ao, r_wo] + cst_refs
                    if f"pr{pp}" in ps_rd:
                        dd.append(ps_rd[f"pr{pp}"])
                p.op("tensor", lambda te, tt=tt, dc=dc, k=k, pp=pp: te.matmul(
                    ps_pr[pp][:],
                    aoT[:, k * TPC + tt * 128: k * TPC + (tt + 1) * 128],
                    wot[:, k * D + dc * 512: k * D + (dc + 1) * 512],
                    start=(k == 0), stop=(k == KT - 1)),
                    deps=dd, inc=(k == KT - 1))
            r_mm = Ref(p.esem["tensor"], p.ecnt["tensor"], "tensor")
            r_ad = p.op("vector", lambda v, tt=tt, dc=dc, pp=pp: v.tensor_add(
                xt[:, tt * D + dc * 512: tt * D + (dc + 1) * 512],
                xt[:, tt * D + dc * 512: tt * D + (dc + 1) * 512],
                ps_pr[pp][:]), deps=[r_mm, r_x])
            ps_rd[f"pr{pp}"] = r_ad
            proj_done.append(r_ad)

    xln, rt, carry, rt_done, r_cz = emit_ln_routing(
        nc, p, cst, "r", xt, proj_done + cst_refs)

    s0 = p.dma("sync", x2_out.rearrange("(t p) d -> p t d", p=128),
               xt.ap().rearrange("p (t d) -> p t d", t=NT), stream="st",
               deps=proj_done + rt_done)
    s1 = p.dma("sync", xln_out.rearrange("(t p) d -> p t d", p=128),
               xln.ap().rearrange("p (t d) -> p t d", t=NT), stream="st",
               deps=rt_done)
    s2 = p.dma("sync", rt_out.rearrange("(t p) e -> p t e", p=128),
               rt.ap().rearrange("p (t e) -> p t e", t=NT), stream="st",
               deps=rt_done)
    s3 = p.dma("sync", cnt_out, carry[:], stream="st", deps=[r_cz])
    p.op("sync", lambda sy: sy.nop(), deps=[s0, s1, s2, s3])
    p.emit()
    return nc


def build_L5():
    return build_L2(with_qkv=False, name="L5")


def _rope_tables(core):
    pos = (core % 2) * TPC + np.arange(TPC)
    inv = (1.0 / 10000.0 ** (np.arange(0, DK, 2, dtype=np.float32) / DK)
           ).astype(np.float32)
    ang = np.outer(pos.astype(np.float32), inv).astype(np.float32)
    didx = np.arange(128) % 64
    ct = np.cos(ang)[:, didx // 2].T.astype(np.float32)
    st = np.sin(ang)[:, didx // 2].T.astype(np.float32)
    st = st * np.where(didx % 2 == 1, 1.0, -1.0)[:, None].astype(np.float32)
    return np.ascontiguousarray(ct), np.ascontiguousarray(st)


def kernel(L0_attn_x, L0_attn_Wq, L0_attn_Wk, L0_attn_Wv, L0_attn_Wo,
           L0_ffn_Wg, L0_ffn_W1, L0_ffn_b1, L0_ffn_W2, L0_ffn_b2):
    xf = np.ascontiguousarray(np.asarray(L0_attn_x, np.float32)
                              ).reshape(NTOK, D)
    shards = [np.ascontiguousarray(xf[c * TPC:(c + 1) * TPC])
              for c in range(NCORES)]
    wg = np.ascontiguousarray(np.asarray(L0_ffn_Wg, np.float32))
    w1 = np.ascontiguousarray(np.asarray(L0_ffn_W1, np.float32))
    b1 = np.ascontiguousarray(np.asarray(L0_ffn_b1, np.float32))
    w2 = np.ascontiguousarray(np.asarray(L0_ffn_W2, np.float32))
    b2 = np.ascontiguousarray(np.asarray(L0_ffn_b2, np.float32))
    wq = np.ascontiguousarray(np.asarray(L0_attn_Wq, np.float32))
    wk = np.ascontiguousarray(np.asarray(L0_attn_Wk, np.float32))
    wv = np.ascontiguousarray(np.asarray(L0_attn_Wv, np.float32))
    wo = np.ascontiguousarray(np.asarray(L0_attn_Wo, np.float32))
    cores = list(range(NCORES))

    w1f = w1.reshape(-1, 8192)
    w2f = w2.reshape(-1, 8192)
    nsl = w1f.shape[0] // NCORES
    nc1 = build_L1()
    r1 = run_bass_kernel_spmd(
        nc1, [{"x": shards[c], "wg": wg,
               "w1s": w1f[c * nsl:(c + 1) * nsl],
               "w2s": w2f[c * nsl:(c + 1) * nsl]} for c in cores],
        core_ids=cores).results
    cnts = np.ascontiguousarray(
        np.concatenate([r1[c]["cnt"] for c in cores], axis=0))
    w1b = np.concatenate([r1[c]["w1b"] for c in cores], axis=0)
    w2b = np.concatenate([r1[c]["w2b"] for c in cores], axis=0)
    w1t = np.ascontiguousarray(
        w1b.reshape(E, KT, 128, HT, 128).transpose(0, 3, 2, 1, 4)
        .reshape(E, HT, 128, KT * 128))
    w2t = np.ascontiguousarray(
        w2b.reshape(E, HT, 128, KT, 128).transpose(0, 3, 2, 1, 4)
        .reshape(E, KT, 128, HT * 128))

    nc2 = build_L2()
    ims = []
    for c in cores:
        ct, st = _rope_tables(c)
        cm = np.zeros((8, 1), np.float32)
        cm[:c] = 1.0
        ims.append({"x": shards[c], "xln": r1[c]["xln"], "rt": r1[c]["rt"],
                    "cnts": cnts, "cmask": cm, "w1": w1t, "b1": b1, "w2": w2t,
                    "b2": b2, "wq": wq, "wk": wk, "wv": wv,
                    "ropec": ct, "ropes": st})
    r2 = run_bass_kernel_spmd(nc2, ims, core_ids=cores).results

    nc3 = build_L3()
    ims3 = []
    for c in cores:
        b = c // 2
        qTb = np.concatenate([r2[2 * b]["qT"], r2[2 * b + 1]["qT"]], axis=1)
        kTb = np.concatenate([r2[2 * b]["kT"], r2[2 * b + 1]["kT"]], axis=1)
        vb = np.concatenate([r2[2 * b]["v"], r2[2 * b + 1]["v"]], axis=0)
        hs = [(c % 2) * NHPC + i for i in range(NHPC)]
        qh = np.ascontiguousarray(
            np.stack([qTb[h * DK:(h + 1) * DK] for h in hs]))
        kh = np.ascontiguousarray(
            np.stack([kTb[h * DK:(h + 1) * DK] for h in hs]))
        vh = np.ascontiguousarray(
            np.stack([vb[:, h * DK:(h + 1) * DK] for h in hs]))
        ims3.append({"qh": qh, "kh": kh, "vh": vh})
    r3 = run_bass_kernel_spmd(nc3, ims3, core_ids=cores).results

    nc4 = build_L4()
    ims4 = []
    for c in cores:
        b = c // 2
        aoTb = np.concatenate([r3[2 * b]["oh"].reshape(NHEAD // 2 * DK, S),
                               r3[2 * b + 1]["oh"].reshape(NHEAD // 2 * DK, S)],
                              axis=0)
        ao = np.ascontiguousarray(aoTb[:, (c % 2) * TPC:(c % 2 + 1) * TPC])
        ims4.append({"x1": r2[c]["x1"], "aoT": ao, "wo": wo, "wg": wg})
    r4 = run_bass_kernel_spmd(nc4, ims4, core_ids=cores).results
    cnts2 = np.ascontiguousarray(
        np.concatenate([r4[c]["cnt"] for c in cores], axis=0))

    nc5 = build_L5()
    ims5 = []
    for c in cores:
        cm = np.zeros((8, 1), np.float32)
        cm[:c] = 1.0
        ims5.append({"x": r4[c]["x2"], "xln": r4[c]["xln"], "rt": r4[c]["rt"],
                     "cnts": cnts2, "cmask": cm, "w1": w1t, "b1": b1,
                     "w2": w2t, "b2": b2})
    r5 = run_bass_kernel_spmd(nc5, ims5, core_ids=cores).results

    out = np.concatenate([r5[c]["x1"] for c in cores], axis=0)
    return np.ascontiguousarray(out.reshape(B, S, D).astype(np.float32))


if __name__ == "__main__":
    for f in (build_L1, build_L2, build_L3, build_L4, build_L5):
        nc = f()
        print(f.__name__, nc.n_instructions())


# version 04
# speedup vs baseline: 1.0911x; 1.0074x over previous
import numpy as np
import ml_dtypes
import concourse.bass as bass
import concourse.mybir as mybir
from concourse.bass_utils import run_bass_kernel_spmd

AF = mybir.ActivationFunctionType
ALU = mybir.AluOpType
AX = mybir.AxisListType
F32 = mybir.dt.float32
BF16 = mybir.dt.bfloat16
U32 = mybir.dt.uint32
I32 = mybir.dt.int32

B, S, D, HID, E = 4, 4096, 1024, 4096, 8
NTOK = B * S
NCORES = 8
TPC = NTOK // NCORES
NT = TPC // 128
KT = D // 128
HT = HID // 128
CAP = 4301
NHEAD, DK = 16, 64
EPS = 1e-5
GC0 = 0.7978845608028654
GC1 = 0.044715


def bf16np(x):
    return np.ascontiguousarray(x).astype(ml_dtypes.bfloat16)


_ffn_bufs = {}


ENGINES = ("tensor", "vector", "scalar", "gpsimd", "sync")


class Ref:
    __slots__ = ("sem", "cnt", "engine")

    def __init__(self, sem, cnt, engine):
        self.sem, self.cnt, self.engine = sem, cnt, engine


class Prog:
    def __init__(self, nc):
        self.nc = nc
        self.items = {e: [] for e in ENGINES}
        self.esem = {e: nc.alloc_semaphore(f"E_{e}") for e in ENGINES}
        self.ecnt = {e: 0 for e in ENGINES}
        self.elast = {}
        self.dsem = {}
        self.dcnt = {}

    def _dma_sem(self, stream):
        if stream not in self.dsem:
            self.dsem[stream] = self.nc.alloc_semaphore(f"D_{stream}")
            self.dcnt[stream] = 0
        return self.dsem[stream]

    def op(self, engine, fn, deps=(), inc=True):
        sem = self.esem[engine]
        deps = tuple(deps)
        if engine in ("vector", "scalar", "gpsimd"):
            prev = self.elast.get(engine)
            if prev is not None:
                deps = deps + (prev,)
        if inc:
            self.ecnt[engine] += 1
        cnt = self.ecnt[engine]
        self.items[engine].append((fn, deps, sem if inc else None, 1))
        ref = Ref(sem, cnt, engine) if inc else None
        if inc:
            self.elast[engine] = ref
        return ref

    def dma(self, engine, out, in_, stream, deps=(), **kw):
        sem = self._dma_sem(stream)
        self.dcnt[stream] += 16
        cnt = self.dcnt[stream]

        def fn(eng, out=out, in_=in_, kw=kw):
            return eng.dma_start(out=out, in_=in_, **kw)
        self.items[engine].append((fn, tuple(deps), sem, 16))
        return Ref(sem, cnt, engine)

    def idma(self, engine, fn, stream, deps=()):
        sem = self._dma_sem(stream)
        self.dcnt[stream] += 16
        cnt = self.dcnt[stream]
        self.items[engine].append((fn, tuple(deps), sem, 16))
        return Ref(sem, cnt, engine)

    def raw(self, engine, fn):
        self.items[engine].append((fn, (), None, 0))
        return None

    def barrier_all(self):
        snap = [Ref(self.esem[e], self.ecnt[e], e) for e in ENGINES]
        snapd = [Ref(self.dsem[s], self.dcnt[s], None) for s in self.dsem]
        for e in ENGINES:
            self.items[e].append((None, tuple(snap + snapd), None, 0))

    def emit(self):
        nc = self.nc
        with nc.Block() as block:
            for engine in ENGINES:
                items = self.items[engine]

                def make(engine=engine, items=items):
                    def body(eng):
                        waited = {}
                        for fn, deps, sem, inc in items:
                            want = {}
                            for d in deps:
                                if d is None:
                                    continue
                                if want.get(d.sem, -1) < d.cnt:
                                    want[d.sem] = d.cnt
                            for wsem, wcnt in want.items():
                                if waited.get(wsem, -1) >= wcnt:
                                    continue
                                eng.wait_ge(wsem, wcnt)
                                waited[wsem] = wcnt
                            if fn is None:
                                continue
                            inst = fn(eng)
                            if sem is not None:
                                assert inst is not None, "op fn returned None"
                                inst.then_inc(sem, inc)
                    return body
                getattr(block, engine)(make())


def alloc_consts(nc, p):
    c = {}
    tri = nc.inline_tensor(np.triu(np.ones((128, 128), np.float32), 1), "c_tri").ap()
    onescol = nc.inline_tensor(np.ones((128, 1), np.float32), "c_onescol").ap()
    onesrow = nc.inline_tensor(np.ones((1, 128), np.float32), "c_onesrow").ap()
    ident = nc.inline_tensor(np.eye(128, dtype=ml_dtypes.bfloat16), "c_ident").ap()
    iota8 = nc.inline_tensor(
        np.tile(np.arange(8, dtype=np.float32), (128, 1)), "c_iota8").ap()
    ones11 = nc.inline_tensor(np.ones((1, 1), np.float32), "c_ones11").ap()
    identf = nc.inline_tensor(np.eye(128, dtype=np.float32), "c_identf").ap()
    tril = nc.inline_tensor(
        np.tril(np.ones((128, 128), np.float32)).astype(ml_dtypes.bfloat16),
        "c_tril").ap()
    ones64 = nc.inline_tensor(np.ones((64, 1), np.float32), "c_ones64").ap()
    c["tri"] = nc.alloc_sbuf_tensor("c_tri_s", [128, 128], F32)
    c["onescol"] = nc.alloc_sbuf_tensor("c_onescol_s", [128, 1], F32)
    c["onesrow"] = nc.alloc_sbuf_tensor("c_onesrow_s", [1, 128], F32)
    c["ident"] = nc.alloc_sbuf_tensor("c_ident_s", [128, 128], BF16)
    c["iota8"] = nc.alloc_sbuf_tensor("c_iota8_s", [128, 8], F32)
    c["ones11"] = nc.alloc_sbuf_tensor("c_ones11_s", [1, 1], F32)
    c["identf"] = nc.alloc_sbuf_tensor("c_identf_s", [128, 128], F32)
    c["tril"] = nc.alloc_sbuf_tensor("c_tril_s", [128, 128], BF16)
    c["ones64"] = nc.alloc_sbuf_tensor("c_ones64_s", [64, 1], F32)
    c["onesrowf"] = c["onesrow"]
    refs = []
    for name, src in [("tri", tri), ("onescol", onescol), ("onesrow", onesrow),
                      ("ident", ident), ("iota8", iota8), ("ones11", ones11),
                      ("identf", identf), ("tril", tril), ("ones64", ones64)]:
        refs.append(p.dma("sync", c[name][:], src, stream="consts"))
    return c, refs


def emit_ln_tile(p, xs, xln_out, sq, stcols, deps=()):
    r_sum = p.op("vector", lambda v: v.reduce_sum(stcols[:, 0:1], xs, axis=AX.X),
                 deps=deps)
    r_sq = p.op("scalar", lambda s: s.activation(sq[:], xs, AF.Square,
                                                accum_out=stcols[:, 1:2]),
                deps=deps)
    r_m = p.op("vector", lambda v: v.tensor_scalar(
        stcols[:, 2:3], stcols[:, 0:1], 1.0 / D, None, ALU.mult), deps=[r_sum])
    r_v1 = p.op("vector", lambda v: v.scalar_tensor_tensor(
        stcols[:, 3:4], stcols[:, 2:3], -1.0, stcols[:, 2:3], ALU.mult,
        ALU.mult), deps=[r_m])
    r_v2 = p.op("vector", lambda v: v.scalar_tensor_tensor(
        stcols[:, 3:4], stcols[:, 1:2], 1.0 / D, stcols[:, 3:4], ALU.mult,
        ALU.add), deps=[r_sq, r_v1])
    r_v3 = p.op("vector", lambda v: v.tensor_scalar(
        stcols[:, 3:4], stcols[:, 3:4], EPS, None, ALU.add), deps=[r_v2])
    r_sq2 = p.op("scalar", lambda s: s.activation(
        stcols[:, 4:5], stcols[:, 3:4], AF.Sqrt), deps=[r_v3])
    r_rc = p.op("vector", lambda v: v.reciprocal(
        stcols[:, 5:6], stcols[:, 4:5]), deps=[r_sq2])
    r_ap = p.op("vector", lambda v: v.tensor_scalar(
        xln_out, xs, stcols[:, 2:3], stcols[:, 5:6],
        ALU.subtract, ALU.mult), deps=[r_rc])
    return r_ap


def emit_transpose_1024(p, src_bf16, dst_bf16, ps_t, ident, deps):
    for k in range(KT):
        last = (k == KT - 1)
        p.op("tensor", lambda t, k=k: t.transpose(
            ps_t[:, k * 128:(k + 1) * 128],
            src_bf16[:, k * 128:(k + 1) * 128], ident[:]),
            deps=(deps if k == 0 else ()), inc=last)
    r_t = Ref(p.esem["tensor"], p.ecnt["tensor"], "tensor")
    r_c = p.op("vector", lambda v: v.tensor_copy(dst_bf16, ps_t[:]),
               deps=[r_t])
    return r_c


def emit_ln_routing(nc, p, cst, pref, xt, gate_deps):
    sq = nc.alloc_sbuf_tensor(f"{pref}_sq", [128, D], F32)
    stats = nc.alloc_sbuf_tensor(f"{pref}_stats", [128, 8], F32)
    xln = nc.alloc_sbuf_tensor(f"{pref}_xln", [128, NT * D], BF16)
    xlnT = nc.alloc_sbuf_tensor(f"{pref}_xlnT", [128, D], BF16)
    logits = nc.alloc_sbuf_tensor(f"{pref}_logits", [128, E], F32)
    top8v = nc.alloc_sbuf_tensor(f"{pref}_top8v", [128, 8], F32)
    top8i = nc.alloc_sbuf_tensor(f"{pref}_top8i", [128, 8], U32)
    ohbuf = nc.alloc_sbuf_tensor(f"{pref}_ohbuf", [128, 3 * E], F32)
    carry = nc.alloc_sbuf_tensor(f"{pref}_carry", [1, E], F32)
    rt = nc.alloc_sbuf_tensor(f"{pref}_rt", [128, NT * 8], F32)
    wgt = nc.alloc_sbuf_tensor(f"{pref}_wgt", [128, KT * E], BF16)

    ps_t = nc.alloc_psum_tensor(f"{pref}_ps_t", [128, D], BF16)
    ps_l = nc.alloc_psum_tensor(f"{pref}_ps_l", [128, E], F32)
    ps_p = nc.alloc_psum_tensor(f"{pref}_ps_p", [128, E], F32)
    ps_c = nc.alloc_psum_tensor(f"{pref}_ps_c", [1, E], F32)

    r_wg = p.dma("gpsimd", wgt.ap().rearrange("p (k e) -> p k e", k=KT),
                 _WG_IN[0].rearrange("(k p) e -> p k e", p=128),
                 stream=f"{pref}_wg")
    r_cz = p.op("vector", lambda v: v.memset(carry[:], 0.0))
    r_rtz = p.op("vector", lambda v: v.memset(rt[:], 0.0))

    gate = list(gate_deps) + [r_wg, r_rtz]
    rt_done = []
    prev_lgc = None
    for t in range(NT):
        xs = xt[:, t * D:(t + 1) * D]
        r_ln = emit_ln_tile(p, xs, xln[:, t * D:(t + 1) * D], sq, stats,
                            deps=(gate if t == 0 else ()))
        r_tr = emit_transpose_1024(p, xln[:, t * D:(t + 1) * D], xlnT[:],
                                   ps_t, cst["ident"],
                                   deps=[r_ln] + (gate if t == 0 else []))
        for k in range(KT):
            kd = []
            if k == 0:
                kd = [r_tr, r_wg]
                if prev_lgc is not None:
                    kd.append(prev_lgc)
            p.op("tensor", lambda te, k=k: te.matmul(
                ps_l[:], xlnT[:, k * 128:(k + 1) * 128],
                wgt[:, k * E:(k + 1) * E],
                start=(k == 0), stop=(k == KT - 1)),
                deps=kd, inc=(k == KT - 1))
        r_lg = Ref(p.esem["tensor"], p.ecnt["tensor"], "tensor")
        r_lgc = p.op("vector", lambda v: v.tensor_copy(logits[:], ps_l[:]),
                     deps=[r_lg])
        prev_lgc = r_lgc
        r_t8 = p.op("vector", lambda v: v.max(top8v[:], logits[:]),
                    deps=[r_lgc])
        r_i8 = p.op("vector", lambda v: v.max_index(top8i[:], top8v[:],
                                                    logits[:]), deps=[r_t8])
        ro = t * 8
        r_d12 = p.op("vector", lambda v: v.tensor_sub(
            stats[:, 6:7], top8v[:, 0:1], top8v[:, 1:2]), deps=[r_t8])
        r_sg = p.op("scalar", lambda s, ro=ro: s.activation(
            rt[:, ro + 2:ro + 3], stats[:, 6:7], AF.Sigmoid), deps=[r_d12])
        r_g1 = r_sg
        r_g2 = p.op("vector", lambda v, ro=ro: v.tensor_scalar(
            rt[:, ro + 3:ro + 4], rt[:, ro + 2:ro + 3], -1.0, 1.0,
            ALU.mult, ALU.add), deps=[r_g1])
        r_i1 = p.op("vector", lambda v, ro=ro: v.tensor_copy(
            rt[:, ro + 0:ro + 1], top8i[:, 0:1]), deps=[r_i8])
        r_i2 = p.op("vector", lambda v, ro=ro: v.tensor_copy(
            rt[:, ro + 1:ro + 2], top8i[:, 1:2]), deps=[r_i8])
        r_oh1 = p.op("vector", lambda v, ro=ro: v.tensor_scalar(
            ohbuf[:, 0:E], cst["iota8"][:], rt[:, ro + 0:ro + 1], None,
            ALU.is_equal), deps=[r_i1])
        r_oh2 = p.op("vector", lambda v, ro=ro: v.tensor_scalar(
            ohbuf[:, E:2 * E], cst["iota8"][:], rt[:, ro + 1:ro + 2], None,
            ALU.is_equal), deps=[r_i2])
        r_ohs = p.op("vector", lambda v: v.tensor_add(
            ohbuf[:, 2 * E:3 * E], ohbuf[:, 0:E], ohbuf[:, E:2 * E]),
            deps=[r_oh1, r_oh2])
        p.op("tensor", lambda te: te.matmul(
            ps_p[:], cst["tri"][:], ohbuf[:, 2 * E:3 * E],
            start=True, stop=False), deps=[r_ohs], inc=False)
        p.op("tensor", lambda te: te.matmul(
            ps_p[:], cst["onesrow"][:], carry[:],
            start=False, stop=True), deps=[r_cz], inc=True)
        r_pos = Ref(p.esem["tensor"], p.ecnt["tensor"], "tensor")
        r_tc = p.op("tensor", lambda te: te.matmul(
            ps_c[:], cst["onescol"][:], ohbuf[:, 2 * E:3 * E],
            start=True, stop=True), deps=[r_ohs])
        r_p1m = p.op("vector", lambda v: v.tensor_mul(
            ohbuf[:, 0:E], ohbuf[:, 0:E], ps_p[:]), deps=[r_pos])
        r_p1 = p.op("vector", lambda v, ro=ro: v.reduce_sum(
            rt[:, ro + 4:ro + 5], ohbuf[:, 0:E], axis=AX.X), deps=[r_p1m])
        r_p2m = p.op("vector", lambda v: v.tensor_mul(
            ohbuf[:, E:2 * E], ohbuf[:, E:2 * E], ps_p[:]), deps=[r_pos])
        r_p2 = p.op("vector", lambda v, ro=ro: v.reduce_sum(
            rt[:, ro + 5:ro + 6], ohbuf[:, E:2 * E], axis=AX.X), deps=[r_p2m])
        r_cu = p.op("vector", lambda v: v.tensor_add(
            carry[:], carry[:], ps_c[:]), deps=[r_tc, r_p1m, r_p2m])
        r_cz = r_cu
        rt_done += [r_p1, r_p2, r_g2, r_i1, r_i2, r_sg]
    return xln, rt, carry, rt_done, r_cz


_WG_IN = [None]


WSLICE = E * D * HID // NCORES


def build_L1():
    nc = bass.Bass("TRN2", target_bir_lowering=False, debug=False,
                   detect_race_conditions=False)
    p = Prog(nc)
    x_in = nc.dram_tensor("x", [TPC, D], F32, kind="ExternalInput").ap()
    wg_in = nc.dram_tensor("wg", [D, E], F32, kind="ExternalInput").ap()
    w1s_in = nc.dram_tensor("w1s", [WSLICE // 8192, 8192], F32,
                            kind="ExternalInput").ap()
    w2s_in = nc.dram_tensor("w2s", [WSLICE // 8192, 8192], F32,
                            kind="ExternalInput").ap()
    _WG_IN[0] = wg_in
    xln_out = nc.dram_tensor("xln", [TPC, D], BF16, kind="ExternalOutput").ap()
    cnt_out = nc.dram_tensor("cnt", [1, E], F32, kind="ExternalOutput").ap()
    rt_out = nc.dram_tensor("rt", [TPC, 8], F32, kind="ExternalOutput").ap()
    w1b_out = nc.dram_tensor("w1b", [WSLICE // 8192, 8192], BF16,
                             kind="ExternalOutput").ap()
    w2b_out = nc.dram_tensor("w2b", [WSLICE // 8192, 8192], BF16,
                             kind="ExternalOutput").ap()

    cst, cst_refs = alloc_consts(nc, p)
    xt = nc.alloc_sbuf_tensor("xt", [128, NT * D], F32)
    r_x = p.dma("sync", xt.ap().rearrange("p (t d) -> p t d", t=NT),
                x_in.rearrange("(t p) d -> p t d", p=128), stream="xload")

    xln, rt, carry, rt_done, r_cz = emit_ln_routing(
        nc, p, cst, "r", xt, [r_x] + cst_refs)

    wcv = [nc.alloc_sbuf_tensor(f"wcv{i}", [128, 8192], BF16) for i in range(2)]
    NROW = WSLICE // 8192
    RCH = NROW // 128
    conv_done = []
    prev_st = {}
    for which, win, wout in (("w1", w1s_in, w1b_out), ("w2", w2s_in, w2b_out)):
        for ch in range(RCH):
            par = ch % 2
            ldeps = [prev_st[par]] if par in prev_st else []
            r_l = p.dma("gpsimd", wcv[par][:],
                        win[ch * 128:(ch + 1) * 128, :],
                        stream=f"cv{which}{par}", deps=ldeps)
            r_s = p.dma("sync", wout[ch * 128:(ch + 1) * 128, :],
                        wcv[par][:], stream=f"cs{which}{par}", deps=[r_l])
            prev_st[par] = r_s
            conv_done.append(r_s)

    s1 = p.dma("sync", xln_out.rearrange("(t p) d -> p t d", p=128),
               xln.ap().rearrange("p (t d) -> p t d", t=NT), stream="st",
               deps=rt_done)
    s2 = p.dma("sync", rt_out.rearrange("(t p) e -> p t e", p=128),
               rt.ap().rearrange("p (t e) -> p t e", t=NT), stream="st",
               deps=rt_done)
    s3 = p.dma("sync", cnt_out, carry[:], stream="st", deps=[r_cz])
    p.op("sync", lambda sy: sy.nop(), deps=[s1, s2, s3] + conv_done)
    p.emit()
    return nc


def emit_dense_ffn(nc, p, cst, name, xres, xlnT, rtbuf, offs_bcast, scr,
                   w1_in, b1_in, w2_in, b2_in, gate_deps, r_offs, psum):
    HALF = TPC // 4
    NCH = HALF // 512

    coef = nc.alloc_sbuf_tensor(f"{name}_coef", [128, NT * E], F32)
    coefT = nc.alloc_sbuf_tensor(f"{name}_coefT", [1, TPC], F32)
    hbuf = nc.alloc_sbuf_tensor(f"{name}_hbuf", [128, HT * HALF], BF16)
    accT = nc.alloc_sbuf_tensor(f"{name}_accT", [128, KT * TPC], BF16)
    _ffn_bufs["hbuf"] = hbuf
    _ffn_bufs["accT"] = accT
    w1t = [nc.alloc_sbuf_tensor(f"{name}_w1t{i}", [128, KT * 128], BF16)
           for i in range(2)]
    w2t = [nc.alloc_sbuf_tensor(f"{name}_w2t{i}", [128, HT * 128], BF16)
           for i in range(2)]
    b1q = nc.alloc_sbuf_tensor(f"{name}_b1q", [128, HT], F32)
    b2h = nc.alloc_sbuf_tensor(f"{name}_b2h", [128, KT], F32)
    ytmp = nc.alloc_sbuf_tensor(f"{name}_ytmp", [128, 512], F32)
    ycs = nc.alloc_sbuf_tensor(f"{name}_ycs", [128, 512], BF16)

    ps_a = psum["a"]
    ps_b = psum["b"]
    ps_y = psum["y"]
    ps_y2 = psum["y2"]
    ps_tr = psum["tr"]
    ps_cb = psum["cb"]
    ps_sm = psum["sm"]

    coef_refs = []
    for t in range(NT):
        ro = t * 8
        co = t * E
        r_o1 = p.op("vector", lambda v, ro=ro, co=co: v.tensor_scalar(
            coef[:, co:co + E], cst["iota8"][:], rtbuf[:, ro:ro + 1], None,
            ALU.is_equal), deps=list(gate_deps) + [r_offs])
        r_s1 = p.op("vector", lambda v, co=co: v.tensor_mul(
            scr[:, 0:E], coef[:, co:co + E], offs_bcast[:]), deps=[r_o1])
        r_g1 = p.op("vector", lambda v: v.reduce_sum(
            scr[:, 8:9], scr[:, 0:E], axis=AX.X), deps=[r_s1])
        r_k1 = p.op("vector", lambda v, ro=ro: v.tensor_add(
            scr[:, 9:10], scr[:, 8:9], rtbuf[:, ro + 4:ro + 5]), deps=[r_g1])
        r_m1 = p.op("vector", lambda v: v.tensor_scalar(
            scr[:, 10:11], scr[:, 9:10], float(CAP), None, ALU.is_lt),
            deps=[r_k1])
        r_w1 = p.op("vector", lambda v, ro=ro: v.tensor_mul(
            scr[:, 11:12], scr[:, 10:11], rtbuf[:, ro + 2:ro + 3]),
            deps=[r_m1])
        r_o2 = p.op("vector", lambda v, ro=ro: v.tensor_scalar(
            scr[:, 16:16 + E], cst["iota8"][:], rtbuf[:, ro + 1:ro + 2], None,
            ALU.is_equal), deps=[r_o1])
        r_s2 = p.op("vector", lambda v: v.tensor_mul(
            scr[:, 24:24 + E], scr[:, 16:16 + E], offs_bcast[:]), deps=[r_o2])
        r_g2 = p.op("vector", lambda v: v.reduce_sum(
            scr[:, 12:13], scr[:, 24:24 + E], axis=AX.X), deps=[r_s2])
        r_k2 = p.op("vector", lambda v, ro=ro: v.tensor_add(
            scr[:, 13:14], scr[:, 12:13], rtbuf[:, ro + 5:ro + 6]), deps=[r_g2])
        r_m2 = p.op("vector", lambda v: v.tensor_scalar(
            scr[:, 14:15], scr[:, 13:14], float(CAP), None, ALU.is_lt),
            deps=[r_k2])
        r_w2 = p.op("vector", lambda v, ro=ro: v.tensor_mul(
            scr[:, 15:16], scr[:, 14:15], rtbuf[:, ro + 3:ro + 4]),
            deps=[r_m2])
        r_c1 = p.op("vector", lambda v, co=co: v.tensor_scalar(
            coef[:, co:co + E], coef[:, co:co + E], scr[:, 11:12], None,
            ALU.mult), deps=[r_w1])
        r_c2 = p.op("vector", lambda v, co=co: v.tensor_scalar(
            scr[:, 16:16 + E], scr[:, 16:16 + E], scr[:, 15:16], None,
            ALU.mult), deps=[r_w2, r_c1])
        r_cf = p.op("vector", lambda v, co=co: v.tensor_add(
            coef[:, co:co + E], coef[:, co:co + E], scr[:, 16:16 + E]),
            deps=[r_c2])
        coef_refs.append(r_cf)

    r_az = p.op("vector", lambda v: v.memset(accT[:], 0.0))

    acc_last = [r_az]
    mm1_group = {}
    mm2_group = {}
    ps_reader = {}
    prev_hb = {}
    for e in range(E):
        brow = scr[0:1, 2048:2560]
        r_prev_chunk = acc_last[-1]
        for cc in range(HT // 4):
            bdeps = [r_prev_chunk]
            if cc == 0 and "sm" in ps_reader:
                bdeps.append(ps_reader["sm"])
            r_br = p.dma("sync", brow,
                         b1_in[e:e + 1, cc * 512:(cc + 1) * 512],
                         stream=f"{name}_br", deps=bdeps)
            for j in range(4):
                p.op("tensor", lambda te, cc=cc, j=j: te.matmul(
                    ps_sm[:, cc * 4 + j:cc * 4 + j + 1],
                    brow[0:1, j * 128:(j + 1) * 128], cst["ones11"][:],
                    start=True, stop=True), deps=[r_br], inc=(j == 3))
            r_prev_chunk = Ref(p.esem["tensor"], p.ecnt["tensor"], "tensor")
        r_b1q = p.op("vector", lambda v: v.tensor_scalar(
            b1q[:], ps_sm[:, 0:HT], 0.25, None, ALU.mult),
            deps=[r_prev_chunk])
        for cc in range(KT // 4):
            r_br = p.dma("sync", brow,
                         b2_in[e:e + 1, cc * 512:(cc + 1) * 512],
                         stream=f"{name}_br", deps=[r_prev_chunk, r_b1q])
            for j in range(4):
                p.op("tensor", lambda te, cc=cc, j=j: te.matmul(
                    ps_sm[:, HT + cc * 4 + j:HT + cc * 4 + j + 1],
                    brow[0:1, j * 128:(j + 1) * 128], cst["ones11"][:],
                    start=True, stop=True), deps=[r_br], inc=(j == 3))
            r_prev_chunk = Ref(p.esem["tensor"], p.ecnt["tensor"], "tensor")
        r_b2h = p.op("vector", lambda v: v.tensor_scalar(
            b2h[:], ps_sm[:, HT:HT + KT], 0.5, None, ALU.mult),
            deps=[r_prev_chunk])
        ps_reader["sm"] = r_b2h

        ct_refs = []
        prev_ctc = None
        for half in range(4):
            for ch in range(NCH):
                base = half * HALF + ch * 512
                for j in range(4):
                    t = (base + j * 128) // 128
                    jdeps = [coef_refs[t]] + acc_last[-1:]
                    if j == 0 and prev_ctc is not None:
                        jdeps.append(prev_ctc)
                    if j == 0 and "cb" in ps_reader:
                        jdeps.append(ps_reader["cb"])
                    p.op("tensor", lambda te, t=t, j=j, e=e: te.matmul(
                        ps_cb[0:1, j * 128:(j + 1) * 128],
                        coef[:, t * E + e: t * E + e + 1],
                        cst["identf"][:],
                        start=True, stop=True),
                        deps=jdeps, inc=(j == 3))
                r_ctm = Ref(p.esem["tensor"], p.ecnt["tensor"], "tensor")
                r_ctc = p.op("vector", lambda v, base=base: v.tensor_copy(
                    coefT[0:1, base:base + 512], ps_cb[0:1, 0:512]),
                    deps=[r_ctm])
                prev_ctc = r_ctc
                ct_refs.append(r_ctc)

        for half in range(4):
            tok0 = half * HALF
            gelu_refs = []
            for ht in range(HT):
                par = ht % 2
                wb = w1t[par]
                wdeps = [mm1_group[par]] if par in mm1_group else []
                r_w = p.dma("sync", wb[:], w1_in[e, ht],
                            stream=f"{name}_w1{par}", deps=wdeps)
                for ch in range(NCH):
                    ps = ps_a if ht % 2 == 0 else ps_b
                    psname = "a" if ht % 2 == 0 else "b"
                    t0 = (tok0 + ch * 512) // 128
                    for k in range(KT):
                        kdeps = []
                        if k == 0:
                            kdeps = [r_w]
                            if psname in ps_reader:
                                kdeps.append(ps_reader[psname])
                        p.op("tensor", lambda te, wb=wb, k=k, t0=t0, ps=ps:
                             te.matmul(
                                 ps[:],
                                 wb[:, k * 128:(k + 1) * 128],
                                 xlnT.ap().rearrange(
                                     "p (t d) -> p t d", t=NT)[
                                     :, t0:t0 + 4, k * 128:(k + 1) * 128],
                                 start=(k == 0), stop=(k == KT - 1)),
                             deps=kdeps if k == 0 else (),
                             inc=(k == KT - 1))
                    r_mm = Ref(p.esem["tensor"], p.ecnt["tensor"], "tensor")
                    if ch == NCH - 1:
                        mm1_group[par] = r_mm
                    hb = hbuf[:, ht * HALF + ch * 512:
                              ht * HALF + (ch + 1) * 512]
                    sb = 1536 * (ht % 2)
                    s0 = scr[:, sb:sb + 512]
                    s1 = scr[:, sb + 512:sb + 1024]
                    s2 = scr[:, sb + 1024:sb + 1536]
                    hq_deps = [r_mm, r_b1q]
                    if (ht % 2) in prev_hb:
                        hq_deps.append(prev_hb[ht % 2])
                    r_hq = p.op("scalar", lambda s, ps=ps, s0=s0, ht=ht:
                                s.activation(s0, ps[:], AF.Identity,
                                             bias=b1q[:, ht:ht + 1],
                                             scale=0.25),
                                deps=hq_deps)
                    ps_reader[psname] = r_hq
                    r_sq2 = p.op("scalar", lambda s, s0=s0, s1=s1:
                                 s.activation(s1, s0, AF.Square, scale=4.0),
                                 deps=[r_hq])
                    r_v = p.op("vector", lambda v, s1=s1: v.tensor_scalar(
                        s1, s1, 4.0 * GC0 * GC1, 4.0 * GC0,
                        ALU.mult, ALU.add), deps=[r_sq2])
                    r_u = p.op("vector", lambda v, s0=s0, s1=s1:
                               v.tensor_mul(s1, s1, s0), deps=[r_v])
                    r_th = p.op("scalar", lambda s, s1=s1, s2=s2:
                                s.activation(s2, s1, AF.Tanh), deps=[r_u])
                    r_hb = p.op("vector", lambda v, s0=s0, s2=s2, hb=hb:
                                v.scalar_tensor_tensor(hb, s2, 1.0, s0,
                                                       ALU.add, ALU.mult),
                                deps=[r_th])
                    prev_hb[ht % 2] = r_hb
                    gelu_refs.append(r_hb)
            for ch in range(NCH):
                base = tok0 + ch * 512
                cbdeps = ct_refs + acc_last[-1:]
                if "cb" in ps_reader:
                    cbdeps.append(ps_reader["cb"])
                r_cb = p.op("tensor", lambda te, base=base: te.matmul(
                    ps_cb[:], cst["onesrowf"][:],
                    coefT[0:1, base:base + 512],
                    start=True, stop=True), deps=cbdeps)
                for dt in range(KT):
                    par = dt % 2
                    wb2 = w2t[par]
                    if ch == 0:
                        wdeps2 = [mm2_group[par]] if par in mm2_group else []
                        r_w2 = p.dma("scalar", wb2[:], w2_in[e, dt],
                                      stream=f"{name}_w2{par}", deps=wdeps2)
                        mm2_wref = r_w2
                    else:
                        mm2_wref = None
                    psy = ps_y if dt % 2 == 0 else ps_y2
                    psyname = "y" if dt % 2 == 0 else "y2"
                    for hk in range(HT):
                        dd = []
                        if hk == 0:
                            dd = gelu_refs[:]
                            if mm2_wref is not None:
                                dd.append(mm2_wref)
                            if psyname in ps_reader:
                                dd.append(ps_reader[psyname])
                        p.op("tensor", lambda te, wb2=wb2, hk=hk, ch=ch,
                             psy=psy, base=base, tok0=tok0: te.matmul(
                                 psy[:],
                                 wb2[:, hk * 128:(hk + 1) * 128],
                                 hbuf[:, hk * HALF + (base - tok0):
                                      hk * HALF + (base - tok0) + 512],
                                 start=(hk == 0), stop=(hk == HT - 1)),
                             deps=dd, inc=(hk == HT - 1))
                    r_mm2 = Ref(p.esem["tensor"], p.ecnt["tensor"], "tensor")
                    if ch == NCH - 1:
                        mm2_group[par] = r_mm2
                    ybdeps = [r_mm2, r_b2h]
                    if "yc" in ps_reader:
                        ybdeps.append(ps_reader["yc"])
                    r_yb = p.op("scalar", lambda s, psy=psy, dt=dt:
                                s.activation(ytmp[:], psy[:], AF.Identity,
                                             bias=b2h[:, dt:dt + 1]),
                                deps=ybdeps)
                    ps_reader[psyname] = r_yb
                    r_yc = p.op("vector", lambda v: v.tensor_mul(
                        ycs[:], ytmp[:], ps_cb[:]), deps=[r_yb, r_cb])
                    ps_reader["cb"] = r_yc
                    ps_reader["yc"] = r_yc
                    acc = accT[:, dt * TPC + base: dt * TPC + base + 512]
                    r_up = p.op("vector", lambda v, acc=acc: v.tensor_add(
                        acc, acc, ycs[:]), deps=[r_yc, r_az])
                    acc_last.append(r_up)

    done = []
    r_cp = None
    for tt in range(NT):
        for dt in range(KT):
            tdeps = []
            if dt == 0:
                tdeps = acc_last[-1:]
                if r_cp is not None:
                    tdeps.append(r_cp)
            p.op("tensor", lambda te, tt=tt, dt=dt: te.transpose(
                ps_tr[:, dt * 128:(dt + 1) * 128],
                accT[:, dt * TPC + tt * 128: dt * TPC + (tt + 1) * 128],
                cst["ident"][:]), deps=tdeps,
                inc=(dt == KT - 1))
        r_trg = Ref(p.esem["tensor"], p.ecnt["tensor"], "tensor")
        r_cp = p.op("vector", lambda v: v.tensor_copy(
            scr[:, 0:1024], ps_tr[:]), deps=[r_trg])
        r_ad = p.op("vector", lambda v, tt=tt: v.tensor_add(
            xres[:, tt * D:(tt + 1) * D], xres[:, tt * D:(tt + 1) * D],
            scr[:, 0:1024]), deps=[r_cp])
        done.append(r_ad)
    return done


def emit_ln_transpose_all(nc, p, cst, name, xres, xlnT, scr, deps, ps_t,
                          stage, stage_ap=None):
    if stage_ap is None:
        stage_ap = stage[:]
    last = None
    outs = []
    for t in range(NT):
        dd = list(deps) + ([last] if last is not None else [])
        r_ln = emit_ln_tile(p, xres[:, t * D:(t + 1) * D], stage_ap,
                            scr[:, 0:D], scr[:, D:D + 8], deps=dd)
        for k in range(KT):
            p.op("tensor", lambda te, k=k: te.transpose(
                ps_t[:, k * 128:(k + 1) * 128],
                stage_ap[:, k * 128:(k + 1) * 128], cst["ident"][:]),
                deps=([r_ln] + ([last] if last is not None else [])
                      if k == 0 else ()),
                inc=(k == KT - 1))
        r_tg = Ref(p.esem["tensor"], p.ecnt["tensor"], "tensor")
        r_c = p.op("vector", lambda v, t=t: v.tensor_copy(
            xlnT[:, t * D:(t + 1) * D], ps_t[:]), deps=[r_tg])
        last = r_c
        outs.append(r_c)
    return outs


def build_L2(with_qkv=True, name="L2"):
    nc = bass.Bass("TRN2", target_bir_lowering=False, debug=False,
                   detect_race_conditions=False)
    p = Prog(nc)
    x_in = nc.dram_tensor("x", [TPC, D], F32, kind="ExternalInput").ap()
    xln_in = nc.dram_tensor("xln", [TPC, D], BF16, kind="ExternalInput").ap()
    rt_in = nc.dram_tensor("rt", [TPC, 8], F32, kind="ExternalInput").ap()
    cnts_in = nc.dram_tensor("cnts", [8, E], F32, kind="ExternalInput").ap()
    cmask_in = nc.dram_tensor("cmask", [8, 1], F32, kind="ExternalInput").ap()
    w1_in = nc.dram_tensor("w1", [E, HT, 128, KT * 128], BF16,
                           kind="ExternalInput").ap()
    b1_in = nc.dram_tensor("b1", [E, HID], F32, kind="ExternalInput").ap()
    w2_in = nc.dram_tensor("w2", [E, KT, 128, HT * 128], BF16,
                           kind="ExternalInput").ap()
    b2_in = nc.dram_tensor("b2", [E, D], F32, kind="ExternalInput").ap()
    x1_out = nc.dram_tensor("x1", [TPC, D], F32, kind="ExternalOutput").ap()
    if with_qkv:
        wq_in = nc.dram_tensor("wq", [D, D], F32, kind="ExternalInput").ap()
        wk_in = nc.dram_tensor("wk", [D, D], F32, kind="ExternalInput").ap()
        wv_in = nc.dram_tensor("wv", [D, D], F32, kind="ExternalInput").ap()
        ropec_in = nc.dram_tensor("ropec", [128, TPC], F32,
                                  kind="ExternalInput").ap()
        ropes_in = nc.dram_tensor("ropes", [128, TPC], F32,
                                  kind="ExternalInput").ap()
        qT_out = nc.dram_tensor("qT", [D, TPC], BF16, kind="ExternalOutput").ap()
        kT_out = nc.dram_tensor("kT", [D, TPC], BF16, kind="ExternalOutput").ap()
        v_out = nc.dram_tensor("v", [TPC, D], BF16, kind="ExternalOutput").ap()

    cst, cst_refs = alloc_consts(nc, p)
    xt = nc.alloc_sbuf_tensor("xt", [128, NT * D], F32)
    xlnT = nc.alloc_sbuf_tensor("xlnT", [128, NT * D], BF16)
    rtb = nc.alloc_sbuf_tensor("rtb", [128, NT * 8], F32)
    scr = nc.alloc_sbuf_tensor("scr", [128, 3072], F32)
    cnts = nc.alloc_sbuf_tensor("s_cnts", [8, E], F32)
    cmask = nc.alloc_sbuf_tensor("s_cmask", [8, 1], F32)
    offs = nc.alloc_sbuf_tensor("offs", [128, E], F32)
    stage2 = None

    stage2_ap = scr.ap()[:, 2560:3072].bitcast(BF16)
    psum = {
        "a": nc.alloc_psum_tensor("ps_a", [128, 512], F32),
        "b": nc.alloc_psum_tensor("ps_b", [128, 512], F32),
        "y": nc.alloc_psum_tensor("ps_y", [128, 512], F32),
        "y2": nc.alloc_psum_tensor("ps_y2", [128, 512], F32),
        "tr": nc.alloc_psum_tensor("ps_tr", [128, 1024], BF16),
        "cb": nc.alloc_psum_tensor("ps_cb", [128, 512], F32),
        "sm": nc.alloc_psum_tensor("ps_sm", [128, 64], F32),
    }
    ps_o = psum["sm"]

    r_x = p.dma("sync", xt.ap().rearrange("p (t d) -> p t d", t=NT),
                x_in.rearrange("(t p) d -> p t d", p=128), stream="ld_x")
    r_rt = p.dma("sync", rtb.ap().rearrange("p (t e) -> p t e", t=NT),
                 rt_in.rearrange("(t p) e -> p t e", p=128), stream="ld_rt")
    r_cn = p.dma("sync", cnts[:], cnts_in, stream="ld_cn")
    r_cm = p.dma("sync", cmask[:], cmask_in, stream="ld_cm")

    r_of1 = p.op("tensor", lambda te: te.matmul(
        ps_o[0:1, 0:E], cmask[:], cnts[:], start=True, stop=True),
        deps=[r_cn, r_cm] + cst_refs)
    r_of2 = p.op("vector", lambda v: v.tensor_copy(offs[0:1, :], ps_o[0:1, 0:E]),
                 deps=[r_of1])
    r_of3 = p.op("tensor", lambda te: te.matmul(
        ps_o[:, 0:E], cst["onesrow"][:], offs[0:1, :], start=True, stop=True),
        deps=[r_of2])
    r_offs = p.op("vector", lambda v: v.tensor_copy(offs[:], ps_o[:, 0:E]),
                  deps=[r_of3])

    ps_ti = psum["tr"]
    last_tr = None
    tr_refs = []
    for t in range(NT):
        r_l = p.dma("sync", stage2_ap,
                    xln_in[t * 128:(t + 1) * 128, :], stream=f"xlnld{t % 2}",
                    deps=[last_tr] if last_tr is not None else cst_refs)
        for k in range(KT):
            p.op("tensor", lambda te, k=k: te.transpose(
                ps_ti[:, k * 128:(k + 1) * 128],
                stage2_ap[:, k * 128:(k + 1) * 128], cst["ident"][:]),
                deps=([r_l] + ([last_tr] if last_tr is not None else [])
                      if k == 0 else ()),
                inc=(k == KT - 1))
        r_tg = Ref(p.esem["tensor"], p.ecnt["tensor"], "tensor")
        r_c = p.op("vector", lambda v, t=t: v.tensor_copy(
            xlnT[:, t * D:(t + 1) * D], ps_ti[:]), deps=[r_tg])
        last_tr = r_c
        tr_refs.append(r_c)

    ffn_done = emit_dense_ffn(nc, p, cst, "f", xt, xlnT, rtb, offs, scr,
                              w1_in, b1_in, w2_in, b2_in,
                              [r_x, r_rt] + tr_refs, r_offs, psum)

    r_sx = p.dma("sync", x1_out.rearrange("(t p) d -> p t d", p=128),
                 xt.ap().rearrange("p (t d) -> p t d", t=NT), stream="stx",
                 deps=ffn_done)
    finals = [r_sx]

    if with_qkv:
        ln_refs = emit_ln_transpose_all(nc, p, cst, "q", xt, xlnT, scr,
                                        deps=ffn_done, ps_t=psum["tr"],
                                        stage=None, stage_ap=stage2_ap)
        finals += emit_qkv(nc, p, cst, xt, xlnT, scr, ln_refs,
                           wq_in, wk_in, wv_in, ropec_in, ropes_in,
                           qT_out, kT_out, v_out,
                           hbuf=_ffn_bufs["hbuf"], accT=_ffn_bufs["accT"],
                           psum=psum)

    p.op("sync", lambda sy: sy.nop(), deps=finals)
    p.emit()
    return nc


def emit_qkv(nc, p, cst, xt, xlnT, scr, ln_refs,
             wq_in, wk_in, wv_in, ropec_in, ropes_in, qT_out, kT_out, v_out,
             hbuf=None, accT=None, psum=None):
    hbF = hbuf.ap().bitcast(F32)
    ropec = hbF[:, 0:TPC]
    ropes = hbF[:, TPC:2 * TPC]
    qs = hbF[:, 2 * TPC:2 * TPC + 512]
    qsh = hbF[:, 2 * TPC + 512:2 * TPC + 1024]
    wqt = [hbuf.ap()[:, 2 * 2 * TPC + 2048 + i * KT * 128:
                     2 * 2 * TPC + 2048 + (i + 1) * KT * 128]
           for i in range(2)]
    acB = accT.ap()
    wvt = acB[:, 0:KT * 1024]
    qcb = [acB[:, KT * 1024 + i * 512:KT * 1024 + (i + 1) * 512]
           for i in range(2)]
    ps_q = psum["a"]
    ps_q2 = psum["b"]

    r_rc = p.dma("gpsimd", ropec, ropec_in, stream="ropec", deps=ln_refs[:1])
    r_rs = p.dma("gpsimd", ropes, ropes_in, stream="ropes", deps=ln_refs[:1])
    finals = []
    NCH4 = TPC // 512
    wstream = {}
    ps_rd = {}
    qc_user = {}
    nchunk = 0
    for which, w_in, outT in (("q", wq_in, qT_out), ("k", wk_in, kT_out)):
        for ot in range(KT):
            par = ot % 2
            wb = wqt[par]
            wdeps = ([wstream[par]] if par in wstream else []) + ln_refs[:1]
            r_w = p.dma("gpsimd", wb.rearrange("p (k c) -> p k c", k=KT),
                        w_in[:, ot * 128:(ot + 1) * 128].rearrange(
                            "(k p) c -> p k c", p=128),
                        stream=f"w{which}{par}", deps=wdeps,
                        allow_slow_non_contiguous=False)
            for ch in range(NCH4):
                ps = ps_q if ch % 2 == 0 else ps_q2
                pname = ch % 2
                for k in range(KT):
                    dd = []
                    if k == 0:
                        dd = [r_w] + ln_refs
                        if pname in ps_rd:
                            dd.append(ps_rd[pname])
                    p.op("tensor", lambda te, wb=wb, k=k, ch=ch, ps=ps:
                         te.matmul(
                             ps[:], wb[:, k * 128:(k + 1) * 128],
                             xlnT.ap().rearrange("p (t d) -> p t d", t=NT)[
                                 :, ch * 4:ch * 4 + 4, k * 128:(k + 1) * 128],
                             start=(k == 0), stop=(k == KT - 1)),
                         deps=dd, inc=(k == KT - 1))
                r_mm = Ref(p.esem["tensor"], p.ecnt["tensor"], "tensor")
                if ch == NCH4 - 1:
                    wstream[par] = r_mm
                r_ev = p.op("vector", lambda v, ps=ps: v.tensor_copy(
                    qs[:], ps[:]), deps=[r_mm])
                ps_rd[pname] = r_ev
                r_sh = p.op("vector", lambda v: v.stream_shuffle(
                    qsh[:], qs[:], [i ^ 1 for i in range(32)]), deps=[r_ev])
                r_m1 = p.op("vector", lambda v, ch=ch: v.tensor_mul(
                    qs[:], qs[:], ropec[:, ch * 512:(ch + 1) * 512]),
                    deps=[r_sh, r_rc])
                r_m2 = p.op("vector", lambda v, ch=ch: v.tensor_mul(
                    qsh[:], qsh[:], ropes[:, ch * 512:(ch + 1) * 512]),
                    deps=[r_m1, r_rs])
                qb = nchunk % 2
                addeps = [r_m2]
                if qb in qc_user:
                    addeps.append(qc_user[qb])
                r_ad = p.op("vector", lambda v, qb=qb: v.tensor_add(
                    qcb[qb], qs[:], qsh[:]), deps=addeps)
                r_st = p.dma("sync", outT[ot * 128:(ot + 1) * 128,
                                          ch * 512:(ch + 1) * 512],
                             qcb[qb], stream=f"st{which}{qb}", deps=[r_ad])
                qc_user[qb] = r_st
                nchunk += 1
                finals.append(r_st)
    r_wv = p.dma("gpsimd", wvt.rearrange("p (k c) -> p k c", k=KT),
                 wv_in.rearrange("(k p) c -> p k c", p=128), stream="wv",
                 deps=ln_refs[:1])
    for tt in range(NT):
        for dc in range(2):
            ps = ps_q if dc == 0 else ps_q2
            pname = dc
            for k in range(KT):
                dd = []
                if k == 0:
                    dd = [r_wv] + ln_refs
                    if pname in ps_rd:
                        dd.append(ps_rd[pname])
                p.op("tensor", lambda te, tt=tt, dc=dc, k=k, ps=ps:
                     te.matmul(
                         ps[:],
                         xlnT[:, tt * D + k * 128: tt * D + (k + 1) * 128],
                         wvt[:, k * 1024 + dc * 512: k * 1024 + (dc + 1) * 512],
                         start=(k == 0), stop=(k == KT - 1)),
                     deps=dd, inc=(k == KT - 1))
            r_mm = Ref(p.esem["tensor"], p.ecnt["tensor"], "tensor")
            qb = nchunk % 2
            evdeps = [r_mm]
            if qb in qc_user:
                evdeps.append(qc_user[qb])
            r_ev = p.op("vector", lambda v, ps=ps, qb=qb: v.tensor_copy(
                qcb[qb], ps[:]), deps=evdeps)
            ps_rd[pname] = r_ev
            r_st = p.dma("sync", v_out[tt * 128:(tt + 1) * 128,
                                       dc * 512:(dc + 1) * 512],
                         qcb[qb], stream=f"stv{qb}", deps=[r_ev])
            qc_user[qb] = r_st
            nchunk += 1
            finals.append(r_st)
    return finals


NHPC = 8


def build_L3():
    nc = bass.Bass("TRN2", target_bir_lowering=False, debug=False,
                   detect_race_conditions=False)
    p = Prog(nc)
    qh_in = nc.dram_tensor("qh", [NHPC, DK, S], BF16, kind="ExternalInput").ap()
    kh_in = nc.dram_tensor("kh", [NHPC, DK, S], BF16, kind="ExternalInput").ap()
    vh_in = nc.dram_tensor("vh", [NHPC, S, DK], BF16, kind="ExternalInput").ap()
    oh_out = nc.dram_tensor("oh", [NHPC, DK, S], BF16, kind="ExternalOutput").ap()

    cst, cst_refs = alloc_consts(nc, p)
    NQT = S // 128

    qT = [nc.alloc_sbuf_tensor(f"qT{i}", [64, S], BF16) for i in range(2)]
    kT = [nc.alloc_sbuf_tensor(f"kT{i}", [64, S], BF16) for i in range(2)]
    vt = [nc.alloc_sbuf_tensor(f"vt{i}", [128, NQT * DK], BF16)
          for i in range(2)]
    sqf = nc.alloc_sbuf_tensor("sqf", [64, S], F32)
    nrow = nc.alloc_sbuf_tensor("nrow", [1, 2 * S + 8], F32)
    mrow = nc.alloc_sbuf_tensor("mrow", [1, S], F32)
    mcol = nc.alloc_sbuf_tensor("mcol", [128, NQT], F32)
    P = nc.alloc_sbuf_tensor("P", [128, S], BF16)
    PT = nc.alloc_sbuf_tensor("PT", [128, S], BF16)
    rs = nc.alloc_sbuf_tensor("rs", [128, 16], F32)
    dnm = nc.alloc_sbuf_tensor("dnm", [128, 2], F32)
    obuf = nc.alloc_sbuf_tensor("obuf", [64, S], BF16)

    ps_s = [nc.alloc_psum_tensor(f"ps_s{i}", [128, 512], F32) for i in range(2)]
    ps_tr = nc.alloc_psum_tensor("ps_tr", [128, 1024], BF16)
    ps_o = nc.alloc_psum_tensor("ps_o", [64, 128], F32)
    ps_m = nc.alloc_psum_tensor("ps_m", [128, NQT], F32)
    ps_n = nc.alloc_psum_tensor("ps_n", [1, 512], F32)

    ld_refs = {}
    rd_last = {}
    ps_rd = {}
    r_store = None

    def load_head(j):
        par = j % 2
        deps = []
        for key in (f"q{par}", f"k{par}", f"v{par}"):
            if key in rd_last:
                deps.append(rd_last[key])
        r1 = p.dma("sync", qT[par][:, :], qh_in[j], stream=f"ldq{par}",
                   deps=deps)
        r2 = p.dma("sync", kT[par][:, :], kh_in[j], stream=f"ldk{par}",
                   deps=deps)
        r3 = p.dma("sync", vt[par].ap().rearrange("p (t d) -> p t d", t=NQT),
                   vh_in[j].rearrange("(t p) d -> p t d", p=128),
                   stream=f"ldv{par}", deps=deps)
        return [r1, r2, r3]

    ld_refs[0] = load_head(0)
    for j in range(NHPC):
        par = j % 2
        r_q, r_k, r_v = ld_refs[j]
        if j + 1 < NHPC:
            ld_refs[j + 1] = load_head(j + 1)
        r_sq = p.op("scalar", lambda s, par=par: s.activation(
            sqf[:], qT[par][:, :], AF.Square), deps=[r_q] + cst_refs)
        nq_parts = []
        for c in range(8):
            r_nm = p.op("tensor", lambda te, c=c, par=par: te.matmul(
                ps_n[:], cst["ones64"][:], sqf[:, c * 512:(c + 1) * 512],
                start=True, stop=True),
                deps=[r_sq] + ([ps_rd["n"]] if "n" in ps_rd else []))
            r_nc = p.op("vector", lambda v, c=c: v.tensor_copy(
                nrow[0:1, c * 512:(c + 1) * 512], ps_n[:]), deps=[r_nm])
            ps_rd["n"] = r_nc
            nq_parts.append(r_nc)
        r_sqk = p.op("scalar", lambda s, par=par: s.activation(
            sqf[:], kT[par][:, :], AF.Square), deps=[r_k] + nq_parts)
        nk_parts = []
        for c in range(8):
            r_nm = p.op("tensor", lambda te, c=c: te.matmul(
                ps_n[:], cst["ones64"][:], sqf[:, c * 512:(c + 1) * 512],
                start=True, stop=True), deps=[r_sqk, ps_rd["n"]])
            r_nc = p.op("vector", lambda v, c=c: v.tensor_copy(
                nrow[0:1, S + c * 512:S + (c + 1) * 512], ps_n[:]),
                deps=[r_nm])
            ps_rd["n"] = r_nc
            nk_parts.append(r_nc)
        r_mx = p.op("vector", lambda v: v.reduce_max(
            nrow[0:1, 2 * S:2 * S + 1], nrow[0:1, S:2 * S], axis=AX.X),
            deps=nk_parts)
        r_sq1 = p.op("scalar", lambda s: s.activation(
            mrow[0:1, :], nrow[0:1, 0:S], AF.Sqrt), deps=nq_parts + [r_mx])
        r_sq2 = p.op("scalar", lambda s: s.activation(
            nrow[0:1, 2 * S + 1:2 * S + 2], nrow[0:1, 2 * S:2 * S + 1],
            AF.Sqrt), deps=[r_mx])
        r_mr = p.op("vector", lambda v: v.tensor_scalar(
            mrow[0:1, :], mrow[0:1, :], nrow[0:1, 2 * S + 1:2 * S + 2],
            -0.125, ALU.mult, ALU.mult), deps=[r_sq1, r_sq2])
        for qt in range(NQT):
            p.op("tensor", lambda te, qt=qt: te.matmul(
                ps_m[:, qt:qt + 1], mrow[0:1, qt * 128:(qt + 1) * 128],
                cst["ones11"][:], start=True, stop=True),
                deps=([r_mr] + ([ps_rd["m"]] if "m" in ps_rd else [])
                      if qt == 0 else ()),
                inc=(qt == NQT - 1))
        r_mt = Ref(p.esem["tensor"], p.ecnt["tensor"], "tensor")
        r_mc = p.op("vector", lambda v: v.tensor_copy(mcol[:], ps_m[:]),
                    deps=[r_mt])
        ps_rd["m"] = r_mc

        prev_ptr = None
        prev_exp = None
        sc_group = None
        av_group = None
        for qt in range(NQT):
            L = 128 * (qt + 1)
            nfull = L // 512
            tail = L - nfull * 512
            if tail == 0:
                nfull -= 1
                tail = 512
            exp_refs = []
            for c in range(nfull + 1):
                w = 512 if c < nfull else tail
                pp = c % 2
                sdeps = []
                if f"s{pp}" in ps_rd:
                    sdeps.append(ps_rd[f"s{pp}"])
                if c == 0:
                    sdeps += [r_k, r_mc]
                p.op("tensor", lambda te, c=c, w=w, qt=qt, par=par, pp=pp:
                     te.matmul(ps_s[pp][:, 0:w],
                               qT[par][:, qt * 128:(qt + 1) * 128],
                               kT[par][:, c * 512:c * 512 + w],
                               start=True, stop=True),
                     deps=sdeps)
                r_sc = Ref(p.esem["tensor"], p.ecnt["tensor"], "tensor")
                sc_group = r_sc
                edeps = [r_sc]
                if prev_exp is not None:
                    edeps.append(prev_exp)
                if c == 0 and prev_ptr is not None:
                    edeps.append(prev_ptr)
                if c < nfull:
                    r_e = p.op("scalar", lambda s, c=c, w=w, qt=qt, pp=pp:
                               s.activation(
                                   P[:, c * 512:c * 512 + w], ps_s[pp][:, 0:w],
                                   AF.Exp, bias=mcol[:, qt:qt + 1],
                                   scale=0.125, accum_out=rs[:, c:c + 1]),
                               deps=edeps)
                else:
                    r_e = p.op("scalar", lambda s, c=c, w=w, qt=qt, pp=pp:
                               s.activation(
                                   P[:, c * 512:c * 512 + w], ps_s[pp][:, 0:w],
                                   AF.Exp, bias=mcol[:, qt:qt + 1],
                                   scale=0.125),
                               deps=edeps)
                ps_rd[f"s{pp}"] = r_e
                prev_exp = r_e
                exp_refs.append(r_e)
            r_mk = p.op("vector", lambda v, L=L: v.tensor_mul(
                P[:, L - 128:L], P[:, L - 128:L], cst["tril"][:]),
                deps=[exp_refs[-1]])
            r_ts = p.op("vector", lambda v, qt=qt, nfull=nfull, tail=tail:
                        v.reduce_sum(rs[:, nfull:nfull + 1],
                                     P[:, nfull * 512:nfull * 512 + tail],
                                     axis=AX.X), deps=[r_mk])
            r_dn = p.op("vector", lambda v, nfull=nfull: v.reduce_sum(
                dnm[:, 0:1], rs[:, 0:nfull + 1], axis=AX.X),
                deps=[r_ts] + exp_refs)
            r_rc = p.op("vector", lambda v: v.reciprocal(
                dnm[:, 1:2], dnm[:, 0:1]), deps=[r_dn])
            r_nm = p.op("vector", lambda v, L=L: v.tensor_scalar(
                P[:, 0:L], P[:, 0:L], dnm[:, 1:2], None, ALU.mult),
                deps=[r_rc])
            pt_evacs = []
            nkt = qt + 1
            for blk in range((nkt + 7) // 8):
                k0 = blk * 8
                k1 = min(k0 + 8, nkt)
                for ki in range(k0, k1):
                    p.op("tensor", lambda te, ki=ki, k0=k0: te.transpose(
                        ps_tr[:, (ki - k0) * 128:(ki - k0 + 1) * 128],
                        P[:, ki * 128:(ki + 1) * 128], cst["ident"][:]),
                        deps=([r_nm] + ([ps_rd["tr"]] if "tr" in ps_rd else [])
                              if ki == k0 else ()),
                        inc=(ki == k1 - 1))
                r_tg = Ref(p.esem["tensor"], p.ecnt["tensor"], "tensor")
                r_pc = p.op("vector", lambda v, k0=k0, k1=k1: v.tensor_copy(
                    PT[:, k0 * 128:k1 * 128],
                    ps_tr[:, 0:(k1 - k0) * 128]), deps=[r_tg])
                ps_rd["tr"] = r_pc
                pt_evacs.append(r_pc)
            prev_ptr = pt_evacs[-1]
            for ki in range(nkt):
                avdeps = []
                if ki == 0:
                    avdeps = pt_evacs + [r_v]
                    if "o" in ps_rd:
                        avdeps.append(ps_rd["o"])
                p.op("tensor", lambda te, ki=ki, par=par, nkt=nkt: te.matmul(
                    ps_o[:], vt[par][:, ki * DK:(ki + 1) * DK],
                    PT[:, ki * 128:(ki + 1) * 128],
                    start=(ki == 0), stop=(ki == nkt - 1)),
                    deps=avdeps, inc=(ki == nkt - 1))
            r_av = Ref(p.esem["tensor"], p.ecnt["tensor"], "tensor")
            av_group = r_av
            r_oe = p.op("vector", lambda v, qt=qt: v.tensor_copy(
                obuf[:, qt * 128:(qt + 1) * 128], ps_o[:]),
                deps=[r_av] + ([r_store] if r_store is not None and qt == 0
                               else []))
            ps_rd["o"] = r_oe
        rd_last[f"q{par}"] = sc_group
        rd_last[f"k{par}"] = sc_group
        rd_last[f"v{par}"] = av_group
        r_store = p.dma("sync", oh_out[j], obuf[:, :], stream=f"sto{j % 2}",
                        deps=[ps_rd["o"]])
    p.op("sync", lambda sy: sy.nop(), deps=[r_store])
    p.emit()
    return nc


def build_L4():
    nc = bass.Bass("TRN2", target_bir_lowering=False, debug=False,
                   detect_race_conditions=False)
    p = Prog(nc)
    x1_in = nc.dram_tensor("x1", [TPC, D], F32, kind="ExternalInput").ap()
    aoT_in = nc.dram_tensor("aoT", [D, TPC], BF16, kind="ExternalInput").ap()
    wo_in = nc.dram_tensor("wo", [D, D], F32, kind="ExternalInput").ap()
    wg_in = nc.dram_tensor("wg", [D, E], F32, kind="ExternalInput").ap()
    _WG_IN[0] = wg_in
    x2_out = nc.dram_tensor("x2", [TPC, D], F32, kind="ExternalOutput").ap()
    xln_out = nc.dram_tensor("xln", [TPC, D], BF16, kind="ExternalOutput").ap()
    cnt_out = nc.dram_tensor("cnt", [1, E], F32, kind="ExternalOutput").ap()
    rt_out = nc.dram_tensor("rt", [TPC, 8], F32, kind="ExternalOutput").ap()

    cst, cst_refs = alloc_consts(nc, p)
    xt = nc.alloc_sbuf_tensor("xt", [128, NT * D], F32)
    aoT = nc.alloc_sbuf_tensor("s_aoT", [128, KT * TPC], BF16)
    wot = nc.alloc_sbuf_tensor("wot", [128, KT * D], BF16)
    xsc = nc.alloc_sbuf_tensor("xsc", [128, 512], F32)
    ps_pr = [nc.alloc_psum_tensor(f"ps_pr{i}", [128, 512], F32)
             for i in range(2)]

    r_x = p.dma("sync", xt.ap().rearrange("p (t d) -> p t d", t=NT),
                x1_in.rearrange("(t p) d -> p t d", p=128), stream="xload")
    r_ao = p.dma("sync", aoT.ap().rearrange("p (k t) -> p k t", k=KT),
                 aoT_in.rearrange("(k p) t -> p k t", p=128), stream="aoload")
    r_wo = p.dma("gpsimd", wot.ap().rearrange("p (k c) -> p k c", k=KT),
                 wo_in.rearrange("(k p) c -> p k c", p=128), stream="woload")

    ps_rd = {}
    proj_done = []
    for tt in range(NT):
        for dc in range(2):
            pp = dc
            for k in range(KT):
                dd = []
                if k == 0:
                    dd = [r_ao, r_wo] + cst_refs
                    if f"pr{pp}" in ps_rd:
                        dd.append(ps_rd[f"pr{pp}"])
                p.op("tensor", lambda te, tt=tt, dc=dc, k=k, pp=pp: te.matmul(
                    ps_pr[pp][:],
                    aoT[:, k * TPC + tt * 128: k * TPC + (tt + 1) * 128],
                    wot[:, k * D + dc * 512: k * D + (dc + 1) * 512],
                    start=(k == 0), stop=(k == KT - 1)),
                    deps=dd, inc=(k == KT - 1))
            r_mm = Ref(p.esem["tensor"], p.ecnt["tensor"], "tensor")
            r_ad = p.op("vector", lambda v, tt=tt, dc=dc, pp=pp: v.tensor_add(
                xt[:, tt * D + dc * 512: tt * D + (dc + 1) * 512],
                xt[:, tt * D + dc * 512: tt * D + (dc + 1) * 512],
                ps_pr[pp][:]), deps=[r_mm, r_x])
            ps_rd[f"pr{pp}"] = r_ad
            proj_done.append(r_ad)

    xln, rt, carry, rt_done, r_cz = emit_ln_routing(
        nc, p, cst, "r", xt, proj_done + cst_refs)

    s0 = p.dma("sync", x2_out.rearrange("(t p) d -> p t d", p=128),
               xt.ap().rearrange("p (t d) -> p t d", t=NT), stream="st",
               deps=proj_done + rt_done)
    s1 = p.dma("sync", xln_out.rearrange("(t p) d -> p t d", p=128),
               xln.ap().rearrange("p (t d) -> p t d", t=NT), stream="st",
               deps=rt_done)
    s2 = p.dma("sync", rt_out.rearrange("(t p) e -> p t e", p=128),
               rt.ap().rearrange("p (t e) -> p t e", t=NT), stream="st",
               deps=rt_done)
    s3 = p.dma("sync", cnt_out, carry[:], stream="st", deps=[r_cz])
    p.op("sync", lambda sy: sy.nop(), deps=[s0, s1, s2, s3])
    p.emit()
    return nc


def build_L5():
    return build_L2(with_qkv=False, name="L5")


def _rope_tables(core):
    pos = (core % 2) * TPC + np.arange(TPC)
    inv = (1.0 / 10000.0 ** (np.arange(0, DK, 2, dtype=np.float32) / DK)
           ).astype(np.float32)
    ang = np.outer(pos.astype(np.float32), inv).astype(np.float32)
    didx = np.arange(128) % 64
    ct = np.cos(ang)[:, didx // 2].T.astype(np.float32)
    st = np.sin(ang)[:, didx // 2].T.astype(np.float32)
    st = st * np.where(didx % 2 == 1, 1.0, -1.0)[:, None].astype(np.float32)
    return np.ascontiguousarray(ct), np.ascontiguousarray(st)


def kernel(L0_attn_x, L0_attn_Wq, L0_attn_Wk, L0_attn_Wv, L0_attn_Wo,
           L0_ffn_Wg, L0_ffn_W1, L0_ffn_b1, L0_ffn_W2, L0_ffn_b2):
    xf = np.ascontiguousarray(np.asarray(L0_attn_x, np.float32)
                              ).reshape(NTOK, D)
    shards = [np.ascontiguousarray(xf[c * TPC:(c + 1) * TPC])
              for c in range(NCORES)]
    wg = np.ascontiguousarray(np.asarray(L0_ffn_Wg, np.float32))
    w1 = np.ascontiguousarray(np.asarray(L0_ffn_W1, np.float32))
    b1 = np.ascontiguousarray(np.asarray(L0_ffn_b1, np.float32))
    w2 = np.ascontiguousarray(np.asarray(L0_ffn_W2, np.float32))
    b2 = np.ascontiguousarray(np.asarray(L0_ffn_b2, np.float32))
    wq = np.ascontiguousarray(np.asarray(L0_attn_Wq, np.float32))
    wk = np.ascontiguousarray(np.asarray(L0_attn_Wk, np.float32))
    wv = np.ascontiguousarray(np.asarray(L0_attn_Wv, np.float32))
    wo = np.ascontiguousarray(np.asarray(L0_attn_Wo, np.float32))
    cores = list(range(NCORES))

    w1f = w1.reshape(-1, 8192)
    w2f = w2.reshape(-1, 8192)
    nsl = w1f.shape[0] // NCORES
    nc1 = build_L1()
    r1 = run_bass_kernel_spmd(
        nc1, [{"x": shards[c], "wg": wg,
               "w1s": w1f[c * nsl:(c + 1) * nsl],
               "w2s": w2f[c * nsl:(c + 1) * nsl]} for c in cores],
        core_ids=cores).results
    cnts = np.ascontiguousarray(
        np.concatenate([r1[c]["cnt"] for c in cores], axis=0))
    w1b = np.concatenate([r1[c]["w1b"] for c in cores], axis=0)
    w2b = np.concatenate([r1[c]["w2b"] for c in cores], axis=0)
    w1t = np.ascontiguousarray(
        w1b.reshape(E, KT, 128, HT, 128).transpose(0, 3, 2, 1, 4)
        .reshape(E, HT, 128, KT * 128))
    w2t = np.ascontiguousarray(
        w2b.reshape(E, HT, 128, KT, 128).transpose(0, 3, 2, 1, 4)
        .reshape(E, KT, 128, HT * 128))

    nc2 = build_L2()
    ims = []
    for c in cores:
        ct, st = _rope_tables(c)
        cm = np.zeros((8, 1), np.float32)
        cm[:c] = 1.0
        ims.append({"x": shards[c], "xln": r1[c]["xln"], "rt": r1[c]["rt"],
                    "cnts": cnts, "cmask": cm, "w1": w1t, "b1": b1, "w2": w2t,
                    "b2": b2, "wq": wq, "wk": wk, "wv": wv,
                    "ropec": ct, "ropes": st})
    r2 = run_bass_kernel_spmd(nc2, ims, core_ids=cores).results

    nc3 = build_L3()
    ims3 = []
    for c in cores:
        b = c // 2
        qTb = np.concatenate([r2[2 * b]["qT"], r2[2 * b + 1]["qT"]], axis=1)
        kTb = np.concatenate([r2[2 * b]["kT"], r2[2 * b + 1]["kT"]], axis=1)
        vb = np.concatenate([r2[2 * b]["v"], r2[2 * b + 1]["v"]], axis=0)
        hs = [(c % 2) * NHPC + i for i in range(NHPC)]
        qh = np.ascontiguousarray(
            np.stack([qTb[h * DK:(h + 1) * DK] for h in hs]))
        kh = np.ascontiguousarray(
            np.stack([kTb[h * DK:(h + 1) * DK] for h in hs]))
        vh = np.ascontiguousarray(
            np.stack([vb[:, h * DK:(h + 1) * DK] for h in hs]))
        ims3.append({"qh": qh, "kh": kh, "vh": vh})
    r3 = run_bass_kernel_spmd(nc3, ims3, core_ids=cores).results

    nc4 = build_L4()
    ims4 = []
    for c in cores:
        b = c // 2
        aoTb = np.concatenate([r3[2 * b]["oh"].reshape(NHEAD // 2 * DK, S),
                               r3[2 * b + 1]["oh"].reshape(NHEAD // 2 * DK, S)],
                              axis=0)
        ao = np.ascontiguousarray(aoTb[:, (c % 2) * TPC:(c % 2 + 1) * TPC])
        ims4.append({"x1": r2[c]["x1"], "aoT": ao, "wo": wo, "wg": wg})
    r4 = run_bass_kernel_spmd(nc4, ims4, core_ids=cores).results
    cnts2 = np.ascontiguousarray(
        np.concatenate([r4[c]["cnt"] for c in cores], axis=0))

    nc5 = build_L5()
    ims5 = []
    for c in cores:
        cm = np.zeros((8, 1), np.float32)
        cm[:c] = 1.0
        ims5.append({"x": r4[c]["x2"], "xln": r4[c]["xln"], "rt": r4[c]["rt"],
                     "cnts": cnts2, "cmask": cm, "w1": w1t, "b1": b1,
                     "w2": w2t, "b2": b2})
    r5 = run_bass_kernel_spmd(nc5, ims5, core_ids=cores).results

    out = np.concatenate([r5[c]["x1"] for c in cores], axis=0)
    return np.ascontiguousarray(out.reshape(B, S, D).astype(np.float32))


if __name__ == "__main__":
    for f in (build_L1, build_L2, build_L3, build_L4, build_L5):
        nc = f()
        print(f.__name__, nc.n_instructions())


# version 05
# speedup vs baseline: 1.6216x; 1.4863x over previous
import numpy as np
import ml_dtypes
import concourse.bass as bass
import concourse.mybir as mybir
from concourse.bass_utils import run_bass_kernel_spmd

AF = mybir.ActivationFunctionType
ALU = mybir.AluOpType
AX = mybir.AxisListType
F32 = mybir.dt.float32
BF16 = mybir.dt.bfloat16
U32 = mybir.dt.uint32
I32 = mybir.dt.int32

B, S, D, HID, E = 4, 4096, 1024, 4096, 8
NTOK = B * S
NCORES = 8
TPC = NTOK // NCORES
NT = TPC // 128
KT = D // 128
HT = HID // 128
CAP = 4301
NHEAD, DK = 16, 64
EPS = 1e-5
GC0 = 0.7978845608028654
GC1 = 0.044715


def bf16np(x):
    return np.ascontiguousarray(x).astype(ml_dtypes.bfloat16)


_ffn_bufs = {}


ENGINES = ("tensor", "vector", "scalar", "gpsimd", "sync")


class Ref:
    __slots__ = ("sem", "cnt", "engine")

    def __init__(self, sem, cnt, engine):
        self.sem, self.cnt, self.engine = sem, cnt, engine


class Prog:
    def __init__(self, nc):
        self.nc = nc
        self.items = {e: [] for e in ENGINES}
        self.esem = {e: nc.alloc_semaphore(f"E_{e}") for e in ENGINES}
        self.ecnt = {e: 0 for e in ENGINES}
        self.elast = {}
        self.dsem = {}
        self.dcnt = {}

    def _dma_sem(self, stream):
        if stream not in self.dsem:
            self.dsem[stream] = self.nc.alloc_semaphore(f"D_{stream}")
            self.dcnt[stream] = 0
        return self.dsem[stream]

    def op(self, engine, fn, deps=(), inc=True):
        sem = self.esem[engine]
        deps = tuple(deps)
        if engine in ("vector", "scalar", "gpsimd"):
            prev = self.elast.get(engine)
            if prev is not None:
                deps = deps + (prev,)
        if inc:
            self.ecnt[engine] += 1
        cnt = self.ecnt[engine]
        self.items[engine].append((fn, deps, sem if inc else None, 1))
        ref = Ref(sem, cnt, engine) if inc else None
        if inc:
            self.elast[engine] = ref
        return ref

    def dma(self, engine, out, in_, stream, deps=(), **kw):
        sem = self._dma_sem(stream)
        self.dcnt[stream] += 16
        cnt = self.dcnt[stream]

        def fn(eng, out=out, in_=in_, kw=kw):
            return eng.dma_start(out=out, in_=in_, **kw)
        self.items[engine].append((fn, tuple(deps), sem, 16))
        return Ref(sem, cnt, engine)

    def idma(self, engine, fn, stream, deps=()):
        sem = self._dma_sem(stream)
        self.dcnt[stream] += 16
        cnt = self.dcnt[stream]
        self.items[engine].append((fn, tuple(deps), sem, 16))
        return Ref(sem, cnt, engine)

    def raw(self, engine, fn):
        self.items[engine].append((fn, (), None, 0))
        return None

    def barrier_all(self):
        snap = [Ref(self.esem[e], self.ecnt[e], e) for e in ENGINES]
        snapd = [Ref(self.dsem[s], self.dcnt[s], None) for s in self.dsem]
        for e in ENGINES:
            self.items[e].append((None, tuple(snap + snapd), None, 0))

    def emit(self):
        nc = self.nc
        with nc.Block() as block:
            for engine in ENGINES:
                items = self.items[engine]

                def make(engine=engine, items=items):
                    def body(eng):
                        waited = {}
                        for fn, deps, sem, inc in items:
                            want = {}
                            for d in deps:
                                if d is None:
                                    continue
                                if want.get(d.sem, -1) < d.cnt:
                                    want[d.sem] = d.cnt
                            for wsem, wcnt in want.items():
                                if waited.get(wsem, -1) >= wcnt:
                                    continue
                                eng.wait_ge(wsem, wcnt)
                                waited[wsem] = wcnt
                            if fn is None:
                                continue
                            inst = fn(eng)
                            if sem is not None:
                                assert inst is not None, "op fn returned None"
                                inst.then_inc(sem, inc)
                    return body
                getattr(block, engine)(make())


def alloc_consts(nc, p):
    c = {}
    tri = nc.inline_tensor(np.triu(np.ones((128, 128), np.float32), 1), "c_tri").ap()
    onescol = nc.inline_tensor(np.ones((128, 1), np.float32), "c_onescol").ap()
    onesrow = nc.inline_tensor(np.ones((1, 128), np.float32), "c_onesrow").ap()
    ident = nc.inline_tensor(np.eye(128, dtype=ml_dtypes.bfloat16), "c_ident").ap()
    iota8 = nc.inline_tensor(
        np.tile(np.arange(8, dtype=np.float32), (128, 1)), "c_iota8").ap()
    ones11 = nc.inline_tensor(np.ones((1, 1), np.float32), "c_ones11").ap()
    identf = nc.inline_tensor(np.eye(128, dtype=np.float32), "c_identf").ap()
    tril = nc.inline_tensor(
        np.tril(np.ones((128, 128), np.float32)).astype(ml_dtypes.bfloat16),
        "c_tril").ap()
    ones64 = nc.inline_tensor(np.ones((64, 1), np.float32), "c_ones64").ap()
    c["tri"] = nc.alloc_sbuf_tensor("c_tri_s", [128, 128], F32)
    c["onescol"] = nc.alloc_sbuf_tensor("c_onescol_s", [128, 1], F32)
    c["onesrow"] = nc.alloc_sbuf_tensor("c_onesrow_s", [1, 128], F32)
    c["ident"] = nc.alloc_sbuf_tensor("c_ident_s", [128, 128], BF16)
    c["iota8"] = nc.alloc_sbuf_tensor("c_iota8_s", [128, 8], F32)
    c["ones11"] = nc.alloc_sbuf_tensor("c_ones11_s", [1, 1], F32)
    c["identf"] = nc.alloc_sbuf_tensor("c_identf_s", [128, 128], F32)
    c["tril"] = nc.alloc_sbuf_tensor("c_tril_s", [128, 128], BF16)
    c["ones64"] = nc.alloc_sbuf_tensor("c_ones64_s", [64, 1], F32)
    c["onesrowf"] = c["onesrow"]
    refs = []
    for name, src in [("tri", tri), ("onescol", onescol), ("onesrow", onesrow),
                      ("ident", ident), ("iota8", iota8), ("ones11", ones11),
                      ("identf", identf), ("tril", tril), ("ones64", ones64)]:
        refs.append(p.dma("sync", c[name][:], src, stream="consts"))
    return c, refs


def emit_ln_tile(p, xs, xln_out, sq, stcols, deps=()):
    r_sum = p.op("vector", lambda v: v.reduce_sum(stcols[:, 0:1], xs, axis=AX.X),
                 deps=deps)
    r_sq = p.op("scalar", lambda s: s.activation(sq[:], xs, AF.Square,
                                                accum_out=stcols[:, 1:2]),
                deps=deps)
    r_m = p.op("vector", lambda v: v.tensor_scalar(
        stcols[:, 2:3], stcols[:, 0:1], 1.0 / D, None, ALU.mult), deps=[r_sum])
    r_v1 = p.op("vector", lambda v: v.scalar_tensor_tensor(
        stcols[:, 3:4], stcols[:, 2:3], -1.0, stcols[:, 2:3], ALU.mult,
        ALU.mult), deps=[r_m])
    r_v2 = p.op("vector", lambda v: v.scalar_tensor_tensor(
        stcols[:, 3:4], stcols[:, 1:2], 1.0 / D, stcols[:, 3:4], ALU.mult,
        ALU.add), deps=[r_sq, r_v1])
    r_v3 = p.op("vector", lambda v: v.tensor_scalar(
        stcols[:, 3:4], stcols[:, 3:4], EPS, None, ALU.add), deps=[r_v2])
    r_sq2 = p.op("scalar", lambda s: s.activation(
        stcols[:, 4:5], stcols[:, 3:4], AF.Sqrt), deps=[r_v3])
    r_rc = p.op("vector", lambda v: v.reciprocal(
        stcols[:, 5:6], stcols[:, 4:5]), deps=[r_sq2])
    r_ap = p.op("vector", lambda v: v.tensor_scalar(
        xln_out, xs, stcols[:, 2:3], stcols[:, 5:6],
        ALU.subtract, ALU.mult), deps=[r_rc])
    return r_ap


def emit_transpose_1024(p, src_bf16, dst_bf16, ps_t, ident, deps):
    for k in range(KT):
        last = (k == KT - 1)
        p.op("tensor", lambda t, k=k: t.transpose(
            ps_t[:, k * 128:(k + 1) * 128],
            src_bf16[:, k * 128:(k + 1) * 128], ident[:]),
            deps=(deps if k == 0 else ()), inc=last)
    r_t = Ref(p.esem["tensor"], p.ecnt["tensor"], "tensor")
    r_c = p.op("vector", lambda v: v.tensor_copy(dst_bf16, ps_t[:]),
               deps=[r_t])
    return r_c


def emit_ln_routing(nc, p, cst, pref, xt, gate_deps):
    sq = nc.alloc_sbuf_tensor(f"{pref}_sq", [128, D], F32)
    stats = nc.alloc_sbuf_tensor(f"{pref}_stats", [128, 8], F32)
    xln = nc.alloc_sbuf_tensor(f"{pref}_xln", [128, NT * D], BF16)
    xlnT = nc.alloc_sbuf_tensor(f"{pref}_xlnT", [128, D], BF16)
    logits = nc.alloc_sbuf_tensor(f"{pref}_logits", [128, E], F32)
    top8v = nc.alloc_sbuf_tensor(f"{pref}_top8v", [128, 8], F32)
    top8i = nc.alloc_sbuf_tensor(f"{pref}_top8i", [128, 8], U32)
    ohbuf = nc.alloc_sbuf_tensor(f"{pref}_ohbuf", [128, 3 * E], F32)
    carry = nc.alloc_sbuf_tensor(f"{pref}_carry", [1, E], F32)
    rt = nc.alloc_sbuf_tensor(f"{pref}_rt", [128, NT * 8], F32)
    wgt = nc.alloc_sbuf_tensor(f"{pref}_wgt", [128, KT * E], BF16)

    ps_t = nc.alloc_psum_tensor(f"{pref}_ps_t", [128, D], BF16)
    ps_l = nc.alloc_psum_tensor(f"{pref}_ps_l", [128, E], F32)
    ps_p = nc.alloc_psum_tensor(f"{pref}_ps_p", [128, E], F32)
    ps_c = nc.alloc_psum_tensor(f"{pref}_ps_c", [1, E], F32)

    r_wg = p.dma("gpsimd", wgt.ap().rearrange("p (k e) -> p k e", k=KT),
                 _WG_IN[0].rearrange("(k p) e -> p k e", p=128),
                 stream=f"{pref}_wg")
    r_cz = p.op("vector", lambda v: v.memset(carry[:], 0.0))
    r_rtz = p.op("vector", lambda v: v.memset(rt[:], 0.0))

    gate = list(gate_deps) + [r_wg, r_rtz]
    rt_done = []
    prev_lgc = None
    for t in range(NT):
        xs = xt[:, t * D:(t + 1) * D]
        r_ln = emit_ln_tile(p, xs, xln[:, t * D:(t + 1) * D], sq, stats,
                            deps=(gate if t == 0 else ()))
        r_tr = emit_transpose_1024(p, xln[:, t * D:(t + 1) * D], xlnT[:],
                                   ps_t, cst["ident"],
                                   deps=[r_ln] + (gate if t == 0 else []))
        for k in range(KT):
            kd = []
            if k == 0:
                kd = [r_tr, r_wg]
                if prev_lgc is not None:
                    kd.append(prev_lgc)
            p.op("tensor", lambda te, k=k: te.matmul(
                ps_l[:], xlnT[:, k * 128:(k + 1) * 128],
                wgt[:, k * E:(k + 1) * E],
                start=(k == 0), stop=(k == KT - 1)),
                deps=kd, inc=(k == KT - 1))
        r_lg = Ref(p.esem["tensor"], p.ecnt["tensor"], "tensor")
        r_lgc = p.op("vector", lambda v: v.tensor_copy(logits[:], ps_l[:]),
                     deps=[r_lg])
        prev_lgc = r_lgc
        r_t8 = p.op("vector", lambda v: v.max(top8v[:], logits[:]),
                    deps=[r_lgc])
        r_i8 = p.op("vector", lambda v: v.max_index(top8i[:], top8v[:],
                                                    logits[:]), deps=[r_t8])
        ro = t * 8
        r_d12 = p.op("vector", lambda v: v.tensor_sub(
            stats[:, 6:7], top8v[:, 0:1], top8v[:, 1:2]), deps=[r_t8])
        r_sg = p.op("scalar", lambda s, ro=ro: s.activation(
            rt[:, ro + 2:ro + 3], stats[:, 6:7], AF.Sigmoid), deps=[r_d12])
        r_g1 = r_sg
        r_g2 = p.op("vector", lambda v, ro=ro: v.tensor_scalar(
            rt[:, ro + 3:ro + 4], rt[:, ro + 2:ro + 3], -1.0, 1.0,
            ALU.mult, ALU.add), deps=[r_g1])
        r_i1 = p.op("vector", lambda v, ro=ro: v.tensor_copy(
            rt[:, ro + 0:ro + 1], top8i[:, 0:1]), deps=[r_i8])
        r_i2 = p.op("vector", lambda v, ro=ro: v.tensor_copy(
            rt[:, ro + 1:ro + 2], top8i[:, 1:2]), deps=[r_i8])
        r_oh1 = p.op("vector", lambda v, ro=ro: v.tensor_scalar(
            ohbuf[:, 0:E], cst["iota8"][:], rt[:, ro + 0:ro + 1], None,
            ALU.is_equal), deps=[r_i1])
        r_oh2 = p.op("vector", lambda v, ro=ro: v.tensor_scalar(
            ohbuf[:, E:2 * E], cst["iota8"][:], rt[:, ro + 1:ro + 2], None,
            ALU.is_equal), deps=[r_i2])
        r_ohs = p.op("vector", lambda v: v.tensor_add(
            ohbuf[:, 2 * E:3 * E], ohbuf[:, 0:E], ohbuf[:, E:2 * E]),
            deps=[r_oh1, r_oh2])
        p.op("tensor", lambda te: te.matmul(
            ps_p[:], cst["tri"][:], ohbuf[:, 2 * E:3 * E],
            start=True, stop=False), deps=[r_ohs], inc=False)
        p.op("tensor", lambda te: te.matmul(
            ps_p[:], cst["onesrow"][:], carry[:],
            start=False, stop=True), deps=[r_cz], inc=True)
        r_pos = Ref(p.esem["tensor"], p.ecnt["tensor"], "tensor")
        r_tc = p.op("tensor", lambda te: te.matmul(
            ps_c[:], cst["onescol"][:], ohbuf[:, 2 * E:3 * E],
            start=True, stop=True), deps=[r_ohs])
        r_p1m = p.op("vector", lambda v: v.tensor_mul(
            ohbuf[:, 0:E], ohbuf[:, 0:E], ps_p[:]), deps=[r_pos])
        r_p1 = p.op("vector", lambda v, ro=ro: v.reduce_sum(
            rt[:, ro + 4:ro + 5], ohbuf[:, 0:E], axis=AX.X), deps=[r_p1m])
        r_p2m = p.op("vector", lambda v: v.tensor_mul(
            ohbuf[:, E:2 * E], ohbuf[:, E:2 * E], ps_p[:]), deps=[r_pos])
        r_p2 = p.op("vector", lambda v, ro=ro: v.reduce_sum(
            rt[:, ro + 5:ro + 6], ohbuf[:, E:2 * E], axis=AX.X), deps=[r_p2m])
        r_cu = p.op("vector", lambda v: v.tensor_add(
            carry[:], carry[:], ps_c[:]), deps=[r_tc, r_p1m, r_p2m])
        r_cz = r_cu
        rt_done += [r_p1, r_p2, r_g2, r_i1, r_i2, r_sg]
    return xln, rt, carry, rt_done, r_cz


_WG_IN = [None]


WSLICE = E * D * HID // NCORES


def build_L1():
    nc = bass.Bass("TRN2", target_bir_lowering=False, debug=False,
                   detect_race_conditions=False)
    p = Prog(nc)
    x_in = nc.dram_tensor("x", [TPC, D], F32, kind="ExternalInput").ap()
    wg_in = nc.dram_tensor("wg", [D, E], F32, kind="ExternalInput").ap()
    w1s_in = nc.dram_tensor("w1s", [WSLICE // 8192, 8192], F32,
                            kind="ExternalInput").ap()
    w2s_in = nc.dram_tensor("w2s", [WSLICE // 8192, 8192], F32,
                            kind="ExternalInput").ap()
    _WG_IN[0] = wg_in
    xln_out = nc.dram_tensor("xln", [TPC, D], BF16, kind="ExternalOutput").ap()
    cnt_out = nc.dram_tensor("cnt", [1, E], F32, kind="ExternalOutput").ap()
    rt_out = nc.dram_tensor("rt", [TPC, 8], F32, kind="ExternalOutput").ap()
    w1b_out = nc.dram_tensor("w1b", [WSLICE // 8192, 8192], BF16,
                             kind="ExternalOutput").ap()
    w2b_out = nc.dram_tensor("w2b", [WSLICE // 8192, 8192], BF16,
                             kind="ExternalOutput").ap()

    cst, cst_refs = alloc_consts(nc, p)
    xt = nc.alloc_sbuf_tensor("xt", [128, NT * D], F32)
    r_x = p.dma("sync", xt.ap().rearrange("p (t d) -> p t d", t=NT),
                x_in.rearrange("(t p) d -> p t d", p=128), stream="xload")

    xln, rt, carry, rt_done, r_cz = emit_ln_routing(
        nc, p, cst, "r", xt, [r_x] + cst_refs)

    wcv = [nc.alloc_sbuf_tensor(f"wcv{i}", [128, 8192], BF16) for i in range(2)]
    NROW = WSLICE // 8192
    RCH = NROW // 128
    conv_done = []
    prev_st = {}
    for which, win, wout in (("w1", w1s_in, w1b_out), ("w2", w2s_in, w2b_out)):
        for ch in range(RCH):
            par = ch % 2
            ldeps = [prev_st[par]] if par in prev_st else []
            r_l = p.dma("gpsimd", wcv[par][:],
                        win[ch * 128:(ch + 1) * 128, :],
                        stream=f"cv{which}{par}", deps=ldeps)
            r_s = p.dma("sync", wout[ch * 128:(ch + 1) * 128, :],
                        wcv[par][:], stream=f"cs{which}{par}", deps=[r_l])
            prev_st[par] = r_s
            conv_done.append(r_s)

    s1 = p.dma("sync", xln_out.rearrange("(t p) d -> p t d", p=128),
               xln.ap().rearrange("p (t d) -> p t d", t=NT), stream="st",
               deps=rt_done)
    s2 = p.dma("sync", rt_out.rearrange("(t p) e -> p t e", p=128),
               rt.ap().rearrange("p (t e) -> p t e", t=NT), stream="st",
               deps=rt_done)
    s3 = p.dma("sync", cnt_out, carry[:], stream="st", deps=[r_cz])
    p.op("sync", lambda sy: sy.nop(), deps=[s1, s2, s3] + conv_done)
    p.emit()
    return nc


PPAD = 1024
BIGIDX = 1 << 20


def emit_moe_ffn(nc, p, cst, name, xres, xln_in, rtbuf, offs_bcast, scr,
                 w1_in, b1_in, w2_in, b2_in, gate_deps, r_offs, psum,
                 stage_ap):
    from concourse.bass_core_rust import IndirectOffsetOnAxis
    breg = nc.alloc_register(mybir.EngineType.Pool, f"{name}_bnd")
    p.raw("gpsimd", lambda g: g.reg_mov(breg, E * PPAD - 1))

    ebuf = nc.dram_tensor(f"{name}_ebuf", [E * PPAD, D], BF16,
                          kind="Internal").ap()
    ybuf = nc.dram_tensor(f"{name}_ybuf", [E * PPAD, D], BF16,
                          kind="Internal").ap()
    kcarry = nc.alloc_sbuf_tensor(f"{name}_kcar", [1, E], F32)
    ohk = nc.alloc_sbuf_tensor(f"{name}_ohk", [128, 3 * E], F32)
    didx = nc.alloc_sbuf_tensor(f"{name}_didx", [128, NT * 4], U32)
    wcoef = nc.alloc_sbuf_tensor(f"{name}_wcoef", [128, NT * 2], F32)
    hbuf = nc.alloc_sbuf_tensor(f"{name}_hbuf", [128, HT * 512], BF16)
    exT = nc.alloc_sbuf_tensor(f"{name}_exT", [128, KT * PPAD], BF16)
    ystage = nc.alloc_sbuf_tensor(f"{name}_yst", [128, 4 * D], BF16)
    w1t = [nc.alloc_sbuf_tensor(f"{name}_w1t{i}", [128, KT * 128], BF16)
           for i in range(2)]
    w2t = [nc.alloc_sbuf_tensor(f"{name}_w2t{i}", [128, HT * 128], BF16)
           for i in range(2)]
    b1q = nc.alloc_sbuf_tensor(f"{name}_b1q", [128, HT], F32)
    b2h = nc.alloc_sbuf_tensor(f"{name}_b2h", [128, KT], F32)
    ytile = nc.alloc_sbuf_tensor(f"{name}_ytile", [128, 512], BF16)
    grow = [nc.alloc_sbuf_tensor(f"{name}_gr{i}", [128, D], F32)
            for i in range(2)]
    estage = [nc.alloc_sbuf_tensor(f"{name}_es{i}", [128, D], BF16)
              for i in range(2)]

    _ffn_bufs["hbuf"] = hbuf
    _ffn_bufs["exT"] = exT
    _ffn_bufs["ystage"] = ystage
    ps_a = psum["a"]
    ps_b = psum["b"]
    ps_y = psum["y"]
    ps_y2 = psum["y2"]
    ps_tr = psum["tr"]
    ps_p = psum["cb"]
    ps_sm = psum["sm"]

    r_kz = p.op("vector", lambda v: v.memset(kcarry[:], 0.0))
    r_gz0 = p.op("vector", lambda v: v.memset(grow[0][:], 0.0))
    r_gz1 = p.op("vector", lambda v: v.memset(grow[1][:], 0.0))
    scat_refs = []
    wc_refs = []
    prev_sc = None
    r_cz = r_kz
    for t in range(NT):
        ro = t * 8
        do = t * 4
        wo = t * 2
        gd = list(gate_deps) + [r_offs, r_kz]
        r_o1 = p.op("vector", lambda v, ro=ro: v.tensor_scalar(
            ohk[:, 0:E], cst["iota8"][:], rtbuf[:, ro:ro + 1], None,
            ALU.is_equal), deps=gd)
        r_s1 = p.op("vector", lambda v: v.tensor_mul(
            scr[:, 0:E], ohk[:, 0:E], offs_bcast[:]), deps=[r_o1])
        r_g1 = p.op("vector", lambda v: v.reduce_sum(
            scr[:, 8:9], scr[:, 0:E], axis=AX.X), deps=[r_s1])
        r_k1 = p.op("vector", lambda v, ro=ro: v.tensor_add(
            scr[:, 9:10], scr[:, 8:9], rtbuf[:, ro + 4:ro + 5]), deps=[r_g1])
        r_m1 = p.op("vector", lambda v: v.tensor_scalar(
            scr[:, 10:11], scr[:, 9:10], float(CAP), None, ALU.is_lt),
            deps=[r_k1])
        r_o2 = p.op("vector", lambda v, ro=ro: v.tensor_scalar(
            ohk[:, E:2 * E], cst["iota8"][:], rtbuf[:, ro + 1:ro + 2], None,
            ALU.is_equal), deps=[r_o1])
        r_s2 = p.op("vector", lambda v: v.tensor_mul(
            scr[:, 0:E], ohk[:, E:2 * E], offs_bcast[:]), deps=[r_o2, r_g1])
        r_g2 = p.op("vector", lambda v: v.reduce_sum(
            scr[:, 12:13], scr[:, 0:E], axis=AX.X), deps=[r_s2])
        r_k2 = p.op("vector", lambda v, ro=ro: v.tensor_add(
            scr[:, 13:14], scr[:, 12:13], rtbuf[:, ro + 5:ro + 6]),
            deps=[r_g2])
        r_m2 = p.op("vector", lambda v: v.tensor_scalar(
            scr[:, 14:15], scr[:, 13:14], float(CAP), None, ALU.is_lt),
            deps=[r_k2])
        r_ok1 = p.op("vector", lambda v: v.tensor_scalar(
            ohk[:, 0:E], ohk[:, 0:E], scr[:, 10:11], None, ALU.mult),
            deps=[r_m1, r_s2])
        r_ok2 = p.op("vector", lambda v: v.tensor_scalar(
            ohk[:, E:2 * E], ohk[:, E:2 * E], scr[:, 14:15], None, ALU.mult),
            deps=[r_m2])
        r_oks = p.op("vector", lambda v: v.tensor_add(
            ohk[:, 2 * E:3 * E], ohk[:, 0:E], ohk[:, E:2 * E]),
            deps=[r_ok1, r_ok2])
        p.op("tensor", lambda te: te.matmul(
            ps_p[:, 0:E], cst["tri"][:], ohk[:, 2 * E:3 * E],
            start=True, stop=False), deps=[r_oks], inc=False)
        p.op("tensor", lambda te: te.matmul(
            ps_p[:, 0:E], cst["onesrow"][:], kcarry[:],
            start=False, stop=True), deps=[r_cz], inc=True)
        r_pos = Ref(p.esem["tensor"], p.ecnt["tensor"], "tensor")
        r_tc = p.op("tensor", lambda te: te.matmul(
            ps_sm[0:1, 0:E], cst["onescol"][:], ohk[:, 2 * E:3 * E],
            start=True, stop=True), deps=[r_oks])
        r_p1m = p.op("vector", lambda v: v.tensor_mul(
            ohk[:, 0:E], ohk[:, 0:E], ps_p[:, 0:E]), deps=[r_pos])
        r_p1 = p.op("vector", lambda v: v.reduce_sum(
            scr[:, 16:17], ohk[:, 0:E], axis=AX.X), deps=[r_p1m])
        r_p2m = p.op("vector", lambda v: v.tensor_mul(
            ohk[:, E:2 * E], ohk[:, E:2 * E], ps_p[:, 0:E]), deps=[r_p1m])
        r_p2 = p.op("vector", lambda v: v.reduce_sum(
            scr[:, 17:18], ohk[:, E:2 * E], axis=AX.X), deps=[r_p2m])
        r_cu = p.op("vector", lambda v: v.tensor_add(
            kcarry[:], kcarry[:], ps_sm[0:1, 0:E]), deps=[r_tc, r_p2m])
        r_cz = r_cu
        r_c1 = p.op("vector", lambda v: v.tensor_scalar(
            scr[:, 18:19], scr[:, 16:17], float(PPAD), None, ALU.is_lt),
            deps=[r_p1])
        r_c2 = p.op("vector", lambda v: v.tensor_scalar(
            scr[:, 19:20], scr[:, 17:18], float(PPAD), None, ALU.is_lt),
            deps=[r_p2])
        r_q1 = p.op("vector", lambda v: v.tensor_mul(
            scr[:, 20:21], scr[:, 10:11], scr[:, 18:19]), deps=[r_c1])
        r_q2 = p.op("vector", lambda v: v.tensor_mul(
            scr[:, 21:22], scr[:, 14:15], scr[:, 19:20]), deps=[r_c2])
        r_w1 = p.op("vector", lambda v, ro=ro, wo=wo: v.tensor_mul(
            wcoef[:, wo:wo + 1], scr[:, 20:21], rtbuf[:, ro + 2:ro + 3]),
            deps=[r_q1])
        r_w2 = p.op("vector", lambda v, ro=ro, wo=wo: v.tensor_mul(
            wcoef[:, wo + 1:wo + 2], scr[:, 21:22], rtbuf[:, ro + 3:ro + 4]),
            deps=[r_q2])
        wc_refs += [r_w1, r_w2]
        r_d1a = p.op("vector", lambda v, ro=ro: v.scalar_tensor_tensor(
            scr[:, 22:23], rtbuf[:, ro:ro + 1], float(PPAD), scr[:, 16:17],
            ALU.mult, ALU.add), deps=[r_p1, r_q1])
        r_d1b = p.op("vector", lambda v: v.tensor_scalar(
            scr[:, 24:25], scr[:, 20:21], -float(BIGIDX), float(BIGIDX),
            ALU.mult, ALU.add), deps=[r_q1])
        r_d1 = p.op("vector", lambda v: v.tensor_add(
            scr[:, 26:27], scr[:, 22:23], scr[:, 24:25]),
            deps=[r_d1a, r_d1b])
        r_d1u = p.op("vector", lambda v, do=do: v.tensor_copy(
            didx[:, do:do + 1], scr[:, 26:27]), deps=[r_d1])
        r_d2a = p.op("vector", lambda v, ro=ro: v.scalar_tensor_tensor(
            scr[:, 22:23], rtbuf[:, ro + 1:ro + 2], float(PPAD),
            scr[:, 17:18], ALU.mult, ALU.add), deps=[r_p2, r_q2, r_d1u])
        r_d2b = p.op("vector", lambda v: v.tensor_scalar(
            scr[:, 24:25], scr[:, 21:22], -float(BIGIDX), float(BIGIDX),
            ALU.mult, ALU.add), deps=[r_q2, r_d1u])
        r_d2 = p.op("vector", lambda v: v.tensor_add(
            scr[:, 26:27], scr[:, 22:23], scr[:, 24:25]),
            deps=[r_d2a, r_d2b])
        r_d2u = p.op("vector", lambda v, do=do: v.tensor_copy(
            didx[:, do + 2:do + 3], scr[:, 26:27]), deps=[r_d2])
        ldeps = [prev_sc] if prev_sc is not None else []
        r_l = p.dma("sync", stage_ap, xln_in[t * 128:(t + 1) * 128, :],
                    stream=f"{name}_xl", deps=ldeps)
        r_sc1 = p.idma("gpsimd", lambda g, do=do: g.indirect_dma_start(
            out=ebuf, out_offset=IndirectOffsetOnAxis(didx[:, do:do + 1], 0),
            in_=stage_ap, in_offset=None,
            bounds_check=breg, oob_is_err=False),
            stream=f"{name}_sc", deps=[r_l, r_d1u])
        r_sc2 = p.idma("gpsimd", lambda g, do=do: g.indirect_dma_start(
            out=ebuf,
            out_offset=IndirectOffsetOnAxis(didx[:, do + 2:do + 3], 0),
            in_=stage_ap, in_offset=None,
            bounds_check=breg, oob_is_err=False),
            stream=f"{name}_sc", deps=[r_l, r_d2u])
        prev_sc = r_sc2
        scat_refs += [r_sc1, r_sc2]

    ps_reader = {}
    prev_hb = {}
    mm1_group = {}
    mm2_group = {}
    ex_group = {}
    yst_reader = None
    y_store = []
    for e in range(E):
        brow = scr[0:1, 2048:2560]
        r_prev_chunk = scat_refs[-1] if e == 0 else y_store[-1]
        for cc in range(HT // 4):
            bdeps = [r_prev_chunk]
            if cc == 0 and "sm2" in ps_reader:
                bdeps.append(ps_reader["sm2"])
            r_br = p.dma("scalar", brow,
                         b1_in[e:e + 1, cc * 512:(cc + 1) * 512],
                         stream=f"{name}_br", deps=bdeps)
            for j in range(4):
                p.op("tensor", lambda te, cc=cc, j=j: te.matmul(
                    ps_sm[:, 8 + cc * 4 + j:8 + cc * 4 + j + 1],
                    brow[0:1, j * 128:(j + 1) * 128], cst["ones11"][:],
                    start=True, stop=True), deps=[r_br], inc=(j == 3))
            r_prev_chunk = Ref(p.esem["tensor"], p.ecnt["tensor"], "tensor")
        r_b1q = p.op("vector", lambda v: v.tensor_scalar(
            b1q[:], ps_sm[:, 8:8 + HT], 0.25, None, ALU.mult),
            deps=[r_prev_chunk])
        for cc in range(KT // 4):
            r_br = p.dma("scalar", brow,
                         b2_in[e:e + 1, cc * 512:(cc + 1) * 512],
                         stream=f"{name}_br", deps=[r_prev_chunk, r_b1q])
            for j in range(4):
                p.op("tensor", lambda te, cc=cc, j=j: te.matmul(
                    ps_sm[:, 40 + cc * 4 + j:40 + cc * 4 + j + 1],
                    brow[0:1, j * 128:(j + 1) * 128], cst["ones11"][:],
                    start=True, stop=True), deps=[r_br], inc=(j == 3))
            r_prev_chunk = Ref(p.esem["tensor"], p.ecnt["tensor"], "tensor")
        r_b2h = p.op("vector", lambda v: v.tensor_scalar(
            b2h[:], ps_sm[:, 40:40 + KT], 0.5, None, ALU.mult),
            deps=[r_prev_chunk])
        ps_reader["sm2"] = r_b2h

        et_refs = []
        for bt in range(PPAD // 128):
            par = bt % 2
            edeps = [scat_refs[-1]] + ([ex_group[par]] if par in ex_group
                                       else [])
            r_el = p.dma("sync", estage[par][:],
                         ebuf[e * PPAD + bt * 128:
                              e * PPAD + (bt + 1) * 128, :],
                         stream=f"{name}_el{par}", deps=edeps)
            for k in range(KT):
                p.op("tensor", lambda te, k=k, par=par: te.transpose(
                    ps_tr[:, k * 128:(k + 1) * 128],
                    estage[par][:, k * 128:(k + 1) * 128], cst["ident"][:]),
                    deps=([r_el] + ([ps_reader["tr"]] if "tr" in ps_reader
                                    else []) if k == 0 else ()),
                    inc=(k == KT - 1))
            r_tg = Ref(p.esem["tensor"], p.ecnt["tensor"], "tensor")
            ex_group[par] = r_tg
            r_ec = p.op("vector", lambda v, bt=bt: v.tensor_copy(
                exT.ap().rearrange("p (k r) -> p k r", k=KT)[
                    :, :, bt * 128:(bt + 1) * 128],
                ps_tr.ap().rearrange("p (k r) -> p k r", k=KT)),
                deps=[r_tg])
            ps_reader["tr"] = r_ec
            et_refs.append(r_ec)

        for sub in range(2):
            r0 = sub * 512
            gelu_refs = []
            for ht in range(HT):
                par = ht % 2
                wb = w1t[par]
                wdeps = [mm1_group[par]] if par in mm1_group else []
                r_w = p.dma("sync", wb[:], w1_in[e, ht],
                            stream=f"{name}_w1{par}", deps=wdeps)
                mm1_wref = r_w
                ps = ps_a if ht % 2 == 0 else ps_b
                psname = "a" if ht % 2 == 0 else "b"
                for k in range(KT):
                    kdeps = []
                    if k == 0:
                        kdeps = et_refs[:]
                        if mm1_wref is not None:
                            kdeps.append(mm1_wref)
                        if psname in ps_reader:
                            kdeps.append(ps_reader[psname])
                    p.op("tensor", lambda te, wb=wb, k=k, r0=r0, ps=ps:
                         te.matmul(
                             ps[:], wb[:, k * 128:(k + 1) * 128],
                             exT[:, k * PPAD + r0:k * PPAD + r0 + 512],
                             start=(k == 0), stop=(k == KT - 1)),
                         deps=kdeps, inc=(k == KT - 1))
                r_mm = Ref(p.esem["tensor"], p.ecnt["tensor"], "tensor")
                mm1_group[par] = r_mm
                sb = 1536 * (ht % 2)
                s0 = scr[:, sb:sb + 512]
                s1 = scr[:, sb + 512:sb + 1024]
                s2 = scr[:, sb + 1024:sb + 1536]
                hq_deps = [r_mm, r_b1q]
                if (ht % 2) in prev_hb:
                    hq_deps.append(prev_hb[ht % 2])
                r_hq = p.op("scalar", lambda s, ps=ps, s0=s0, ht=ht:
                            s.activation(s0, ps[:], AF.Identity,
                                         bias=b1q[:, ht:ht + 1],
                                         scale=0.25), deps=hq_deps)
                ps_reader[psname] = r_hq
                r_sq2 = p.op("scalar", lambda s, s0=s0, s1=s1:
                             s.activation(s1, s0, AF.Square, scale=4.0),
                             deps=[r_hq])
                r_v = p.op("vector", lambda v, s1=s1: v.tensor_scalar(
                    s1, s1, 4.0 * GC0 * GC1, 4.0 * GC0, ALU.mult, ALU.add),
                    deps=[r_sq2])
                r_u = p.op("vector", lambda v, s0=s0, s1=s1:
                           v.tensor_mul(s1, s1, s0), deps=[r_v])
                r_th = p.op("scalar", lambda s, s1=s1, s2=s2:
                            s.activation(s2, s1, AF.Tanh), deps=[r_u])
                r_hb = p.op("vector", lambda v, s0=s0, s2=s2, ht=ht:
                            v.scalar_tensor_tensor(
                                hbuf[:, ht * 512:(ht + 1) * 512],
                                s2, 1.0, s0, ALU.add, ALU.mult),
                            deps=[r_th])
                prev_hb[ht % 2] = r_hb
                gelu_refs.append(r_hb)
            st_refs = []
            for dt in range(KT):
                par = dt % 2
                wb2 = w2t[par]
                wdeps2 = [mm2_group[par]] if par in mm2_group else []
                r_w2 = p.dma("scalar", wb2[:], w2_in[e, dt],
                             stream=f"{name}_w2{par}", deps=wdeps2)
                mm2_wref = r_w2
                psy = ps_y if dt % 2 == 0 else ps_y2
                psyname = "y" if dt % 2 == 0 else "y2"
                for hk in range(HT):
                    dd = []
                    if hk == 0:
                        dd = gelu_refs[:]
                        if mm2_wref is not None:
                            dd.append(mm2_wref)
                        if psyname in ps_reader:
                            dd.append(ps_reader[psyname])
                    p.op("tensor", lambda te, wb2=wb2, hk=hk, psy=psy:
                         te.matmul(
                             psy[:], wb2[:, hk * 128:(hk + 1) * 128],
                             hbuf[:, hk * 512:(hk + 1) * 512],
                             start=(hk == 0), stop=(hk == HT - 1)),
                         deps=dd, inc=(hk == HT - 1))
                r_mm2 = Ref(p.esem["tensor"], p.ecnt["tensor"], "tensor")
                mm2_group[par] = r_mm2
                ybdeps = [r_mm2, r_b2h]
                if "ytile" in ps_reader:
                    ybdeps.append(ps_reader["ytile"])
                r_yb = p.op("scalar", lambda s, psy=psy, dt=dt:
                            s.activation(ytile[:], psy[:], AF.Identity,
                                         bias=b2h[:, dt:dt + 1]),
                            deps=ybdeps)
                ps_reader[psyname] = r_yb
                for j in range(4):
                    tdeps = []
                    if j == 0:
                        tdeps = [r_yb]
                        if "tr" in ps_reader:
                            tdeps.append(ps_reader["tr"])
                    p.op("tensor", lambda te, j=j: te.transpose(
                        ps_tr[:, j * 128:(j + 1) * 128],
                        ytile[:, j * 128:(j + 1) * 128], cst["ident"][:]),
                        deps=tdeps, inc=(j == 3))
                r_tg2 = Ref(p.esem["tensor"], p.ecnt["tensor"], "tensor")
                ps_reader["ytile"] = Ref(p.esem["tensor"],
                                         p.ecnt["tensor"], "tensor")
                cdeps = [r_tg2]
                if yst_reader is not None and dt == 0:
                    cdeps.append(yst_reader)
                r_yc = p.op("vector", lambda v, dt=dt: v.tensor_copy(
                    ystage.ap().rearrange("p (r d) -> p r d", r=4)[
                        :, :, dt * 128:(dt + 1) * 128],
                    ps_tr.ap()[:, 0:512].rearrange("p (r d) -> p r d", r=4)),
                    deps=cdeps)
                ps_reader["tr"] = r_yc
                st_refs.append(r_yc)
            for r in range(4):
                r_st = p.dma("sync",
                             ybuf[e * PPAD + r0 + r * 128:
                                  e * PPAD + r0 + (r + 1) * 128, :],
                             ystage[:, r * D:(r + 1) * D],
                             stream=f"{name}_ys", deps=st_refs)
                y_store.append(r_st)
            yst_reader = y_store[-1]

    done = []
    gr_user = {}
    for t in range(NT):
        do = t * 4
        wo = t * 2
        for kk in range(2):
            par = kk
            gdeps = [y_store[-1]] + wc_refs[-1:]
            if par in gr_user:
                gdeps.append(gr_user[par])
            r_gt = p.idma("gpsimd", lambda g, do=do, kk=kk, par=par:
                          g.indirect_dma_start(
                              out=grow[par][:], out_offset=None,
                              in_=ybuf,
                              in_offset=IndirectOffsetOnAxis(
                                  didx[:, do + 2 * kk:do + 2 * kk + 1], 0),
                              bounds_check=breg, oob_is_err=False),
                          stream=f"{name}_gt{par}", deps=gdeps)
            r_up = p.op("vector", lambda v, t=t, wo=wo, kk=kk, par=par:
                        v.scalar_tensor_tensor(
                            xres[:, t * D:(t + 1) * D], grow[par][:],
                            wcoef[:, wo + kk:wo + kk + 1],
                            xres[:, t * D:(t + 1) * D],
                            ALU.mult, ALU.add),
                        deps=[r_gt, r_gz0, r_gz1])
            gr_user[par] = r_up
            done.append(r_up)
    return done


def emit_ln_transpose_all(nc, p, cst, name, xres, xlnT, scr, deps, ps_t,
                          stage, stage_ap=None):
    if stage_ap is None:
        stage_ap = stage[:]
    last = None
    outs = []
    for t in range(NT):
        dd = list(deps) + ([last] if last is not None else [])
        r_ln = emit_ln_tile(p, xres[:, t * D:(t + 1) * D], stage_ap,
                            scr[:, 0:D], scr[:, D:D + 8], deps=dd)
        for k in range(KT):
            p.op("tensor", lambda te, k=k: te.transpose(
                ps_t[:, k * 128:(k + 1) * 128],
                stage_ap[:, k * 128:(k + 1) * 128], cst["ident"][:]),
                deps=([r_ln] + ([last] if last is not None else [])
                      if k == 0 else ()),
                inc=(k == KT - 1))
        r_tg = Ref(p.esem["tensor"], p.ecnt["tensor"], "tensor")
        r_c = p.op("vector", lambda v, t=t: v.tensor_copy(
            xlnT[:, t * D:(t + 1) * D], ps_t[:]), deps=[r_tg])
        last = r_c
        outs.append(r_c)
    return outs


def build_L2(with_qkv=True, name="L2"):
    nc = bass.Bass("TRN2", target_bir_lowering=False, debug=False,
                   detect_race_conditions=False)
    p = Prog(nc)
    x_in = nc.dram_tensor("x", [TPC, D], F32, kind="ExternalInput").ap()
    xln_in = nc.dram_tensor("xln", [TPC, D], BF16, kind="ExternalInput").ap()
    rt_in = nc.dram_tensor("rt", [TPC, 8], F32, kind="ExternalInput").ap()
    cnts_in = nc.dram_tensor("cnts", [8, E], F32, kind="ExternalInput").ap()
    cmask_in = nc.dram_tensor("cmask", [8, 1], F32, kind="ExternalInput").ap()
    w1_in = nc.dram_tensor("w1", [E, HT, 128, KT * 128], BF16,
                           kind="ExternalInput").ap()
    b1_in = nc.dram_tensor("b1", [E, HID], F32, kind="ExternalInput").ap()
    w2_in = nc.dram_tensor("w2", [E, KT, 128, HT * 128], BF16,
                           kind="ExternalInput").ap()
    b2_in = nc.dram_tensor("b2", [E, D], F32, kind="ExternalInput").ap()
    x1_out = nc.dram_tensor("x1", [TPC, D], F32, kind="ExternalOutput").ap()
    if with_qkv:
        wq_in = nc.dram_tensor("wq", [D, D], F32, kind="ExternalInput").ap()
        wk_in = nc.dram_tensor("wk", [D, D], F32, kind="ExternalInput").ap()
        wv_in = nc.dram_tensor("wv", [D, D], F32, kind="ExternalInput").ap()
        ropec_in = nc.dram_tensor("ropec", [128, TPC], F32,
                                  kind="ExternalInput").ap()
        ropes_in = nc.dram_tensor("ropes", [128, TPC], F32,
                                  kind="ExternalInput").ap()
        qT_out = nc.dram_tensor("qT", [D, TPC], BF16, kind="ExternalOutput").ap()
        kT_out = nc.dram_tensor("kT", [D, TPC], BF16, kind="ExternalOutput").ap()
        v_out = nc.dram_tensor("v", [TPC, D], BF16, kind="ExternalOutput").ap()

    cst, cst_refs = alloc_consts(nc, p)
    xt = nc.alloc_sbuf_tensor("xt", [128, NT * D], F32)
    xlnT = nc.alloc_sbuf_tensor("xlnT", [128, NT * D], BF16)
    rtb = nc.alloc_sbuf_tensor("rtb", [128, NT * 8], F32)
    scr = nc.alloc_sbuf_tensor("scr", [128, 3072], F32)
    cnts = nc.alloc_sbuf_tensor("s_cnts", [8, E], F32)
    cmask = nc.alloc_sbuf_tensor("s_cmask", [8, 1], F32)
    offs = nc.alloc_sbuf_tensor("offs", [128, E], F32)
    stage2 = None

    stage2_ap = scr.ap()[:, 2560:3072].bitcast(BF16)
    psum = {
        "a": nc.alloc_psum_tensor("ps_a", [128, 512], F32),
        "b": nc.alloc_psum_tensor("ps_b", [128, 512], F32),
        "y": nc.alloc_psum_tensor("ps_y", [128, 512], F32),
        "y2": nc.alloc_psum_tensor("ps_y2", [128, 512], F32),
        "tr": nc.alloc_psum_tensor("ps_tr", [128, 1024], BF16),
        "cb": nc.alloc_psum_tensor("ps_cb", [128, 512], F32),
        "sm": nc.alloc_psum_tensor("ps_sm", [128, 64], F32),
    }
    ps_o = psum["sm"]

    r_x = p.dma("sync", xt.ap().rearrange("p (t d) -> p t d", t=NT),
                x_in.rearrange("(t p) d -> p t d", p=128), stream="ld_x")
    r_rt = p.dma("sync", rtb.ap().rearrange("p (t e) -> p t e", t=NT),
                 rt_in.rearrange("(t p) e -> p t e", p=128), stream="ld_rt")
    r_cn = p.dma("sync", cnts[:], cnts_in, stream="ld_cn")
    r_cm = p.dma("sync", cmask[:], cmask_in, stream="ld_cm")

    r_of1 = p.op("tensor", lambda te: te.matmul(
        ps_o[0:1, 0:E], cmask[:], cnts[:], start=True, stop=True),
        deps=[r_cn, r_cm] + cst_refs)
    r_of2 = p.op("vector", lambda v: v.tensor_copy(offs[0:1, :], ps_o[0:1, 0:E]),
                 deps=[r_of1])
    r_of3 = p.op("tensor", lambda te: te.matmul(
        ps_o[:, 0:E], cst["onesrow"][:], offs[0:1, :], start=True, stop=True),
        deps=[r_of2])
    r_offs = p.op("vector", lambda v: v.tensor_copy(offs[:], ps_o[:, 0:E]),
                  deps=[r_of3])

    ffn_done = emit_moe_ffn(nc, p, cst, "f", xt, xln_in, rtb, offs, scr,
                            w1_in, b1_in, w2_in, b2_in,
                            [r_x, r_rt] + cst_refs, r_offs, psum, stage2_ap)


    r_sx = p.dma("sync", x1_out.rearrange("(t p) d -> p t d", p=128),
                 xt.ap().rearrange("p (t d) -> p t d", t=NT), stream="stx",
                 deps=ffn_done)
    finals = [r_sx]

    if with_qkv:
        ln_refs = emit_ln_transpose_all(nc, p, cst, "q", xt, xlnT, scr,
                                        deps=ffn_done, ps_t=psum["tr"],
                                        stage=None, stage_ap=stage2_ap)
        finals += emit_qkv(nc, p, cst, xt, xlnT, scr, ln_refs,
                           wq_in, wk_in, wv_in, ropec_in, ropes_in,
                           qT_out, kT_out, v_out,
                           hbuf=_ffn_bufs["hbuf"], accT=None,
                           psum=psum)

    p.op("sync", lambda sy: sy.nop(), deps=finals)
    p.emit()
    return nc


def emit_qkv(nc, p, cst, xt, xlnT, scr, ln_refs,
             wq_in, wk_in, wv_in, ropec_in, ropes_in, qT_out, kT_out, v_out,
             hbuf=None, accT=None, psum=None):
    hbF = hbuf.ap().bitcast(F32)
    ropec = hbF[:, 0:TPC]
    ropes = hbF[:, TPC:2 * TPC]
    qs = hbF[:, 2 * TPC:2 * TPC + 512]
    qsh = hbF[:, 2 * TPC + 512:2 * TPC + 1024]
    wqt = [hbuf.ap()[:, 2 * 2 * TPC + 2048 + i * KT * 128:
                     2 * 2 * TPC + 2048 + (i + 1) * KT * 128]
           for i in range(2)]
    exB = _ffn_bufs["exT"].ap()
    wvt = exB[:, 0:KT * 1024]
    ysB = _ffn_bufs["ystage"].ap()
    qcb = [ysB[:, i * 512:(i + 1) * 512] for i in range(2)]
    ps_q = psum["a"]
    ps_q2 = psum["b"]

    r_rc = p.dma("gpsimd", ropec, ropec_in, stream="ropec", deps=ln_refs[:1])
    r_rs = p.dma("gpsimd", ropes, ropes_in, stream="ropes", deps=ln_refs[:1])
    finals = []
    NCH4 = TPC // 512
    wstream = {}
    ps_rd = {}
    qc_user = {}
    nchunk = 0
    for which, w_in, outT in (("q", wq_in, qT_out), ("k", wk_in, kT_out)):
        for ot in range(KT):
            par = ot % 2
            wb = wqt[par]
            wdeps = ([wstream[par]] if par in wstream else []) + ln_refs[:1]
            r_w = p.dma("gpsimd", wb.rearrange("p (k c) -> p k c", k=KT),
                        w_in[:, ot * 128:(ot + 1) * 128].rearrange(
                            "(k p) c -> p k c", p=128),
                        stream=f"w{which}{par}", deps=wdeps,
                        allow_slow_non_contiguous=False)
            for ch in range(NCH4):
                ps = ps_q if ch % 2 == 0 else ps_q2
                pname = ch % 2
                for k in range(KT):
                    dd = []
                    if k == 0:
                        dd = [r_w] + ln_refs
                        if pname in ps_rd:
                            dd.append(ps_rd[pname])
                    p.op("tensor", lambda te, wb=wb, k=k, ch=ch, ps=ps:
                         te.matmul(
                             ps[:], wb[:, k * 128:(k + 1) * 128],
                             xlnT.ap().rearrange("p (t d) -> p t d", t=NT)[
                                 :, ch * 4:ch * 4 + 4, k * 128:(k + 1) * 128],
                             start=(k == 0), stop=(k == KT - 1)),
                         deps=dd, inc=(k == KT - 1))
                r_mm = Ref(p.esem["tensor"], p.ecnt["tensor"], "tensor")
                if ch == NCH4 - 1:
                    wstream[par] = r_mm
                r_ev = p.op("vector", lambda v, ps=ps: v.tensor_copy(
                    qs[:], ps[:]), deps=[r_mm])
                ps_rd[pname] = r_ev
                r_sh = p.op("vector", lambda v: v.stream_shuffle(
                    qsh[:], qs[:], [i ^ 1 for i in range(32)]), deps=[r_ev])
                r_m1 = p.op("vector", lambda v, ch=ch: v.tensor_mul(
                    qs[:], qs[:], ropec[:, ch * 512:(ch + 1) * 512]),
                    deps=[r_sh, r_rc])
                r_m2 = p.op("vector", lambda v, ch=ch: v.tensor_mul(
                    qsh[:], qsh[:], ropes[:, ch * 512:(ch + 1) * 512]),
                    deps=[r_m1, r_rs])
                qb = nchunk % 2
                addeps = [r_m2]
                if qb in qc_user:
                    addeps.append(qc_user[qb])
                r_ad = p.op("vector", lambda v, qb=qb: v.tensor_add(
                    qcb[qb], qs[:], qsh[:]), deps=addeps)
                r_st = p.dma("sync", outT[ot * 128:(ot + 1) * 128,
                                          ch * 512:(ch + 1) * 512],
                             qcb[qb], stream=f"st{which}{qb}", deps=[r_ad])
                qc_user[qb] = r_st
                nchunk += 1
                finals.append(r_st)
    r_wv = p.dma("gpsimd", wvt.rearrange("p (k c) -> p k c", k=KT),
                 wv_in.rearrange("(k p) c -> p k c", p=128), stream="wv",
                 deps=ln_refs[:1])
    for tt in range(NT):
        for dc in range(2):
            ps = ps_q if dc == 0 else ps_q2
            pname = dc
            for k in range(KT):
                dd = []
                if k == 0:
                    dd = [r_wv] + ln_refs
                    if pname in ps_rd:
                        dd.append(ps_rd[pname])
                p.op("tensor", lambda te, tt=tt, dc=dc, k=k, ps=ps:
                     te.matmul(
                         ps[:],
                         xlnT[:, tt * D + k * 128: tt * D + (k + 1) * 128],
                         wvt[:, k * 1024 + dc * 512: k * 1024 + (dc + 1) * 512],
                         start=(k == 0), stop=(k == KT - 1)),
                     deps=dd, inc=(k == KT - 1))
            r_mm = Ref(p.esem["tensor"], p.ecnt["tensor"], "tensor")
            qb = nchunk % 2
            evdeps = [r_mm]
            if qb in qc_user:
                evdeps.append(qc_user[qb])
            r_ev = p.op("vector", lambda v, ps=ps, qb=qb: v.tensor_copy(
                qcb[qb], ps[:]), deps=evdeps)
            ps_rd[pname] = r_ev
            r_st = p.dma("sync", v_out[tt * 128:(tt + 1) * 128,
                                       dc * 512:(dc + 1) * 512],
                         qcb[qb], stream=f"stv{qb}", deps=[r_ev])
            qc_user[qb] = r_st
            nchunk += 1
            finals.append(r_st)
    return finals


NHPC = 8


def build_L3():
    nc = bass.Bass("TRN2", target_bir_lowering=False, debug=False,
                   detect_race_conditions=False)
    p = Prog(nc)
    qh_in = nc.dram_tensor("qh", [NHPC, DK, S], BF16, kind="ExternalInput").ap()
    kh_in = nc.dram_tensor("kh", [NHPC, DK, S], BF16, kind="ExternalInput").ap()
    vh_in = nc.dram_tensor("vh", [NHPC, S, DK], BF16, kind="ExternalInput").ap()
    oh_out = nc.dram_tensor("oh", [NHPC, DK, S], BF16, kind="ExternalOutput").ap()

    cst, cst_refs = alloc_consts(nc, p)
    NQT = S // 128

    qT = [nc.alloc_sbuf_tensor(f"qT{i}", [64, S], BF16) for i in range(2)]
    kT = [nc.alloc_sbuf_tensor(f"kT{i}", [64, S], BF16) for i in range(2)]
    vt = [nc.alloc_sbuf_tensor(f"vt{i}", [128, NQT * DK], BF16)
          for i in range(2)]
    sqf = nc.alloc_sbuf_tensor("sqf", [64, S], F32)
    nrow = nc.alloc_sbuf_tensor("nrow", [1, 2 * S + 8], F32)
    mrow = nc.alloc_sbuf_tensor("mrow", [1, S], F32)
    mcol = nc.alloc_sbuf_tensor("mcol", [128, NQT], F32)
    P = nc.alloc_sbuf_tensor("P", [128, S], BF16)
    PT = nc.alloc_sbuf_tensor("PT", [128, S], BF16)
    rs = nc.alloc_sbuf_tensor("rs", [128, 16], F32)
    dnm = nc.alloc_sbuf_tensor("dnm", [128, 2], F32)
    obuf = nc.alloc_sbuf_tensor("obuf", [64, S], BF16)

    ps_s = [nc.alloc_psum_tensor(f"ps_s{i}", [128, 512], F32) for i in range(2)]
    ps_tr = nc.alloc_psum_tensor("ps_tr", [128, 1024], BF16)
    ps_o = nc.alloc_psum_tensor("ps_o", [64, 128], F32)
    ps_m = nc.alloc_psum_tensor("ps_m", [128, NQT], F32)
    ps_n = nc.alloc_psum_tensor("ps_n", [1, 512], F32)

    ld_refs = {}
    rd_last = {}
    ps_rd = {}
    r_store = None

    def load_head(j):
        par = j % 2
        deps = []
        for key in (f"q{par}", f"k{par}", f"v{par}"):
            if key in rd_last:
                deps.append(rd_last[key])
        r1 = p.dma("sync", qT[par][:, :], qh_in[j], stream=f"ldq{par}",
                   deps=deps)
        r2 = p.dma("sync", kT[par][:, :], kh_in[j], stream=f"ldk{par}",
                   deps=deps)
        r3 = p.dma("sync", vt[par].ap().rearrange("p (t d) -> p t d", t=NQT),
                   vh_in[j].rearrange("(t p) d -> p t d", p=128),
                   stream=f"ldv{par}", deps=deps)
        return [r1, r2, r3]

    ld_refs[0] = load_head(0)
    for j in range(NHPC):
        par = j % 2
        r_q, r_k, r_v = ld_refs[j]
        if j + 1 < NHPC:
            ld_refs[j + 1] = load_head(j + 1)
        r_sq = p.op("scalar", lambda s, par=par: s.activation(
            sqf[:], qT[par][:, :], AF.Square), deps=[r_q] + cst_refs)
        nq_parts = []
        for c in range(8):
            r_nm = p.op("tensor", lambda te, c=c, par=par: te.matmul(
                ps_n[:], cst["ones64"][:], sqf[:, c * 512:(c + 1) * 512],
                start=True, stop=True),
                deps=[r_sq] + ([ps_rd["n"]] if "n" in ps_rd else []))
            r_nc = p.op("vector", lambda v, c=c: v.tensor_copy(
                nrow[0:1, c * 512:(c + 1) * 512], ps_n[:]), deps=[r_nm])
            ps_rd["n"] = r_nc
            nq_parts.append(r_nc)
        r_sqk = p.op("scalar", lambda s, par=par: s.activation(
            sqf[:], kT[par][:, :], AF.Square), deps=[r_k] + nq_parts)
        nk_parts = []
        for c in range(8):
            r_nm = p.op("tensor", lambda te, c=c: te.matmul(
                ps_n[:], cst["ones64"][:], sqf[:, c * 512:(c + 1) * 512],
                start=True, stop=True), deps=[r_sqk, ps_rd["n"]])
            r_nc = p.op("vector", lambda v, c=c: v.tensor_copy(
                nrow[0:1, S + c * 512:S + (c + 1) * 512], ps_n[:]),
                deps=[r_nm])
            ps_rd["n"] = r_nc
            nk_parts.append(r_nc)
        r_mx = p.op("vector", lambda v: v.reduce_max(
            nrow[0:1, 2 * S:2 * S + 1], nrow[0:1, S:2 * S], axis=AX.X),
            deps=nk_parts)
        r_sq1 = p.op("scalar", lambda s: s.activation(
            mrow[0:1, :], nrow[0:1, 0:S], AF.Sqrt), deps=nq_parts + [r_mx])
        r_sq2 = p.op("scalar", lambda s: s.activation(
            nrow[0:1, 2 * S + 1:2 * S + 2], nrow[0:1, 2 * S:2 * S + 1],
            AF.Sqrt), deps=[r_mx])
        r_mr = p.op("vector", lambda v: v.tensor_scalar(
            mrow[0:1, :], mrow[0:1, :], nrow[0:1, 2 * S + 1:2 * S + 2],
            -0.125, ALU.mult, ALU.mult), deps=[r_sq1, r_sq2])
        for qt in range(NQT):
            p.op("tensor", lambda te, qt=qt: te.matmul(
                ps_m[:, qt:qt + 1], mrow[0:1, qt * 128:(qt + 1) * 128],
                cst["ones11"][:], start=True, stop=True),
                deps=([r_mr] + ([ps_rd["m"]] if "m" in ps_rd else [])
                      if qt == 0 else ()),
                inc=(qt == NQT - 1))
        r_mt = Ref(p.esem["tensor"], p.ecnt["tensor"], "tensor")
        r_mc = p.op("vector", lambda v: v.tensor_copy(mcol[:], ps_m[:]),
                    deps=[r_mt])
        ps_rd["m"] = r_mc

        prev_ptr = None
        prev_exp = None
        sc_group = None
        av_group = None
        for qt in range(NQT):
            L = 128 * (qt + 1)
            nfull = L // 512
            tail = L - nfull * 512
            if tail == 0:
                nfull -= 1
                tail = 512
            exp_refs = []
            for c in range(nfull + 1):
                w = 512 if c < nfull else tail
                pp = c % 2
                sdeps = []
                if f"s{pp}" in ps_rd:
                    sdeps.append(ps_rd[f"s{pp}"])
                if c == 0:
                    sdeps += [r_k, r_mc]
                p.op("tensor", lambda te, c=c, w=w, qt=qt, par=par, pp=pp:
                     te.matmul(ps_s[pp][:, 0:w],
                               qT[par][:, qt * 128:(qt + 1) * 128],
                               kT[par][:, c * 512:c * 512 + w],
                               start=True, stop=True),
                     deps=sdeps)
                r_sc = Ref(p.esem["tensor"], p.ecnt["tensor"], "tensor")
                sc_group = r_sc
                edeps = [r_sc]
                if prev_exp is not None:
                    edeps.append(prev_exp)
                if c == 0 and prev_ptr is not None:
                    edeps.append(prev_ptr)
                if c < nfull:
                    r_e = p.op("scalar", lambda s, c=c, w=w, qt=qt, pp=pp:
                               s.activation(
                                   P[:, c * 512:c * 512 + w], ps_s[pp][:, 0:w],
                                   AF.Exp, bias=mcol[:, qt:qt + 1],
                                   scale=0.125, accum_out=rs[:, c:c + 1]),
                               deps=edeps)
                else:
                    r_e = p.op("scalar", lambda s, c=c, w=w, qt=qt, pp=pp:
                               s.activation(
                                   P[:, c * 512:c * 512 + w], ps_s[pp][:, 0:w],
                                   AF.Exp, bias=mcol[:, qt:qt + 1],
                                   scale=0.125),
                               deps=edeps)
                ps_rd[f"s{pp}"] = r_e
                prev_exp = r_e
                exp_refs.append(r_e)
            r_mk = p.op("vector", lambda v, L=L: v.tensor_mul(
                P[:, L - 128:L], P[:, L - 128:L], cst["tril"][:]),
                deps=[exp_refs[-1]])
            r_ts = p.op("vector", lambda v, qt=qt, nfull=nfull, tail=tail:
                        v.reduce_sum(rs[:, nfull:nfull + 1],
                                     P[:, nfull * 512:nfull * 512 + tail],
                                     axis=AX.X), deps=[r_mk])
            r_dn = p.op("vector", lambda v, nfull=nfull: v.reduce_sum(
                dnm[:, 0:1], rs[:, 0:nfull + 1], axis=AX.X),
                deps=[r_ts] + exp_refs)
            r_rc = p.op("vector", lambda v: v.reciprocal(
                dnm[:, 1:2], dnm[:, 0:1]), deps=[r_dn])
            r_nm = p.op("vector", lambda v, L=L: v.tensor_scalar(
                P[:, 0:L], P[:, 0:L], dnm[:, 1:2], None, ALU.mult),
                deps=[r_rc])
            pt_evacs = []
            nkt = qt + 1
            for blk in range((nkt + 7) // 8):
                k0 = blk * 8
                k1 = min(k0 + 8, nkt)
                for ki in range(k0, k1):
                    p.op("tensor", lambda te, ki=ki, k0=k0: te.transpose(
                        ps_tr[:, (ki - k0) * 128:(ki - k0 + 1) * 128],
                        P[:, ki * 128:(ki + 1) * 128], cst["ident"][:]),
                        deps=([r_nm] + ([ps_rd["tr"]] if "tr" in ps_rd else [])
                              if ki == k0 else ()),
                        inc=(ki == k1 - 1))
                r_tg = Ref(p.esem["tensor"], p.ecnt["tensor"], "tensor")
                r_pc = p.op("vector", lambda v, k0=k0, k1=k1: v.tensor_copy(
                    PT[:, k0 * 128:k1 * 128],
                    ps_tr[:, 0:(k1 - k0) * 128]), deps=[r_tg])
                ps_rd["tr"] = r_pc
                pt_evacs.append(r_pc)
            prev_ptr = pt_evacs[-1]
            for ki in range(nkt):
                avdeps = []
                if ki == 0:
                    avdeps = pt_evacs + [r_v]
                    if "o" in ps_rd:
                        avdeps.append(ps_rd["o"])
                p.op("tensor", lambda te, ki=ki, par=par, nkt=nkt: te.matmul(
                    ps_o[:], vt[par][:, ki * DK:(ki + 1) * DK],
                    PT[:, ki * 128:(ki + 1) * 128],
                    start=(ki == 0), stop=(ki == nkt - 1)),
                    deps=avdeps, inc=(ki == nkt - 1))
            r_av = Ref(p.esem["tensor"], p.ecnt["tensor"], "tensor")
            av_group = r_av
            r_oe = p.op("vector", lambda v, qt=qt: v.tensor_copy(
                obuf[:, qt * 128:(qt + 1) * 128], ps_o[:]),
                deps=[r_av] + ([r_store] if r_store is not None and qt == 0
                               else []))
            ps_rd["o"] = r_oe
        rd_last[f"q{par}"] = sc_group
        rd_last[f"k{par}"] = sc_group
        rd_last[f"v{par}"] = av_group
        r_store = p.dma("sync", oh_out[j], obuf[:, :], stream=f"sto{j % 2}",
                        deps=[ps_rd["o"]])
    p.op("sync", lambda sy: sy.nop(), deps=[r_store])
    p.emit()
    return nc


def build_L4():
    nc = bass.Bass("TRN2", target_bir_lowering=False, debug=False,
                   detect_race_conditions=False)
    p = Prog(nc)
    x1_in = nc.dram_tensor("x1", [TPC, D], F32, kind="ExternalInput").ap()
    aoT_in = nc.dram_tensor("aoT", [D, TPC], BF16, kind="ExternalInput").ap()
    wo_in = nc.dram_tensor("wo", [D, D], F32, kind="ExternalInput").ap()
    wg_in = nc.dram_tensor("wg", [D, E], F32, kind="ExternalInput").ap()
    _WG_IN[0] = wg_in
    x2_out = nc.dram_tensor("x2", [TPC, D], F32, kind="ExternalOutput").ap()
    xln_out = nc.dram_tensor("xln", [TPC, D], BF16, kind="ExternalOutput").ap()
    cnt_out = nc.dram_tensor("cnt", [1, E], F32, kind="ExternalOutput").ap()
    rt_out = nc.dram_tensor("rt", [TPC, 8], F32, kind="ExternalOutput").ap()

    cst, cst_refs = alloc_consts(nc, p)
    xt = nc.alloc_sbuf_tensor("xt", [128, NT * D], F32)
    aoT = nc.alloc_sbuf_tensor("s_aoT", [128, KT * TPC], BF16)
    wot = nc.alloc_sbuf_tensor("wot", [128, KT * D], BF16)
    xsc = nc.alloc_sbuf_tensor("xsc", [128, 512], F32)
    ps_pr = [nc.alloc_psum_tensor(f"ps_pr{i}", [128, 512], F32)
             for i in range(2)]

    r_x = p.dma("sync", xt.ap().rearrange("p (t d) -> p t d", t=NT),
                x1_in.rearrange("(t p) d -> p t d", p=128), stream="xload")
    r_ao = p.dma("sync", aoT.ap().rearrange("p (k t) -> p k t", k=KT),
                 aoT_in.rearrange("(k p) t -> p k t", p=128), stream="aoload")
    r_wo = p.dma("gpsimd", wot.ap().rearrange("p (k c) -> p k c", k=KT),
                 wo_in.rearrange("(k p) c -> p k c", p=128), stream="woload")

    ps_rd = {}
    proj_done = []
    for tt in range(NT):
        for dc in range(2):
            pp = dc
            for k in range(KT):
                dd = []
                if k == 0:
                    dd = [r_ao, r_wo] + cst_refs
                    if f"pr{pp}" in ps_rd:
                        dd.append(ps_rd[f"pr{pp}"])
                p.op("tensor", lambda te, tt=tt, dc=dc, k=k, pp=pp: te.matmul(
                    ps_pr[pp][:],
                    aoT[:, k * TPC + tt * 128: k * TPC + (tt + 1) * 128],
                    wot[:, k * D + dc * 512: k * D + (dc + 1) * 512],
                    start=(k == 0), stop=(k == KT - 1)),
                    deps=dd, inc=(k == KT - 1))
            r_mm = Ref(p.esem["tensor"], p.ecnt["tensor"], "tensor")
            r_ad = p.op("vector", lambda v, tt=tt, dc=dc, pp=pp: v.tensor_add(
                xt[:, tt * D + dc * 512: tt * D + (dc + 1) * 512],
                xt[:, tt * D + dc * 512: tt * D + (dc + 1) * 512],
                ps_pr[pp][:]), deps=[r_mm, r_x])
            ps_rd[f"pr{pp}"] = r_ad
            proj_done.append(r_ad)

    xln, rt, carry, rt_done, r_cz = emit_ln_routing(
        nc, p, cst, "r", xt, proj_done + cst_refs)

    s0 = p.dma("sync", x2_out.rearrange("(t p) d -> p t d", p=128),
               xt.ap().rearrange("p (t d) -> p t d", t=NT), stream="st",
               deps=proj_done + rt_done)
    s1 = p.dma("sync", xln_out.rearrange("(t p) d -> p t d", p=128),
               xln.ap().rearrange("p (t d) -> p t d", t=NT), stream="st",
               deps=rt_done)
    s2 = p.dma("sync", rt_out.rearrange("(t p) e -> p t e", p=128),
               rt.ap().rearrange("p (t e) -> p t e", t=NT), stream="st",
               deps=rt_done)
    s3 = p.dma("sync", cnt_out, carry[:], stream="st", deps=[r_cz])
    p.op("sync", lambda sy: sy.nop(), deps=[s0, s1, s2, s3])
    p.emit()
    return nc


def build_L5():
    return build_L2(with_qkv=False, name="L5")


def _rope_tables(core):
    pos = (core % 2) * TPC + np.arange(TPC)
    inv = (1.0 / 10000.0 ** (np.arange(0, DK, 2, dtype=np.float32) / DK)
           ).astype(np.float32)
    ang = np.outer(pos.astype(np.float32), inv).astype(np.float32)
    didx = np.arange(128) % 64
    ct = np.cos(ang)[:, didx // 2].T.astype(np.float32)
    st = np.sin(ang)[:, didx // 2].T.astype(np.float32)
    st = st * np.where(didx % 2 == 1, 1.0, -1.0)[:, None].astype(np.float32)
    return np.ascontiguousarray(ct), np.ascontiguousarray(st)


def kernel(L0_attn_x, L0_attn_Wq, L0_attn_Wk, L0_attn_Wv, L0_attn_Wo,
           L0_ffn_Wg, L0_ffn_W1, L0_ffn_b1, L0_ffn_W2, L0_ffn_b2):
    xf = np.ascontiguousarray(np.asarray(L0_attn_x, np.float32)
                              ).reshape(NTOK, D)
    shards = [np.ascontiguousarray(xf[c * TPC:(c + 1) * TPC])
              for c in range(NCORES)]
    wg = np.ascontiguousarray(np.asarray(L0_ffn_Wg, np.float32))
    w1 = np.ascontiguousarray(np.asarray(L0_ffn_W1, np.float32))
    b1 = np.ascontiguousarray(np.asarray(L0_ffn_b1, np.float32))
    w2 = np.ascontiguousarray(np.asarray(L0_ffn_W2, np.float32))
    b2 = np.ascontiguousarray(np.asarray(L0_ffn_b2, np.float32))
    wq = np.ascontiguousarray(np.asarray(L0_attn_Wq, np.float32))
    wk = np.ascontiguousarray(np.asarray(L0_attn_Wk, np.float32))
    wv = np.ascontiguousarray(np.asarray(L0_attn_Wv, np.float32))
    wo = np.ascontiguousarray(np.asarray(L0_attn_Wo, np.float32))
    cores = list(range(NCORES))

    w1f = w1.reshape(-1, 8192)
    w2f = w2.reshape(-1, 8192)
    nsl = w1f.shape[0] // NCORES
    nc1 = build_L1()
    r1 = run_bass_kernel_spmd(
        nc1, [{"x": shards[c], "wg": wg,
               "w1s": w1f[c * nsl:(c + 1) * nsl],
               "w2s": w2f[c * nsl:(c + 1) * nsl]} for c in cores],
        core_ids=cores).results
    cnts = np.ascontiguousarray(
        np.concatenate([r1[c]["cnt"] for c in cores], axis=0))
    w1b = np.concatenate([r1[c]["w1b"] for c in cores], axis=0)
    w2b = np.concatenate([r1[c]["w2b"] for c in cores], axis=0)
    w1t = np.ascontiguousarray(
        w1b.reshape(E, KT, 128, HT, 128).transpose(0, 3, 2, 1, 4)
        .reshape(E, HT, 128, KT * 128))
    w2t = np.ascontiguousarray(
        w2b.reshape(E, HT, 128, KT, 128).transpose(0, 3, 2, 1, 4)
        .reshape(E, KT, 128, HT * 128))

    nc2 = build_L2()
    ims = []
    for c in cores:
        ct, st = _rope_tables(c)
        cm = np.zeros((8, 1), np.float32)
        cm[:c] = 1.0
        ims.append({"x": shards[c], "xln": r1[c]["xln"], "rt": r1[c]["rt"],
                    "cnts": cnts, "cmask": cm, "w1": w1t, "b1": b1, "w2": w2t,
                    "b2": b2, "wq": wq, "wk": wk, "wv": wv,
                    "ropec": ct, "ropes": st})
    r2 = run_bass_kernel_spmd(nc2, ims, core_ids=cores).results

    nc3 = build_L3()
    ims3 = []
    for c in cores:
        b = c // 2
        qTb = np.concatenate([r2[2 * b]["qT"], r2[2 * b + 1]["qT"]], axis=1)
        kTb = np.concatenate([r2[2 * b]["kT"], r2[2 * b + 1]["kT"]], axis=1)
        vb = np.concatenate([r2[2 * b]["v"], r2[2 * b + 1]["v"]], axis=0)
        hs = [(c % 2) * NHPC + i for i in range(NHPC)]
        qh = np.ascontiguousarray(
            np.stack([qTb[h * DK:(h + 1) * DK] for h in hs]))
        kh = np.ascontiguousarray(
            np.stack([kTb[h * DK:(h + 1) * DK] for h in hs]))
        vh = np.ascontiguousarray(
            np.stack([vb[:, h * DK:(h + 1) * DK] for h in hs]))
        ims3.append({"qh": qh, "kh": kh, "vh": vh})
    r3 = run_bass_kernel_spmd(nc3, ims3, core_ids=cores).results

    nc4 = build_L4()
    ims4 = []
    for c in cores:
        b = c // 2
        aoTb = np.concatenate([r3[2 * b]["oh"].reshape(NHEAD // 2 * DK, S),
                               r3[2 * b + 1]["oh"].reshape(NHEAD // 2 * DK, S)],
                              axis=0)
        ao = np.ascontiguousarray(aoTb[:, (c % 2) * TPC:(c % 2 + 1) * TPC])
        ims4.append({"x1": r2[c]["x1"], "aoT": ao, "wo": wo, "wg": wg})
    r4 = run_bass_kernel_spmd(nc4, ims4, core_ids=cores).results
    cnts2 = np.ascontiguousarray(
        np.concatenate([r4[c]["cnt"] for c in cores], axis=0))

    nc5 = build_L5()
    ims5 = []
    for c in cores:
        cm = np.zeros((8, 1), np.float32)
        cm[:c] = 1.0
        ims5.append({"x": r4[c]["x2"], "xln": r4[c]["xln"], "rt": r4[c]["rt"],
                     "cnts": cnts2, "cmask": cm, "w1": w1t, "b1": b1,
                     "w2": w2t, "b2": b2})
    r5 = run_bass_kernel_spmd(nc5, ims5, core_ids=cores).results

    out = np.concatenate([r5[c]["x1"] for c in cores], axis=0)
    return np.ascontiguousarray(out.reshape(B, S, D).astype(np.float32))


if __name__ == "__main__":
    for f in (build_L1, build_L2, build_L3, build_L4, build_L5):
        nc = f()
        print(f.__name__, nc.n_instructions())
